# version 10
# baseline (speedup 1.0000x reference)
import os
import numpy as np
import ml_dtypes
from contextlib import ExitStack
import concourse.bass as bass
import concourse.mybir as mybir
from concourse.bass_utils import run_bass_kernel_spmd

F32 = mybir.dt.float32
BF16 = mybir.dt.bfloat16
U32 = mybir.dt.uint32
U8 = mybir.dt.uint8
AF = mybir.ActivationFunctionType
ALU = mybir.AluOpType
AX = mybir.AxisListType
bf16 = ml_dtypes.bfloat16

NCORES = 8
D = 4096
T = 1024
INTER = 11008
H = 32
HD = 128
SEG = 512
EPS = 1e-5
ATT_SCALE = 1.0 / np.sqrt(128.0)
NEG = -1e9

PHASES = int(os.environ.get("WK_PHASES", "6"))
DEBUG = os.environ.get("WK_DEBUG", "0") == "1"
AGSHARED = os.environ.get("WK_AGSHARED", "1") == "1"

ENGINES = ("sync", "scalar", "vector", "tensor", "gpsimd")


def mkperm(F, win=1024):
    perm = np.empty(F, dtype=np.int64)
    for base in range(0, F, win):
        m = min(win, F - base)
        kk = m // 8
        j = np.arange(m)
        perm[base + j] = base + 8 * (j % kk) + (j // kk)
    return perm


PERM_H = mkperm(D)
PERM_D = mkperm(INTER)
DN_CHUNKS = [(wp, n, 128 if wp < 10 else 96) for wp in range(11) for n in range(8)]
DN_ROWBASE = {}
_rb = 0
for _i, (_wp, _n, _kk) in enumerate(DN_CHUNKS):
    DN_ROWBASE[_i] = _rb
    _rb += _kk
assert _rb == INTER


class Sem:
    def __init__(self, nc, es, name):
        self.h = es.enter_context(nc.semaphore(name))
        self.n = 0


class Prog:
    def __init__(self, nc, es):
        self.nc = nc
        self.streams = {e: [] for e in ENGINES}
        self.es = es
        self._sems = {}
        self.pguard = []

    def sem(self, name):
        if name not in self._sems:
            self._sems[name] = Sem(self.nc, self.es, name)
        return self._sems[name]

    def emit(self, eng, fn):
        self.streams[eng].append(fn)

    def wait(self, eng, sem, val):
        if val > 0:
            self.emit(eng, lambda e, s=sem.h, v=val: e.wait_ge(s, v))

    def dma(self, eng, out, in_, sem):
        sem.n += 16
        self.emit(eng, lambda e, o=out, i=in_, s=sem.h: e.dma_start(out=o, in_=i).then_inc(s, 16))
        return sem.n

    def op(self, eng, fn, sem=None, k=1):
        if sem is None:
            self.emit(eng, fn)
            return None
        sem.n += k
        self.emit(eng, lambda e, f=fn, s=sem.h, kk=k: f(e).then_inc(s, kk))
        return sem.n

    def set_pguard(self, *entries):
        self.pguard = [(s, v) for (s, v) in entries if v]

    def guard(self, eng):
        for s, v in self.pguard:
            self.wait(eng, s, v)

    def barrier(self):
        s = self.sem("bar")
        own = {"vector": "sv", "scalar": "sx", "tensor": "sp", "gpsimd": "sg"}
        for e in ENGINES:
            if e in own and own[e] in self._sems:
                osem = self._sems[own[e]]
                self.wait(e, osem, osem.n)
            s.n += 1
            self.emit(e, lambda en, h=s.h: en.nop().then_inc(h, 1))
        tgt = s.n
        for e in ENGINES:
            self.wait(e, s, tgt)

    def finalize(self, block):
        decs = {"sync": block.sync, "scalar": block.scalar, "vector": block.vector,
                "tensor": block.tensor, "gpsimd": block.gpsimd}
        for name, dec in decs.items():
            ops = list(self.streams[name])
            def make(ops_):
                def _(eng):
                    for f in ops_:
                        f(eng)
                return _
            dec(make(ops))


class Ring:

    def __init__(self, P, name, n):
        self.P = P
        self.n = n
        self.sems = [P.sem(f"{name}_{i}") for i in range(n)]
        self.cons = [None] * n
        self.val = [0] * n

    def load(self, eng, i, pairs):
        s = i % self.n
        if self.cons[s] is not None:
            for c_ in self.cons[s]:
                self.P.wait(eng, *c_)
            self.cons[s] = None
        for dst, src in pairs:
            self.P.dma(eng, dst, src, self.sems[s])
        self.val[s] = self.sems[s].n
        return (self.sems[s], self.val[s])

    def ready(self, eng, i):
        s = i % self.n
        self.P.wait(eng, self.sems[s], self.val[s])

    def consumed(self, i, sem, val):
        self.cons[i % self.n] = [(sem, val)]

    def consumed_multi(self, i, entries):
        self.cons[i % self.n] = [e_ for e_ in entries if e_[1]]

    def guard_write(self, eng, i):
        s = i % self.n
        if self.val[s]:
            self.P.wait(eng, self.sems[s], self.val[s])

    def totals(self):
        return [(s, s.n) for s in self.sems]


def build_program():
    nc = bass.Bass("TRN2", target_bir_lowering=False)
    ap = lambda t: t.ap()

    XP = ap(nc.dram_tensor("XP", [T, D], F32, kind="ExternalInput"))
    XNAT = ap(nc.dram_tensor("XNAT", [T, D], F32, kind="ExternalInput"))
    WQ = ap(nc.dram_tensor("WQ", [D, D], F32, kind="ExternalInput"))
    WK = ap(nc.dram_tensor("WK", [D, D], F32, kind="ExternalInput"))
    WV = ap(nc.dram_tensor("WV", [D, D], F32, kind="ExternalInput"))
    WO = ap(nc.dram_tensor("WO", [D, D], F32, kind="ExternalInput"))
    QWGU = ap(nc.dram_tensor("QWGU", [512, 2 * INTER], U32, kind="ExternalInput"))
    SCGU = ap(nc.dram_tensor("SCGU", [32, 2 * INTER], F32, kind="ExternalInput"))
    QZGU = ap(nc.dram_tensor("QZGU", [32, 2752], U32, kind="ExternalInput"))
    QWDN = ap(nc.dram_tensor("QWDN", [1376, D], U32, kind="ExternalInput"))
    SCDN = ap(nc.dram_tensor("SCDN", [86, D], F32, kind="ExternalInput"))
    QZDN = ap(nc.dram_tensor("QZDN", [86, 512], U32, kind="ExternalInput"))
    SCEXP = ap(nc.dram_tensor("SCEXP", [128, 88 * 1024], F32, kind="ExternalInput"))
    SCDEXP = ap(nc.dram_tensor("SCDEXP", [128, 11 * D], F32, kind="ExternalInput"))
    ROPEC = ap(nc.dram_tensor("ROPEC", [128, T], F32, kind="ExternalInput"))
    ROPES = ap(nc.dram_tensor("ROPES", [128, T], F32, kind="ExternalInput"))
    PERMM = ap(nc.dram_tensor("PERMM", [128, 128], F32, kind="ExternalInput"))
    IDTI = ap(nc.dram_tensor("IDTI", [128, 128], F32, kind="ExternalInput"))
    ENEG = ap(nc.dram_tensor("ENEG", [128, 128], F32, kind="ExternalInput"))
    ENEGD = ap(nc.dram_tensor("ENEGD", [128, 11 * 86], F32, kind="ExternalInput"))
    ONESC = ap(nc.dram_tensor("ONESC", [128, 1], F32, kind="ExternalInput"))
    ONESR = ap(nc.dram_tensor("ONESR", [1, 128], F32, kind="ExternalInput"))
    MASKS = ap(nc.dram_tensor("MASKS", [128, 4 * SEG], F32, kind="ExternalInput"))
    BIAST = ap(nc.dram_tensor("BIAST", [128, 33], F32, kind="ExternalInput"))

    OUT = ap(nc.dram_tensor("OUT", [T, D], F32, kind="ExternalOutput"))

    agK_in = ap(nc.dram_tensor("agK_in", [D, T], BF16))
    agV_in = ap(nc.dram_tensor("agV_in", [T, D], BF16))
    agK_out = ap(nc.dram_tensor("agK_out", [NCORES * D, T], BF16, addr_space="Shared" if AGSHARED else "Local"))
    agV_out = ap(nc.dram_tensor("agV_out", [NCORES * T, D], BF16, addr_space="Shared" if AGSHARED else "Local"))
    QSP = ap(nc.dram_tensor("QSP", [D, T], BF16))
    WQB = ap(nc.dram_tensor("WQB", [D, D], BF16))
    WKB = ap(nc.dram_tensor("WKB", [D, D], BF16))
    WVB = ap(nc.dram_tensor("WVB", [D, D], BF16))
    WOB = ap(nc.dram_tensor("WOB", [D, D], BF16))
    SCEXPB = ap(nc.dram_tensor("SCEXPB", [128, 88 * 1024], BF16))
    SCDEXPB = ap(nc.dram_tensor("SCDEXPB", [128, 11 * D], BF16))
    SCGUB = ap(nc.dram_tensor("SCGUB", [32, 2 * INTER], BF16))
    SCDNB = ap(nc.dram_tensor("SCDNB", [86, D], BF16))
    HTS = ap(nc.dram_tensor("HTS", [INTER, T], BF16))
    GSP = ap(nc.dram_tensor("GSP", [T, D], BF16))

    dbg = {}
    if DEBUG:
        def dbgt(name, shape, dt):
            dbg[name] = ap(nc.dram_tensor("DBG_" + name, shape, dt, kind="ExternalOutput"))
        dbgt("XNT", [D, T], BF16)
        dbgt("SG", [32, T], BF16)
        if PHASES >= 2:
            dbgt("QSP", [D, T], BF16)
            dbgt("KIN", [D, T], BF16)
            dbgt("VIN", [T, D], BF16)
        if PHASES >= 3:
            dbgt("HTS", [INTER, T], BF16)
            dbgt("SH", [86, T], BF16)
        if PHASES >= 4:
            dbgt("GSP", [T, D], BF16)
        if PHASES >= 5:
            dbgt("SPRE", [D, T], BF16)

    es = ExitStack()
    with es:
        P = Prog(nc, es)
        sb = lambda name, shape, dt: es.enter_context(nc.sbuf_tensor(name, shape, dt))

        c_idt = sb("c_idt", [128, 128], BF16)
        c_perm = sb("c_perm", [128, 128], BF16)
        c_eneg = sb("c_eneg", [128, 128], BF16)
        c_enegd = sb("c_enegd", [128, 11 * 86], BF16)
        c_ones = sb("c_ones", [128, 1], BF16)
        c_onesr = sb("c_onesr", [1, 128], F32)
        c_masks = sb("c_masks", [128, 4 * SEG], BF16)
        c_bias = sb("c_bias", [128, 33], F32)
        c_ropec = sb("c_ropec", [128, T], BF16)
        c_ropes = sb("c_ropes", [128, T], BF16)
        sgneg = sb("sgneg", [32, T], BF16)
        shneg = sb("shneg", [86, T], BF16)

        sv = P.sem("sv")
        sx = P.sem("sx")
        sp = P.sem("sp")
        sg = P.sem("sg")
        cc = P.sem("cc")
        qdbg = P.sem("qdbg")

        qc_g = P.sem("qc_g")
        qc_s = P.sem("qc_s")
        for dst, src in [(c_idt, IDTI), (c_perm, PERMM), (c_eneg, ENEG), (c_enegd, ENEGD),
                         (c_ones, ONESC), (c_masks, MASKS), (c_ropec, ROPEC), (c_ropes, ROPES)]:
            P.dma("gpsimd", dst[:], src, qc_g)
        P.dma("sync", c_onesr[:], ONESR, qc_s)
        P.dma("sync", c_bias[:], BIAST, qc_s)
        pc_sems = {}
        pc_chunk = {}

        def precast_cols(nm_, dst_, src_, nch):
            rows = dst_.shape[0]
            step = max(1, rows // nch)
            vals = []
            for k_ in range(nch):
                s_ = P.sem(f"qpc_{nm_}_{k_}")
                P.dma("gpsimd", dst_[k_ * step:(k_ + 1) * step, :],
                      src_[k_ * step:(k_ + 1) * step, :], s_)
                vals.append((s_, s_.n))
            pc_chunk[nm_] = vals
            pc_sems[nm_] = vals[-1]

        def wait_pc(eng, nm_):
            for c_ in pc_chunk[nm_]:
                P.wait(eng, *c_)

        precast_cols("wqb", WQB, WQ, 4)
        for c_ in pc_chunk["wqb"]:
            P.wait("gpsimd", *c_)
        precast_cols("wkb", WKB, WK, 4)
        precast_cols("wvb", WVB, WV, 4)
        precast_cols("scexpb", SCEXPB, SCEXP, 4)
        precast_cols("wob", WOB, WO, 4)
        precast_cols("scdexpb", SCDEXPB, SCDEXP, 2)
        precast_cols("scgub", SCGUB, SCGU, 1)
        precast_cols("scdnb", SCDNB, SCDN, 1)
        QCG = qc_g.n
        QCS = qc_s.n
        for e_ in ENGINES:
            P.wait(e_, qc_g, QCG)
            P.wait(e_, qc_s, QCS)

        scopeA = ExitStack()
        sbA = lambda name, shape, dt: scopeA.enter_context(nc.sbuf_tensor(name, shape, dt))

        xnt = sbA("xnt", [128, 32 * 1024], BF16)
        xnt_v = xnt[:].rearrange("p (j c) -> p j c", c=1024)

        s1 = ExitStack()
        sb1 = lambda name, shape, dt: s1.enter_context(nc.sbuf_tensor(name, shape, dt))
        ps1 = lambda name, shape, dt=F32: s1.enter_context(nc.psum_tensor(name, shape, dt))
        xp = [sb1(f"xp{i}", [128, D], F32) for i in range(2)]
        xsq = [sb1(f"xsq{i}", [128, D], F32) for i in range(2)]
        xnb = [sb1(f"xnb{i}", [128, D], BF16) for i in range(2)]
        st1 = sb1("st1", [128, 8], F32)
        ptr_ = [ps1(f"ptr{i}", [128, 1024], BF16) for i in range(2)]
        psg = ps1("psg", [32, 1024], F32)

        r_xp = Ring(P, "r_xp", 2)
        done_tr = {}
        evac_tr = {}
        sq_cons = {}

        for t in range(8):
            r_xp.load("sync", t, [(xp[t % 2][:], XP[128 * t:128 * (t + 1), :])])
            r_xp.ready("scalar", t)
            if t >= 2:
                P.wait("scalar", sv, sq_cons[t - 2])
            sq_done = P.op("scalar", lambda e, a=xsq[t % 2], b=xp[t % 2]:
                           e.activation(out=a[:], in_=b[:], func=AF.Square), sem=sx)
            r_xp.ready("vector", t)
            m1 = P.op("vector", lambda e, o=st1, i=xp[t % 2]:
                      e.reduce_sum(out=o[:, 0:1], in_=i[:], axis=AX.X), sem=sv)
            P.wait("vector", sx, sq_done)
            m2 = P.op("vector", lambda e, o=st1, i=xsq[t % 2]:
                      e.reduce_sum(out=o[:, 1:2], in_=i[:], axis=AX.X), sem=sv)
            sq_cons[t] = m2
            P.wait("vector", sv, m2)
            m3 = P.op("vector", lambda e, o=st1: e.tensor_scalar(
                out=o[:, 2:3], in0=o[:, 0:1], scalar1=1.0 / D, scalar2=None, op0=ALU.mult), sem=sv)
            P.wait("vector", sv, m3)
            m4 = P.op("vector", lambda e, o=st1: e.tensor_tensor(
                out=o[:, 3:4], in0=o[:, 2:3], in1=o[:, 2:3], op=ALU.mult), sem=sv)
            P.wait("vector", sv, m4)
            m5 = P.op("vector", lambda e, o=st1: e.tensor_scalar(
                out=o[:, 3:4], in0=o[:, 3:4], scalar1=-1.0, scalar2=EPS,
                op0=ALU.mult, op1=ALU.add), sem=sv)
            P.wait("scalar", sv, m5)
            m6 = P.op("scalar", lambda e, o=st1: e.activation(
                out=o[:, 4:5], in_=o[:, 1:2], func=AF.Sqrt, bias=o[:, 3:4], scale=1.0 / D), sem=sx)
            P.wait("vector", sx, m6)
            m7 = P.op("vector", lambda e, o=st1: e.reciprocal(out=o[:, 5:6], in_=o[:, 4:5]), sem=sv)
            P.wait("vector", sv, m7)
            m8 = P.op("vector", lambda e, o=st1: e.tensor_scalar(
                out=o[:, 6:7], in0=o[:, 2:3], scalar1=o[:, 5:6], scalar2=-1.0,
                op0=ALU.mult, op1=ALU.mult), sem=sv)
            P.wait("vector", sv, m8)
            if t >= 2:
                P.wait("vector", sp, done_tr[(t - 2, 3)])
            m9 = P.op("vector", lambda e, o=xnb[t % 2], i=xp[t % 2], s=st1: e.tensor_scalar(
                out=o[:], in0=i[:], scalar1=s[:, 5:6], scalar2=None, op0=ALU.mult), sem=sv)
            P.wait("vector", sv, m9)
            m10 = P.op("vector", lambda e, o=xnb[t % 2], s=st1: e.tensor_scalar(
                out=o[:], in0=o[:], scalar1=s[:, 6:7], scalar2=None, op0=ALU.add), sem=sv)
            r_xp.consumed_multi(t, [(sv, m10), (sx, sq_done)])

            P.wait("tensor", sv, m10)
            if t == 0:
                P.wait("tensor", qc_g, QCG)
            for j8 in range(4):
                key = (t, j8)
                prevkey = (t, j8 - 2) if j8 >= 2 else ((t - 1, j8 + 2) if t >= 1 else None)
                if prevkey is not None and prevkey in evac_tr:
                    P.wait("tensor", sv, evac_tr[prevkey])
                for i in range(8):
                    j = 8 * j8 + i
                    done_tr[key] = P.op("tensor", lambda e, pt=ptr_[j8 % 2], ii=i, xb=xnb[t % 2], jj=j:
                                        e.transpose(out=pt[:, 128 * ii:128 * (ii + 1)],
                                                    in_=xb[:, 128 * jj:128 * (jj + 1)],
                                                    identity=c_idt[:]), sem=sp)
                P.wait("vector", sp, done_tr[key])
                evac_tr[key] = P.op("vector", lambda e, pt=ptr_[j8 % 2], jb=j8, tt=t:
                                    e.tensor_copy(
                                        out=xnt_v[:, 8 * jb:8 * (jb + 1), 128 * tt:128 * (tt + 1)],
                                        in_=pt[:].rearrange("p (i c) -> p i c", c=128)), sem=sv)

        xnt_ready = evac_tr[(7, 3)]
        P.wait("tensor", sv, xnt_ready)
        last_mm = None
        for w in range(4):
            for n in range(8):
                j = 8 * w + n
                for hf in range(2):
                    last_mm = P.op("tensor", lambda e, jj=j, h=hf, ww=w,
                                   st=(w == 0 and n == 0), sp_=(w == 3 and n == 7):
                                   e.matmul(out=psg[0:32, 512 * h:512 * (h + 1)],
                                            lhsT=c_eneg[:, 32 * ww:32 * (ww + 1)],
                                            rhs=xnt[:, 1024 * jj + 512 * h:1024 * jj + 512 * (h + 1)],
                                            start=st, stop=sp_), sem=sp)
        P.wait("vector", sp, last_mm)
        sg_evac = P.op("vector", lambda e: e.tensor_copy(
            out=sgneg[0:32, :], in_=psg[0:32, :]), sem=sv)

        if DEBUG:
            P.wait("sync", sv, sg_evac)
            P.dma("sync", dbg["XNT"].rearrange("(j p) c -> p j c", p=128),
                  xnt[:].rearrange("p (j c) -> p j c", c=1024), qdbg)
            P.dma("sync", dbg["SG"], sgneg[:], qdbg)

        s1.close()
        P.set_pguard((sv, sg_evac))
        P.barrier()

        if PHASES >= 2:
            s2 = ExitStack()
            sb2 = lambda name, shape, dt: s2.enter_context(nc.sbuf_tensor(name, shape, dt))
            s2p = ExitStack()
            ps2 = lambda name, shape, dt=F32: s2p.enter_context(nc.psum_tensor(name, shape, dt))
            wst = [sb2(f"wst{i}", [128, 32 * 128], BF16) for i in range(3)]
            rx = [sb2(f"rx{i}", [128, 512], BF16) for i in range(4)]
            t1b = [sb2(f"t1b{i}", [128, 512], F32) for i in range(4)]
            t2b = [sb2(f"t2b{i}", [128, 512], F32) for i in range(4)]
            qkst = [sb2(f"qkst{i}", [128, 512], BF16) for i in range(4)]
            psq = [ps2(f"psq{i}", [128, 512]) for i in range(4)]
            psw = [ps2(f"psw{i}", [128, 512]) for i in range(2)]

            r_wst = Ring(P, "r_wst", 3)
            r_qko = Ring(P, "r_qko", 4)
            rx_done = {}
            t1_done = {}
            t2_done = {}
            add_done = {}
            swap_done = {}
            idx = 0
            for (Wsrc, dstT, scale) in ((WQB, QSP, ATT_SCALE), (WKB, agK_in, 1.0)):
                wait_pc("sync", "wqb" if idx == 0 else "wkb")
                for oc in range(32):
                    gi = idx * 32 + oc
                    r_wst.load("sync", gi,
                               [(wst[gi % 3][:].rearrange("p (j c) -> p j c", c=128),
                                 Wsrc[:, 128 * oc:128 * (oc + 1)].rearrange("(j p) c -> p j c", p=128))])
                    for hf in range(2):
                        ii = 2 * gi + hf
                        if ii == 0:
                            P.guard("tensor")
                        r_wst.ready("tensor", gi)
                        if ii >= 4:
                            P.wait("tensor", sx, rx_done[ii - 4])
                        mm_last = None
                        for j in range(32):
                            mm_last = P.op("tensor", lambda e, jj=j, h=hf, b=gi % 3, pq=psq[ii % 4]:
                                           e.matmul(out=pq[:],
                                                    lhsT=wst[b][:, 128 * jj:128 * (jj + 1)],
                                                    rhs=xnt[:, 1024 * jj + 512 * h:1024 * jj + 512 * (h + 1)],
                                                    start=(jj == 0), stop=(jj == 31)), sem=sp)
                        if hf == 1:
                            r_wst.consumed(gi, sp, mm_last)
                        P.wait("scalar", sp, mm_last)
                        if ii >= 4:
                            P.wait("scalar", sp, swap_done[ii - 4])
                            P.wait("scalar", sg, t2_done[ii - 4])
                        rd = P.op("scalar", lambda e, b=ii % 4, pq=psq[ii % 4], s=scale:
                                  e.mul(out=rx[b][:], in_=pq[:], mul=s), sem=sx)
                        rx_done[ii] = rd
                        P.wait("tensor", sx, rd)
                        if ii >= 2:
                            P.wait("tensor", sv, t1_done[ii - 2])
                        sw = P.op("tensor", lambda e, b=ii % 4, w2=ii % 2:
                                  e.matmul(out=psw[w2][:], lhsT=c_perm[:], rhs=rx[b][:],
                                           start=True, stop=True), sem=sp)
                        swap_done[ii] = sw
                        P.wait("vector", sp, sw)
                        if ii >= 4:
                            P.wait("vector", sv, add_done[ii - 4])
                        td1 = P.op("vector", lambda e, b=ii % 4, h=hf, w2=ii % 2:
                                   e.tensor_tensor(out=t1b[b][:], in0=psw[w2][:],
                                                   in1=c_ropes[:, 512 * h:512 * (h + 1)],
                                                   op=ALU.mult), sem=sv)
                        t1_done[ii] = td1
                        P.wait("gpsimd", sx, rd)
                        if ii == 0:
                            P.wait("gpsimd", qc_g, QCG)
                        if ii >= 4:
                            P.wait("gpsimd", sv, add_done[ii - 4])
                        td2 = P.op("gpsimd", lambda e, b=ii % 4, h=hf:
                                   e.tensor_tensor(out=t2b[b][:], in0=rx[b][:],
                                                   in1=c_ropec[:, 512 * h:512 * (h + 1)],
                                                   op=ALU.mult), sem=sg)
                        t2_done[ii] = td2
                        P.wait("vector", sg, td2)
                        P.wait("vector", sv, td1)
                        r_qko.guard_write("vector", ii)
                        ad = P.op("vector", lambda e, b=ii % 4:
                                  e.tensor_tensor(out=qkst[b][:], in0=t1b[b][:], in1=t2b[b][:],
                                                  op=ALU.add), sem=sv)
                        add_done[ii] = ad
                        P.wait("scalar", sv, ad)
                        r_qko.load("scalar", ii,
                                   [(dstT[128 * oc:128 * (oc + 1), 512 * hf:512 * (hf + 1)],
                                     qkst[ii % 4][:])])
                idx += 1
            s2p.close()
            for s_, v_ in r_qko.totals():
                P.wait("gpsimd", s_, v_)
            P.op("gpsimd", lambda e: e.collective_compute(
                "AllGather", ALU.bypass, replica_groups=[list(range(NCORES))],
                ins=[agK_in.opt()], outs=[agK_out.opt()]), sem=cc)
            P.set_pguard((sx, rx_done[127]), (sv, max(t1_done[127], add_done[127])))
            P.barrier()

            s2v = ExitStack()
            ps2v = lambda name, shape: s2v.enter_context(nc.psum_tensor(name, shape, F32))
            psv = [ps2v(f"psv{i}", [128, 512]) for i in range(8)]
            vw = [sb2(f"vw{i}", [128, 512], BF16) for i in range(4)]
            vst = [sb2(f"vst{i}", [128, 512], BF16) for i in range(2)]
            r_vw = Ring(P, "r_vw", 4)
            r_vo = Ring(P, "r_vo", 2)
            v_evac = {}
            wait_pc("sync", "wvb")
            for fc in range(8):
                for j in range(32):
                    gi = fc * 32 + j
                    r_vw.load("sync", gi,
                              [(vw[gi % 4][:], WVB[128 * j:128 * (j + 1), 512 * fc:512 * (fc + 1)])])
                    r_vw.ready("tensor", gi)
                    if gi == 0:
                        P.guard("tensor")
                    if j == 0 and fc > 0:
                        P.wait("tensor", sx, v_evac[(fc - 1, 7)])
                    last = None
                    for t in range(8):
                        last = P.op("tensor", lambda e, jj=j, tt=t, b=gi % 4:
                                    e.matmul(out=psv[tt][:],
                                             lhsT=xnt[:, 1024 * jj + 128 * tt:1024 * jj + 128 * (tt + 1)],
                                             rhs=vw[b][:],
                                             start=(jj == 0), stop=(jj == 31)), sem=sp)
                    r_vw.consumed(gi, sp, last)
                v_fill = last
                for t in range(8):
                    gi2 = fc * 8 + t
                    P.wait("scalar", sp, v_fill)
                    r_vo.guard_write("scalar", gi2)
                    ev = P.op("scalar", lambda e, tt=t: e.copy(out=vst[tt % 2][:], in_=psv[tt][:]), sem=sx)
                    v_evac[(fc, t)] = ev
                    P.wait("scalar", sx, ev)
                    r_vo.load("scalar", gi2,
                              [(agV_in[128 * t:128 * (t + 1), 512 * fc:512 * (fc + 1)], vst[t % 2][:])])
            s2v.close()

            for s_, v_ in r_vo.totals():
                P.wait("gpsimd", s_, v_)
            P.op("gpsimd", lambda e: e.collective_compute(
                "AllGather", ALU.bypass, replica_groups=[list(range(NCORES))],
                ins=[agV_in.opt()], outs=[agV_out.opt()]), sem=cc)

            if DEBUG:
                for s_, v_ in r_qko.totals() + r_vo.totals():
                    P.wait("sync", s_, v_)
                P.dma("sync", dbg["QSP"], QSP, qdbg)
                P.dma("sync", dbg["KIN"], agK_in, qdbg)
                P.dma("sync", dbg["VIN"], agV_in, qdbg)
            s2.close()
            P.set_pguard((sx, v_evac[(7, 7)]))
            P.barrier()

        if PHASES >= 3:
            s3 = ExitStack()
            sb3 = lambda name, shape, dt: s3.enter_context(nc.sbuf_tensor(name, shape, dt))
            ps3 = lambda name, shape, dt=F32: s3.enter_context(nc.psum_tensor(name, shape, dt))
            qst = [sb3(f"qst{i}", [128, 1024], U32) for i in range(2)]
            lowb = [sb3(f"lowb{i}", [128, 1024], U32) for i in range(2)]
            highb = [sb3(f"highb{i}", [128, 1024], U32) for i in range(2)]
            scb = [sb3(f"scb{i}", [128, 1024], BF16) for i in range(2)]
            w1t = [sb3(f"w1t{i}", [128, 32 * 256], BF16) for i in range(3)]
            zqst = [sb3(f"zqst{i}", [32, 256], U32) for i in range(2)]
            sczb = [sb3(f"sczb{i}", [32, 256], BF16) for i in range(2)]
            zu32 = sb3("zu32", [32, 256], U32)
            zf32 = sb3("zf32", [32, 256], F32)
            zbuf = [sb3(f"zbuf{i}", [32, 256], BF16) for i in range(2)]
            hgate = [sb3(f"hgate{i}", [128, 512], BF16) for i in range(2)]
            hchunk = [sb3(f"hchunk{i}", [128, 1024], BF16) for i in range(2)]
            pg = [ps3(f"pg{i}", [128, 512]) for i in range(2)]
            pu = [ps3(f"pu{i}", [128, 512]) for i in range(2)]
            psh = ps3("psh", [86, 1024])

            r_qw = Ring(P, "r_qw", 2)
            r_sc = Ring(P, "r_sc", 2)
            r_zq = Ring(P, "r_zq", 2)
            r_scz = Ring(P, "r_scz", 2)
            r_hsp = Ring(P, "r_hsp", 2)
            mult_done = {}
            h_done = {}
            gu_fill = {}
            hg_done = {}
            bit_done = {}
            sh_mm_done = {}

            for c_ in pc_chunk["scgub"]:
                P.wait("scalar", *c_)
            z_done = {}
            scexp_waited = [0]

            def stageA(ci):
                wp, n, kk = DN_CHUNKS[ci]
                cb = 1024 * wp + kk * n
                b = ci % 2
                r_qw.load("sync", ci, [
                    (qst[b][:].rearrange("p (w g c) -> p w g c", w=4, g=2)[:, :, 0, 0:kk],
                     QWGU[:, cb:cb + kk].rearrange("(w p) c -> p w c", p=128)),
                    (qst[b][:].rearrange("p (w g c) -> p w g c", w=4, g=2)[:, :, 1, 0:kk],
                     QWGU[:, INTER + cb:INTER + cb + kk].rearrange("(w p) c -> p w c", p=128)),
                ])
                if scexp_waited[0] == 0:
                    wait_pc("scalar", "scexpb")
                    scexp_waited[0] = 1
                r_sc.load("scalar", ci, [(scb[b][:], SCEXPB[:, 1024 * ci:1024 * (ci + 1)])])
                r_zq.load("sync", ci, [
                    (zqst[b][0:32, 0:kk], QZGU[:, 128 * wp:128 * wp + kk]),
                    (zqst[b][0:32, 128:128 + kk], QZGU[:, 1376 + 128 * wp:1376 + 128 * wp + kk]),
                ])
                r_scz.load("scalar", ci, [
                    (sczb[b][0:32, 0:kk], SCGUB[:, cb:cb + kk]),
                    (sczb[b][0:32, 128:128 + kk], SCGUB[:, INTER + cb:INTER + cb + kk]),
                ])

                r_qw.ready("vector", ci)
                if ci >= 2:
                    P.wait("vector", sv, mult_done[ci - 2][0])
                    P.wait("vector", sg, mult_done[ci - 2][1])
                P.op("vector", lambda e, bb=b: e.tensor_scalar(
                    out=lowb[bb][:], in0=qst[bb][:], scalar1=0x0F0F0F0F, scalar2=None,
                    op0=ALU.bitwise_and), sem=sv)
                bd = P.op("vector", lambda e, bb=b: e.tensor_scalar(
                    out=highb[bb][:], in0=qst[bb][:], scalar1=4, scalar2=0x0F0F0F0F,
                    op0=ALU.logical_shift_right, op1=ALU.bitwise_and), sem=sv)
                bit_done[ci] = bd
                r_qw.consumed(ci, sv, bd)

                r_zq.ready("vector", ci)
                r_scz.ready("vector", ci)
                if ci >= 2:
                    P.wait("vector", sp, gu_fill[(ci - 2, 1, 1)])
                z1 = P.op("vector", lambda e, bb=b, nn=n: e.tensor_scalar(
                    out=zu32[0:32, :], in0=zqst[bb][:], scalar1=4 * nn, scalar2=15,
                    op0=ALU.logical_shift_right, op1=ALU.bitwise_and), sem=sv)
                r_zq.consumed(ci, sv, z1)
                P.wait("vector", sv, z1)
                z2 = P.op("vector", lambda e: e.tensor_scalar(
                    out=zf32[0:32, :], in0=zu32[0:32, :], scalar1=1.0, scalar2=None,
                    op0=ALU.add), sem=sv)
                P.wait("vector", sv, z2)
                z3 = P.op("vector", lambda e, bb=b: e.tensor_tensor(
                    out=zbuf[bb][0:32, :], in0=zf32[0:32, :], in1=sczb[bb][0:32, :],
                    op=ALU.mult), sem=sv)
                z_done[ci] = z3
                r_scz.consumed(ci, sv, z3)

                P.wait("vector", sv, bd)
                r_sc.ready("vector", ci)
                P.wait("gpsimd", sv, bd)
                r_sc.ready("gpsimd", ci)
                if ci >= 3:
                    P.wait("vector", sp, gu_fill[(ci - 3, 1, 1)])
                    P.wait("gpsimd", sp, gu_fill[(ci - 3, 1, 1)])
                mv = mg = 0
                for j in range(32):
                    w, nib = j // 8, j % 8
                    plane = (highb if (nib % 2) else lowb)[b]
                    byte_off = 1024 * w + (nib // 2)
                    eng = "vector" if (j % 2 == 0) else "gpsimd"
                    msem = sv if eng == "vector" else sg
                    val = P.op(eng, lambda e, pl=plane, bo=byte_off, ww=w, bb=b, w3=ci % 3, jj=j: e.tensor_tensor(
                        out=w1t[w3][:, 256 * jj:256 * (jj + 1)],
                        in0=pl[:].bitcast(U8)[:, bo:(bo - bo % 1024) + 1024:4],
                        in1=scb[bb][:, 256 * ww:256 * (ww + 1)],
                        op=ALU.mult), sem=msem)
                    if eng == "vector":
                        mv = val
                    else:
                        mg = val
                mult_done[ci] = (mv, mg)
                r_sc.consumed_multi(ci, [(sv, mv), (sg, mg)])

            def emit_sh_tail(cj):
                wpj, nj, kkj = DN_CHUNKS[cj]
                bj = cj % 2
                P.wait("tensor", sv, h_done[(cj, 1)])
                for hf2 in range(2):
                    sh_mm_done[cj] = P.op("tensor", lambda e, h=hf2, bb=bj, k2=kkj, w2=wpj,
                                          st=(cj == 0), sp_=(cj == 87):
                                          e.matmul(out=psh[0:86, 512 * h:512 * (h + 1)],
                                                   lhsT=c_enegd[0:k2, 86 * w2:86 * (w2 + 1)],
                                                   rhs=hchunk[bb][0:k2, 512 * h:512 * (h + 1)],
                                                   start=st, stop=sp_), sem=sp)
                P.wait("scalar", sv, h_done[(cj, 1)])
                P.wait("scalar", sp, sh_mm_done[cj])
                rbj = DN_ROWBASE[cj]
                r_hsp.load("scalar", cj, [(HTS[rbj:rbj + kkj, :], hchunk[bj][0:kkj, :])])

            def stageB(ci):
                wp, n, kk = DN_CHUNKS[ci]
                b = ci % 2
                if ci == 0:
                    P.guard("tensor")
                P.wait("tensor", sv, mult_done[ci][0])
                P.wait("tensor", sg, mult_done[ci][1])
                P.wait("tensor", sv, z_done[ci])
                for hf in range(2):
                    if ci >= 1:
                        P.wait("tensor", sx, hg_done[(ci - 1, hf)])
                        P.wait("tensor", sv, h_done[(ci - 1, hf)])
                    for gu in range(2):
                        tgt = (pg if gu == 0 else pu)[hf]
                        for j in range(32):
                            P.op("tensor", lambda e, jj=j, g=gu, h=hf, w3=ci % 3, tg=tgt, k2=kk:
                                 e.matmul(out=tg[0:k2, :],
                                          lhsT=w1t[w3][:, 256 * jj + 128 * g:256 * jj + 128 * g + k2],
                                          rhs=xnt[:, 1024 * jj + 512 * h:1024 * jj + 512 * (h + 1)],
                                          start=(jj == 0), stop=False), sem=None)
                        gu_fill[(ci, hf, gu)] = P.op("tensor", lambda e, g=gu, h=hf, bb=b, tg=tgt, k2=kk:
                                                     e.matmul(out=tg[0:k2, :],
                                                              lhsT=zbuf[bb][0:32, 128 * g:128 * g + k2],
                                                              rhs=sgneg[:, 512 * h:512 * (h + 1)],
                                                              start=False, stop=True), sem=sp)
                    P.wait("scalar", sp, gu_fill[(ci, hf, 0)])
                    hg = P.op("scalar", lambda e, h=hf, k2=kk: e.activation(
                        out=hgate[h][0:k2, :], in_=pg[h][0:k2, :], func=AF.Silu), sem=sx)
                    hg_done[(ci, hf)] = hg
                    P.wait("vector", sp, gu_fill[(ci, hf, 1)])
                    P.wait("vector", sx, hg)
                    if hf == 0:
                        r_hsp.guard_write("vector", ci)
                    if ci >= 2:
                        P.wait("vector", sp, sh_mm_done.get(ci - 2, 0))
                    hd = P.op("vector", lambda e, h=hf, bb=b, k2=kk: e.tensor_tensor(
                        out=hchunk[bb][0:k2, 512 * h:512 * (h + 1)],
                        in0=pu[h][0:k2, :], in1=hgate[h][0:k2, :], op=ALU.mult), sem=sv)
                    h_done[(ci, hf)] = hd
                if ci >= 1:
                    emit_sh_tail(ci - 1)

            stageA(0)
            for ci in range(1, 88):
                stageA(ci)
                stageB(ci - 1)
            stageB(87)
            emit_sh_tail(87)
            P.wait("vector", sp, sh_mm_done[87])
            sh_evac = P.op("vector", lambda e: e.tensor_copy(
                out=shneg[0:86, :], in_=psh[0:86, :]), sem=sv)

            if DEBUG:
                for s_, v_ in r_hsp.totals():
                    P.wait("sync", s_, v_)
                P.wait("sync", sv, sh_evac)
                P.dma("sync", dbg["HTS"], HTS, qdbg)
                P.dma("sync", dbg["SH"], shneg[:], qdbg)
            s3.close()
            for s_, v_ in r_hsp.totals():
                P.wait("sync", s_, v_)
            P.wait("sync", qdbg, qdbg.n)
            P.set_pguard((sx, hg_done[(87, 1)]), (sv, max(h_done[(87, 1)], sh_evac)))
            P.barrier()

        scopeA.close()

        if PHASES >= 4:
            s4 = ExitStack()
            sb4 = lambda name, shape, dt: s4.enter_context(nc.sbuf_tensor(name, shape, dt))
            ps4 = lambda name, shape: s4.enter_context(nc.psum_tensor(name, shape, F32))
            zd = sb4("zd", [86, D], BF16)
            zdq = sb4("zdq", [86, 512], U32)
            zdnu = sb4("zdnu", [86, 512], U32)
            zdn = sb4("zdn", [86, 512], F32)
            scd_full = sb4("scd_full", [86, D], BF16)
            qwd = [sb4(f"qwd{i}", [128, 512], U32) for i in range(2)]
            lowd = [sb4(f"lowd{i}", [128, 512], U32) for i in range(2)]
            highd = [sb4(f"highd{i}", [128, 512], U32) for i in range(2)]
            scdx = [sb4(f"scdx{i}", [128, 512], BF16) for i in range(2)]
            w1d = [sb4(f"w1d{i}", [128, 512], BF16) for i in range(3)]
            htb = [sb4(f"htb{i}", [128, 1024], BF16) for i in range(3)]
            gst = [sb4(f"gst{i}", [128, 512], BF16) for i in range(2)]
            pd = [ps4(f"pd{i}", [128, 512]) for i in range(8)]

            r_zd = Ring(P, "r_zd", 1)
            r_scd = Ring(P, "r_scd", 1)
            r_qwd = Ring(P, "r_qwd", 2)
            r_sdx = Ring(P, "r_sdx", 2)
            r_ht = Ring(P, "r_ht", 3)
            r_go = Ring(P, "r_go", 2)

            for c_ in pc_chunk["scdnb"] + pc_chunk["scdexpb"]:
                P.wait("sync", *c_)
            r_zd.load("sync", 0, [(zdq[:], QZDN)])
            r_scd.load("sync", 0, [(scd_full[:], SCDNB)])
            r_zd.ready("vector", 0)
            r_scd.ready("vector", 0)
            zprev = 0
            for n2 in range(8):
                P.wait("vector", sv, zprev)
                a1 = P.op("vector", lambda e, nn=n2: e.tensor_scalar(
                    out=zdnu[:], in0=zdq[:], scalar1=4 * nn, scalar2=15,
                    op0=ALU.logical_shift_right, op1=ALU.bitwise_and), sem=sv)
                P.wait("vector", sv, a1)
                a2 = P.op("vector", lambda e: e.tensor_scalar(
                    out=zdn[:], in0=zdnu[:], scalar1=1.0, scalar2=None, op0=ALU.add), sem=sv)
                P.wait("vector", sv, a2)
                zprev = P.op("vector", lambda e, nn=n2: e.tensor_tensor(
                    out=zd[:].rearrange("p (c e) -> p c e", e=8)[:, :, nn],
                    in0=zdn[:],
                    in1=scd_full[:].rearrange("p (c e) -> p c e", e=8)[:, :, nn],
                    op=ALU.mult), sem=sv)
            zd_ready = zprev

            dn_bit_done = {}
            dn_mult_used = {}
            g_evac = {}
            w1d_used = {}
            for hc in range(8):
                for ci, (wp, n, kk) in enumerate(DN_CHUNKS):
                    gi = hc * 88 + ci
                    b3 = gi % 3
                    wi = hc * 11 + wp
                    if n == 0:
                        r_qwd.load("sync", wi,
                                   [(qwd[wi % 2][0:kk, :],
                                     QWDN[128 * wp:128 * wp + kk, 512 * hc:512 * (hc + 1)])])
                        r_sdx.load("sync", wi,
                                   [(scdx[wi % 2][:],
                                     SCDEXPB[:, 4096 * wp + 512 * hc:4096 * wp + 512 * (hc + 1)])])
                        r_qwd.ready("vector", wi)
                        if wi >= 2:
                            P.wait("vector", sv, dn_mult_used[wi - 2][0])
                            P.wait("vector", sg, dn_mult_used[wi - 2][1])
                        P.op("vector", lambda e, w2=wi % 2, k2=kk: e.tensor_scalar(
                            out=lowd[w2][0:k2, :], in0=qwd[w2][0:k2, :], scalar1=0x0F0F0F0F,
                            scalar2=None, op0=ALU.bitwise_and), sem=sv)
                        bitd = P.op("vector", lambda e, w2=wi % 2, k2=kk: e.tensor_scalar(
                            out=highd[w2][0:k2, :], in0=qwd[w2][0:k2, :], scalar1=4,
                            scalar2=0x0F0F0F0F, op0=ALU.logical_shift_right,
                            op1=ALU.bitwise_and), sem=sv)
                        dn_bit_done[wi] = bitd
                        r_qwd.consumed(wi, sv, bitd)
                        dn_mult_mv = 0
                        dn_mult_mg = 0
                    rb = DN_ROWBASE[ci]
                    r_ht.load("scalar", gi, [(htb[b3][0:kk, :], HTS[rb:rb + kk, :])])
                    eng = "vector" if (ci % 2 == 0) else "gpsimd"
                    msem = sv if eng == "vector" else sg
                    P.wait(eng, sv, dn_bit_done[wi])
                    r_sdx.ready(eng, wi)
                    if gi >= 3:
                        P.wait(eng, sp, w1d_used[gi - 3])
                    plane = (highd if (n % 2) else lowd)[wi % 2]
                    mval = P.op(eng, lambda e, pl=plane, bo=(n // 2), b3_=b3, k2=kk, w2=wi % 2:
                                e.tensor_tensor(out=w1d[b3_][0:k2, :],
                                                in0=pl[0:k2, :].bitcast(U8)[:, bo:2048:4],
                                                in1=scdx[w2][0:k2, :], op=ALU.mult), sem=msem)
                    if eng == "vector":
                        dn_mult_mv = mval
                    else:
                        dn_mult_mg = mval
                    if n == 7:
                        dn_mult_used[wi] = (dn_mult_mv, dn_mult_mg)
                        r_sdx.consumed_multi(wi, [(sv, dn_mult_mv), (sg, dn_mult_mg)])
                    if gi == 0:
                        P.guard("tensor")
                    P.wait("tensor", msem, mval)
                    r_ht.ready("tensor", gi)
                    if ci == 0 and hc > 0:
                        P.wait("tensor", sx, g_evac[(hc - 1, 7)])
                    lastmm = None
                    for t in range(8):
                        lastmm = P.op("tensor", lambda e, tt=t, b3_=b3, k2=kk, st=(ci == 0):
                                      e.matmul(out=pd[tt][:],
                                               lhsT=htb[b3_][0:k2, 128 * tt:128 * (tt + 1)],
                                               rhs=w1d[b3_][0:k2, :],
                                               start=st, stop=False),
                                      sem=(sp if t == 7 else None))
                    r_ht.consumed(gi, sp, lastmm)
                    w1d_used[gi] = lastmm
                P.wait("tensor", sv, zd_ready)
                zf = None
                for t in range(8):
                    zf = P.op("tensor", lambda e, tt=t, h2=hc: e.matmul(
                        out=pd[tt][:], lhsT=shneg[:, 128 * tt:128 * (tt + 1)],
                        rhs=zd[:, 512 * h2:512 * (h2 + 1)], start=False, stop=True),
                        sem=(sp if t == 7 else None))
                for t in range(8):
                    gi2 = hc * 8 + t
                    P.wait("scalar", sp, zf)
                    r_go.guard_write("scalar", gi2)
                    ev = P.op("scalar", lambda e, tt=t: e.activation(
                        out=gst[tt % 2][:], in_=pd[tt][:], func=AF.Silu), sem=sx)
                    g_evac[(hc, t)] = ev
                    P.wait("scalar", sx, ev)
                    r_go.load("scalar", gi2,
                              [(GSP[128 * t:128 * (t + 1), 512 * hc:512 * (hc + 1)], gst[t % 2][:])])
            if DEBUG:
                for s_, v_ in r_go.totals():
                    P.wait("sync", s_, v_)
                P.dma("sync", dbg["GSP"], GSP, qdbg)
            s4.close()
            for s_, v_ in r_go.totals():
                P.wait("sync", s_, v_)
            P.set_pguard((sx, g_evac[(7, 7)]))
            P.barrier()

        if PHASES >= 5:
            s5 = ExitStack()
            sb5 = lambda name, shape, dt: s5.enter_context(nc.sbuf_tensor(name, shape, dt))
            s5p = ExitStack()
            ps5 = lambda name, shape, dt=F32: s5p.enter_context(nc.psum_tensor(name, shape, dt))
            slab5 = sb5("slab5", [128, 32 * 1024], BF16)
            ktb = [sb5(f"ktb{i}", [128, 512], BF16) for i in range(6)]
            vtb = [sb5(f"vtb{i}", [128, 512], BF16) for i in range(6)]
            pb = [sb5(f"pb{i}", [128, 512], BF16) for i in range(8)]
            linv = [sb5(f"linv{i}", [1, 512], F32) for i in range(2)]
            linvrep = [sb5(f"linvrep{i}", [128, 512], F32) for i in range(2)]
            ps_s = [ps5(f"ps_s{i}", [128, 512]) for i in range(3)]
            ps_o = [ps5(f"ps_o{i}", [128, 512]) for i in range(2)]
            ps_l = [ps5(f"ps_l{i}", [1, 512]) for i in range(2)]
            ps_b = ps5("ps_b", [128, 512])

            r_ql = Ring(P, "r_ql", 1)
            r_kt = Ring(P, "r_kt", 6)
            r_vt = Ring(P, "r_vt", 6)
            r_ql.load("sync", 0, [(slab5[:].rearrange("p (j c) -> p j c", c=1024),
                                   QSP.rearrange("(j p) c -> p j c", p=128))])
            P.wait("sync", cc, 2)
            P.wait("scalar", cc, 2)
            exp_done = {}
            pb_used = {}
            norm_done = {}
            for u in range(2):
                for h in range(H):
                    uh = u * H + h
                    qo = 1024 * h + 512 * u
                    def emit_outsum(ib2):
                        r2, i2 = divmod(ib2, 4)
                        gi2_ = uh * 9 + r2
                        ib_g = uh * 36 + ib2
                        if r2 == 8:
                            use_sem, use_val = sv, mask_done[ib_g]
                        else:
                            use_sem, use_val = sx, exp_done[ib_g]
                        P.wait("tensor", use_sem, use_val)
                        if i2 == 0:
                            r_vt.ready("tensor", gi2_)
                        first = (ib2 == 0)
                        last = (ib2 == 35)
                        if first and uh > 1:
                            P.wait("tensor", sv, norm_done[uh - 2])
                        ub2 = uh % 2
                        P.op("tensor", lambda e, g3=gi2_ % 6, ii=i2, ibb=ib_g, f=first, l=last, u2=ub2:
                             e.matmul(out=ps_o[u2][:], lhsT=vtb[g3][:, 128 * ii:128 * (ii + 1)],
                                      rhs=pb[ibb % 8][:], start=f, stop=l), sem=None)
                        omm = P.op("tensor", lambda e, ibb=ib_g, f=first, l=last, u2=ub2:
                                   e.matmul(out=ps_l[u2][:], lhsT=c_ones[:],
                                            rhs=pb[ibb % 8][:], start=f, stop=l), sem=sp)
                        pb_used[ib_g] = omm
                        if i2 == 3:
                            r_vt.consumed(gi2_, sp, omm)

                    mask_done = {}
                    for r in range(9):
                        gi = uh * 9 + r
                        if r < 8:
                            ksrc = agK_out[4096 * r + 128 * h:4096 * r + 128 * (h + 1),
                                           512 * u:512 * (u + 1)]
                            vsrc = agV_out[1024 * r + 512 * u:1024 * r + 512 * (u + 1),
                                           128 * h:128 * (h + 1)].rearrange("(i p) d -> p i d", p=128)
                        else:
                            ksrc = agK_in[128 * h:128 * (h + 1), 512 * u:512 * (u + 1)]
                            vsrc = agV_in[512 * u:512 * (u + 1),
                                          128 * h:128 * (h + 1)].rearrange("(i p) d -> p i d", p=128)
                        r_kt.load("sync", gi, [(ktb[gi % 6][:], ksrc)])
                        r_vt.load("scalar", gi,
                                  [(vtb[gi % 6][:].rearrange("p (i d) -> p i d", d=128), vsrc)])
                        if gi == 0:
                            P.guard("tensor")
                            r_ql.ready("tensor", 0)
                        r_kt.ready("tensor", gi)
                        for i in range(4):
                            kt = 4 * r + i
                            lb = r * 4 + i
                            ib = uh * 36 + lb
                            sb_ = ib % 3
                            if ib >= 3:
                                P.wait("tensor", sx, exp_done[ib - 3])
                            smm = P.op("tensor", lambda e, g3=gi % 6, ii=i, s2=sb_, q1=qo:
                                       e.matmul(out=ps_s[s2][:],
                                                lhsT=ktb[g3][:, 128 * ii:128 * (ii + 1)],
                                                rhs=slab5[:, q1:q1 + 512],
                                                start=True, stop=True), sem=sp)
                            if i == 3:
                                r_kt.consumed(gi, sp, smm)
                            P.wait("scalar", sp, smm)
                            if ib >= 8:
                                P.wait("scalar", sp, pb_used[ib - 8])
                            bias_col = kt if r < 8 else 32
                            ed = P.op("scalar", lambda e, s2=sb_, ibb=ib, bc=bias_col:
                                      e.activation(out=pb[ibb % 8][:], in_=ps_s[s2][:],
                                                   func=AF.Exp, bias=c_bias[:, bc:bc + 1],
                                                   scale=1.0), sem=sx)
                            exp_done[ib] = ed
                            if r == 8:
                                P.wait("vector", sx, ed)
                                md = P.op("vector", lambda e, ibb=ib, ii=i: e.tensor_tensor(
                                    out=pb[ibb % 8][:], in0=pb[ibb % 8][:],
                                    in1=c_masks[:, 512 * ii:512 * (ii + 1)], op=ALU.mult), sem=sv)
                                mask_done[ib] = md
                            if lb >= 2:
                                emit_outsum(lb - 2)
                    emit_outsum(34)
                    emit_outsum(35)
                    ub = uh % 2
                    P.wait("vector", sp, pb_used[uh * 36 + 35])
                    r1 = P.op("vector", lambda e, u2=ub: e.reciprocal(
                        out=linv[u2][:], in_=ps_l[u2][:]), sem=sv)
                    P.wait("tensor", sv, r1)
                    if uh >= 1:
                        P.wait("tensor", sv, linvrep_done[uh - 1])
                    bc2 = P.op("tensor", lambda e, u2=ub: e.matmul(
                        out=ps_b[:], lhsT=c_onesr[:], rhs=linv[u2][:], start=True, stop=True), sem=sp)
                    P.wait("vector", sp, bc2)
                    r2 = P.op("vector", lambda e, u2=ub: e.tensor_copy(
                        out=linvrep[u2][:], in_=ps_b[:]), sem=sv)
                    if uh == 0:
                        linvrep_done = {}
                    linvrep_done[uh] = r2
                    P.wait("vector", sv, r2)
                    nd = P.op("vector", lambda e, q1=qo, u2=ub: e.tensor_tensor(
                        out=slab5[:, q1:q1 + 512], in0=ps_o[u2][:], in1=linvrep[u2][:],
                        op=ALU.mult), sem=sv)
                    norm_done[uh] = nd

            if DEBUG:
                P.wait("sync", sv, norm_done[63])
                P.dma("sync", dbg["SPRE"].rearrange("(j p) c -> p j c", p=128),
                      slab5[:].rearrange("p (j c) -> p j c", c=1024), qdbg)
            s5p.close()
            P.set_pguard((sv, norm_done[63]))
            P.barrier()

            if PHASES >= 6:
                s6 = ExitStack()
                sb6 = lambda name, shape, dt: s6.enter_context(nc.sbuf_tensor(name, shape, dt))
                ps6 = lambda name, shape: s6.enter_context(nc.psum_tensor(name, shape, F32))
                wot = [sb6(f"wot{i}", [128, 512], BF16) for i in range(4)]
                xb6 = [sb6(f"xb6{i}", [128, 512], F32) for i in range(2)]
                gb6 = [sb6(f"gb6{i}", [128, 512], BF16) for i in range(2)]
                sgb = [sb6(f"sgb{i}", [128, 512], F32) for i in range(2)]
                ob6 = [sb6(f"ob6{i}", [128, 512], F32) for i in range(2)]
                pc6 = [ps6(f"pc6{i}", [128, 512]) for i in range(8)]

                r_wo = Ring(P, "r_wo", 4)
                r_xb = Ring(P, "r_xb", 2)
                r_gb = Ring(P, "r_gb", 2)
                r_out = Ring(P, "r_out", 2)
                comb = {}
                wait_pc("scalar", "wob")
                for fc in range(8):
                    for dc in range(32):
                        gi = fc * 32 + dc
                        r_wo.load("scalar", gi,
                                  [(wot[gi % 4][:], WOB[128 * dc:128 * (dc + 1), 512 * fc:512 * (fc + 1)])])
                        r_wo.ready("tensor", gi)
                        if gi == 0:
                            P.guard("tensor")
                        if dc == 0 and fc > 0:
                            P.wait("tensor", sv, comb[(fc - 1, 7)])
                        wdone = None
                        for t in range(8):
                            wdone = P.op("tensor", lambda e, tt=t, b=gi % 4, d2=dc:
                                         e.matmul(out=pc6[tt][:],
                                                  lhsT=slab5[:, 1024 * d2 + 128 * tt:1024 * d2 + 128 * (tt + 1)],
                                                  rhs=wot[b][:],
                                                  start=(d2 == 0), stop=(d2 == 31)),
                                         sem=(sp if t == 7 else None))
                        r_wo.consumed(gi, sp, wdone)
                    for t in range(8):
                        gi2 = fc * 8 + t
                        r_xb.load("sync", gi2,
                                  [(xb6[t % 2][:], XNAT[128 * t:128 * (t + 1), 512 * fc:512 * (fc + 1)])])
                        r_gb.load("scalar", gi2,
                                  [(gb6[t % 2][:], GSP[128 * t:128 * (t + 1), 512 * fc:512 * (fc + 1)])])
                        P.wait("vector", sp, wdone)
                        r_gb.ready("vector", gi2)
                        r_xb.ready("vector", gi2)
                        r_out.guard_write("vector", gi2)
                        c1 = P.op("vector", lambda e, tt=t: e.tensor_tensor(
                            out=sgb[tt % 2][:], in0=pc6[tt][:], in1=gb6[tt % 2][:],
                            op=ALU.mult), sem=sv)
                        P.wait("vector", sv, c1)
                        c2 = P.op("vector", lambda e, tt=t: e.tensor_tensor(
                            out=ob6[tt % 2][:], in0=sgb[tt % 2][:], in1=xb6[tt % 2][:],
                            op=ALU.add), sem=sv)
                        comb[(fc, t)] = c2
                        r_xb.consumed(gi2, sv, c2)
                        r_gb.consumed(gi2, sv, c2)
                        P.wait("sync", sv, c2)
                        r_out.load("sync", gi2,
                                   [(OUT[128 * t:128 * (t + 1), 512 * fc:512 * (fc + 1)], ob6[t % 2][:])])
                for s_, v_ in r_out.totals():
                    P.wait("sync", s_, v_)
                s6.close()
            s5.close()

        P.wait("sync", qdbg, qdbg.n)

        block = es.enter_context(nc.Block())
        P.finalize(block)

    counts = {e: len(P.streams[e]) for e in ENGINES}
    return nc, counts



def prep_inputs(L0_attn_x, L0_attn_wq, L0_attn_wk, L0_attn_wv, L0_attn_wo,
                L0_ffn_qweight_gu, L0_ffn_qzeros_gu, L0_ffn_scales_gu, L0_ffn_g_idx_gu,
                L0_ffn_qweight_down, L0_ffn_qzeros_down, L0_ffn_scales_down, L0_ffn_g_idx_down):
    f32 = np.float32
    x = np.asarray(L0_attn_x, f32)
    wq = np.asarray(L0_attn_wq, f32)
    wk = np.asarray(L0_attn_wk, f32)
    wv = np.asarray(L0_attn_wv, f32)
    wo = np.ascontiguousarray(np.asarray(L0_attn_wo, f32))
    qwgu = np.asarray(L0_ffn_qweight_gu, np.uint32)
    qzgu = np.ascontiguousarray(np.asarray(L0_ffn_qzeros_gu, np.uint32))
    scgu = np.asarray(L0_ffn_scales_gu, f32)
    qwdn = np.ascontiguousarray(np.asarray(L0_ffn_qweight_down, np.uint32))
    qzdn = np.ascontiguousarray(np.asarray(L0_ffn_qzeros_down, np.uint32))
    scdn = np.ascontiguousarray(np.asarray(L0_ffn_scales_down, f32))

    wq_p = np.ascontiguousarray(wq[PERM_H, :])
    wk_p = np.ascontiguousarray(wk[PERM_H, :])
    wv_p = np.ascontiguousarray(wv[PERM_H, :])
    qwgu_p = np.ascontiguousarray(
        np.concatenate([qwgu[:, :INTER][:, PERM_D], qwgu[:, INTER:][:, PERM_D]], axis=1))
    scgu_p = np.ascontiguousarray(
        np.concatenate([scgu[:, :INTER][:, PERM_D], scgu[:, INTER:][:, PERM_D]], axis=1))

    gidx = np.arange(128) // 16
    scexp = np.zeros((128, 88, 4, 2, 128), f32)
    rows_by_w = [scgu_p[8 * w + gidx] for w in range(4)]
    for ci, (wp, n, kk) in enumerate(DN_CHUNKS):
        cb = 1024 * wp + kk * n
        for w in range(4):
            scexp[:, ci, w, 0, :kk] = rows_by_w[w][:, cb:cb + kk]
            scexp[:, ci, w, 1, :kk] = rows_by_w[w][:, INTER + cb:INTER + cb + kk]
    scexp = np.ascontiguousarray(scexp.reshape(128, -1))
    scdexp = np.empty((128, 11, D), f32)
    for wp in range(11):
        ridx = np.minimum(8 * wp + gidx, 85)
        scdexp[:, wp, :] = scdn[ridx]
    scdexp = np.ascontiguousarray(scdexp.reshape(128, -1))

    permm = np.zeros((128, 128), f32)
    for i in range(0, 128, 2):
        permm[i, i + 1] = 1.0
        permm[i + 1, i] = 1.0
    eneg = np.zeros((128, 4, 32), f32)
    for w in range(4):
        eneg[np.arange(128), w, 8 * w + np.arange(128) // 16] = -1.0
    eneg = eneg.reshape(128, 128)
    enegd = np.zeros((128, 11, 86), f32)
    for wp in range(11):
        kk = 128 if wp < 10 else 96
        enegd[np.arange(kk), wp, 8 * wp + np.arange(kk) // 16] = -1.0
    enegd = enegd.reshape(128, 11 * 86)
    onesc = np.ones((128, 1), f32)
    onesr = np.ones((1, 128), f32)
    masks = np.zeros((128, 4, SEG), f32)
    for dlt in range(4):
        pp = np.arange(128)[:, None]
        jj = np.arange(SEG)[None, :]
        masks[:, dlt, :] = (pp <= jj - 128 * dlt).astype(f32)
    masks = masks.reshape(128, 4 * SEG)

    inv = 1.0 / (10000.0 ** (np.arange(0, HD, 2)[:64].astype(f32) / HD))

    per_core = []
    for c in range(NCORES):
        pos = np.concatenate([512 * c + np.arange(SEG), 512 * c + np.arange(SEG)])
        f = np.outer(pos, inv).astype(f32)
        cosf = np.cos(f)
        sinf = np.sin(f)
        ropec = np.empty((128, T), f32)
        ropes = np.empty((128, T), f32)
        ii = np.arange(64)
        ropec[2 * ii, :] = cosf.T
        ropec[2 * ii + 1, :] = cosf.T
        ropes[2 * ii, :] = -sinf.T
        ropes[2 * ii + 1, :] = sinf.T
        biast = np.zeros((128, 33), f32)
        for kt in range(32):
            if kt >= 4 * c:
                biast[:, kt] = NEG
        xa = x[0, 512 * c:512 * (c + 1)]
        xb = x[1, 512 * c:512 * (c + 1)]
        xnat = np.ascontiguousarray(np.concatenate([xa, xb], axis=0))
        xperm = np.ascontiguousarray(xnat[:, PERM_H])
        per_core.append(dict(XP=xperm, XNAT=xnat, ROPEC=ropec, ROPES=ropes, BIAST=biast))

    shared = dict(
        WQ=wq_p, WK=wk_p, WV=wv_p, WO=wo, QWGU=qwgu_p, SCGU=scgu_p, QZGU=qzgu,
        QWDN=qwdn, SCDN=scdn, QZDN=qzdn, SCEXP=scexp, SCDEXP=scdexp,
        PERMM=permm, IDTI=np.eye(128, dtype=f32), ENEG=eneg, ENEGD=enegd,
        ONESC=onesc, ONESR=onesr, MASKS=masks,
    )
    in_maps = []
    for c in range(NCORES):
        m = dict(shared)
        m.update(per_core[c])
        in_maps.append(m)
    return in_maps


def kernel(**inputs):
    in_maps = prep_inputs(**inputs)
    nc, counts = build_program()
    res = run_bass_kernel_spmd(nc, in_maps, core_ids=list(range(NCORES)))
    out = np.empty((2, 4096, D), np.float32)
    for c in range(NCORES):
        o = np.asarray(res.results[c]["OUT"])
        out[0, 512 * c:512 * (c + 1)] = o[:SEG]
        out[1, 512 * c:512 * (c + 1)] = o[SEG:]
    return out


if __name__ == "__main__":
    import sys
    if sys.argv[1:] and sys.argv[1] == "build":
        nc, counts = build_program()
        print("build ok", counts, "total", sum(counts.values()))


# version 11
# speedup vs baseline: 1.0474x; 1.0474x over previous
import os
import numpy as np
import ml_dtypes
from contextlib import ExitStack
import concourse.bass as bass
import concourse.mybir as mybir
from concourse.bass_utils import run_bass_kernel_spmd

F32 = mybir.dt.float32
BF16 = mybir.dt.bfloat16
U32 = mybir.dt.uint32
U8 = mybir.dt.uint8
AF = mybir.ActivationFunctionType
ALU = mybir.AluOpType
AX = mybir.AxisListType
bf16 = ml_dtypes.bfloat16

NCORES = 8
D = 4096
T = 1024
INTER = 11008
H = 32
HD = 128
SEG = 512
EPS = 1e-5
ATT_SCALE = 1.0 / np.sqrt(128.0)
NEG = -1e9

PHASES = int(os.environ.get("WK_PHASES", "6"))
DEBUG = os.environ.get("WK_DEBUG", "0") == "1"
AGSHARED = os.environ.get("WK_AGSHARED", "1") == "1"

ENGINES = ("sync", "scalar", "vector", "tensor", "gpsimd")


def mkperm(F, win=1024):
    perm = np.empty(F, dtype=np.int64)
    for base in range(0, F, win):
        m = min(win, F - base)
        kk = m // 8
        j = np.arange(m)
        perm[base + j] = base + 8 * (j % kk) + (j // kk)
    return perm


PERM_H = mkperm(D)
PERM_D = mkperm(INTER)
DN_CHUNKS = [(wp, n, 128 if wp < 10 else 96) for wp in range(11) for n in range(8)]
DN_ROWBASE = {}
_rb = 0
for _i, (_wp, _n, _kk) in enumerate(DN_CHUNKS):
    DN_ROWBASE[_i] = _rb
    _rb += _kk
assert _rb == INTER


class Sem:
    def __init__(self, nc, es, name):
        self.h = es.enter_context(nc.semaphore(name))
        self.n = 0


class Prog:
    def __init__(self, nc, es):
        self.nc = nc
        self.streams = {e: [] for e in ENGINES}
        self.es = es
        self._sems = {}
        self.pguard = []

    def sem(self, name):
        if name not in self._sems:
            self._sems[name] = Sem(self.nc, self.es, name)
        return self._sems[name]

    def emit(self, eng, fn):
        self.streams[eng].append(fn)

    def wait(self, eng, sem, val):
        if val > 0:
            self.emit(eng, lambda e, s=sem.h, v=val: e.wait_ge(s, v))

    def dma(self, eng, out, in_, sem):
        sem.n += 16
        self.emit(eng, lambda e, o=out, i=in_, s=sem.h: e.dma_start(out=o, in_=i).then_inc(s, 16))
        return sem.n

    def op(self, eng, fn, sem=None, k=1):
        if sem is None:
            self.emit(eng, fn)
            return None
        sem.n += k
        self.emit(eng, lambda e, f=fn, s=sem.h, kk=k: f(e).then_inc(s, kk))
        return sem.n

    def set_pguard(self, *entries):
        self.pguard = [(s, v) for (s, v) in entries if v]

    def guard(self, eng):
        for s, v in self.pguard:
            self.wait(eng, s, v)

    def barrier(self):
        s = self.sem("bar")
        own = {"vector": "sv", "scalar": "sx", "tensor": "sp", "gpsimd": "sg"}
        for e in ENGINES:
            if e in own and own[e] in self._sems:
                osem = self._sems[own[e]]
                self.wait(e, osem, osem.n)
            s.n += 1
            self.emit(e, lambda en, h=s.h: en.nop().then_inc(h, 1))
        tgt = s.n
        for e in ENGINES:
            self.wait(e, s, tgt)

    def finalize(self, block):
        decs = {"sync": block.sync, "scalar": block.scalar, "vector": block.vector,
                "tensor": block.tensor, "gpsimd": block.gpsimd}
        for name, dec in decs.items():
            ops = list(self.streams[name])
            def make(ops_):
                def _(eng):
                    for f in ops_:
                        f(eng)
                return _
            dec(make(ops))


class Ring:

    def __init__(self, P, name, n):
        self.P = P
        self.n = n
        self.sems = [P.sem(f"{name}_{i}") for i in range(n)]
        self.cons = [None] * n
        self.val = [0] * n

    def load(self, eng, i, pairs):
        s = i % self.n
        if self.cons[s] is not None:
            for c_ in self.cons[s]:
                self.P.wait(eng, *c_)
            self.cons[s] = None
        for dst, src in pairs:
            self.P.dma(eng, dst, src, self.sems[s])
        self.val[s] = self.sems[s].n
        return (self.sems[s], self.val[s])

    def ready(self, eng, i):
        s = i % self.n
        self.P.wait(eng, self.sems[s], self.val[s])

    def consumed(self, i, sem, val):
        self.cons[i % self.n] = [(sem, val)]

    def consumed_multi(self, i, entries):
        self.cons[i % self.n] = [e_ for e_ in entries if e_[1]]

    def guard_write(self, eng, i):
        s = i % self.n
        if self.val[s]:
            self.P.wait(eng, self.sems[s], self.val[s])

    def totals(self):
        return [(s, s.n) for s in self.sems]


def build_program():
    nc = bass.Bass("TRN2", target_bir_lowering=False)
    ap = lambda t: t.ap()

    XP = ap(nc.dram_tensor("XP", [T, D], F32, kind="ExternalInput"))
    XNAT = ap(nc.dram_tensor("XNAT", [T, D], F32, kind="ExternalInput"))
    WQ = ap(nc.dram_tensor("WQ", [D, D], F32, kind="ExternalInput"))
    WK = ap(nc.dram_tensor("WK", [D, D], F32, kind="ExternalInput"))
    WV = ap(nc.dram_tensor("WV", [D, D], F32, kind="ExternalInput"))
    WO = ap(nc.dram_tensor("WO", [D, D], F32, kind="ExternalInput"))
    QWGU = ap(nc.dram_tensor("QWGU", [512, 2 * INTER], U32, kind="ExternalInput"))
    SCGU = ap(nc.dram_tensor("SCGU", [32, 2 * INTER], F32, kind="ExternalInput"))
    QZGU = ap(nc.dram_tensor("QZGU", [32, 2752], U32, kind="ExternalInput"))
    QWDN = ap(nc.dram_tensor("QWDN", [1376, D], U32, kind="ExternalInput"))
    SCDN = ap(nc.dram_tensor("SCDN", [86, D], F32, kind="ExternalInput"))
    QZDN = ap(nc.dram_tensor("QZDN", [86, 512], U32, kind="ExternalInput"))
    SCEXP = ap(nc.dram_tensor("SCEXP", [128, 88 * 1024], F32, kind="ExternalInput"))
    SCDEXP = ap(nc.dram_tensor("SCDEXP", [128, 11 * D], F32, kind="ExternalInput"))
    ROPEC = ap(nc.dram_tensor("ROPEC", [128, T], F32, kind="ExternalInput"))
    ROPES = ap(nc.dram_tensor("ROPES", [128, T], F32, kind="ExternalInput"))
    PERMM = ap(nc.dram_tensor("PERMM", [128, 128], F32, kind="ExternalInput"))
    IDTI = ap(nc.dram_tensor("IDTI", [128, 128], F32, kind="ExternalInput"))
    ENEG = ap(nc.dram_tensor("ENEG", [128, 128], F32, kind="ExternalInput"))
    ENEGD = ap(nc.dram_tensor("ENEGD", [128, 11 * 86], F32, kind="ExternalInput"))
    ONESC = ap(nc.dram_tensor("ONESC", [128, 1], F32, kind="ExternalInput"))
    ONESR = ap(nc.dram_tensor("ONESR", [1, 128], F32, kind="ExternalInput"))
    MASKS = ap(nc.dram_tensor("MASKS", [128, 4 * SEG], F32, kind="ExternalInput"))
    BIAST = ap(nc.dram_tensor("BIAST", [128, 33], F32, kind="ExternalInput"))

    OUT = ap(nc.dram_tensor("OUT", [T, D], F32, kind="ExternalOutput"))

    agK_in = ap(nc.dram_tensor("agK_in", [D, T], BF16))
    agV_in = ap(nc.dram_tensor("agV_in", [T, D], BF16))
    agK_out = ap(nc.dram_tensor("agK_out", [NCORES * D, T], BF16, addr_space="Shared" if AGSHARED else "Local"))
    agV_out = ap(nc.dram_tensor("agV_out", [NCORES * T, D], BF16, addr_space="Shared" if AGSHARED else "Local"))
    QSP = ap(nc.dram_tensor("QSP", [D, T], BF16))
    SCGUB = ap(nc.dram_tensor("SCGUB", [32, 2 * INTER], BF16))
    SCDNB = ap(nc.dram_tensor("SCDNB", [86, D], BF16))
    HTS = ap(nc.dram_tensor("HTS", [INTER, T], BF16))
    GSP = ap(nc.dram_tensor("GSP", [T, D], BF16))

    dbg = {}
    if DEBUG:
        def dbgt(name, shape, dt):
            dbg[name] = ap(nc.dram_tensor("DBG_" + name, shape, dt, kind="ExternalOutput"))
        dbgt("XNT", [D, T], BF16)
        dbgt("SG", [32, T], BF16)
        if PHASES >= 2:
            dbgt("QSP", [D, T], BF16)
            dbgt("KIN", [D, T], BF16)
            dbgt("VIN", [T, D], BF16)
        if PHASES >= 3:
            dbgt("HTS", [INTER, T], BF16)
            dbgt("SH", [86, T], BF16)
        if PHASES >= 4:
            dbgt("GSP", [T, D], BF16)
        if PHASES >= 5:
            dbgt("SPRE", [D, T], BF16)

    es = ExitStack()
    with es:
        P = Prog(nc, es)
        sb = lambda name, shape, dt: es.enter_context(nc.sbuf_tensor(name, shape, dt))

        c_idt = sb("c_idt", [128, 128], BF16)
        c_perm = sb("c_perm", [128, 128], BF16)
        c_eneg = sb("c_eneg", [128, 128], BF16)
        c_enegd = sb("c_enegd", [128, 11 * 86], BF16)
        c_ones = sb("c_ones", [128, 1], BF16)
        c_onesr = sb("c_onesr", [1, 128], F32)
        c_masks = sb("c_masks", [128, 4 * SEG], BF16)
        c_bias = sb("c_bias", [128, 33], F32)
        c_ropec = sb("c_ropec", [128, T], BF16)
        c_ropes = sb("c_ropes", [128, T], BF16)
        sgneg = sb("sgneg", [32, T], BF16)
        shneg = sb("shneg", [86, T], BF16)

        sv = P.sem("sv")
        sx = P.sem("sx")
        sp = P.sem("sp")
        sg = P.sem("sg")
        cc = P.sem("cc")
        qdbg = P.sem("qdbg")

        qc_g = P.sem("qc_g")
        qc_s = P.sem("qc_s")
        for dst, src in [(c_idt, IDTI), (c_perm, PERMM), (c_eneg, ENEG), (c_enegd, ENEGD),
                         (c_ones, ONESC), (c_masks, MASKS), (c_ropec, ROPEC), (c_ropes, ROPES)]:
            P.dma("gpsimd", dst[:], src, qc_g)
        P.dma("sync", c_onesr[:], ONESR, qc_s)
        P.dma("sync", c_bias[:], BIAST, qc_s)
        pc_sems = {}
        pc_chunk = {}

        def precast_cols(nm_, dst_, src_, nch):
            rows = dst_.shape[0]
            step = max(1, rows // nch)
            vals = []
            for k_ in range(nch):
                s_ = P.sem(f"qpc_{nm_}_{k_}")
                P.dma("gpsimd", dst_[k_ * step:(k_ + 1) * step, :],
                      src_[k_ * step:(k_ + 1) * step, :], s_)
                vals.append((s_, s_.n))
            pc_chunk[nm_] = vals
            pc_sems[nm_] = vals[-1]

        def wait_pc(eng, nm_):
            for c_ in pc_chunk[nm_]:
                P.wait(eng, *c_)

        precast_cols("scgub", SCGUB, SCGU, 1)
        precast_cols("scdnb", SCDNB, SCDN, 1)
        QCG = qc_g.n
        QCS = qc_s.n
        for e_ in ENGINES:
            P.wait(e_, qc_g, QCG)
            P.wait(e_, qc_s, QCS)

        scopeA = ExitStack()
        sbA = lambda name, shape, dt: scopeA.enter_context(nc.sbuf_tensor(name, shape, dt))

        xnt = sbA("xnt", [128, 32 * 1024], BF16)
        xnt_v = xnt[:].rearrange("p (j c) -> p j c", c=1024)

        s1 = ExitStack()
        sb1 = lambda name, shape, dt: s1.enter_context(nc.sbuf_tensor(name, shape, dt))
        ps1 = lambda name, shape, dt=F32: s1.enter_context(nc.psum_tensor(name, shape, dt))
        xp = [sb1(f"xp{i}", [128, D], F32) for i in range(2)]
        xsq = [sb1(f"xsq{i}", [128, D], F32) for i in range(2)]
        xnb = [sb1(f"xnb{i}", [128, D], BF16) for i in range(2)]
        st1 = sb1("st1", [128, 8], F32)
        ptr_ = [ps1(f"ptr{i}", [128, 1024], BF16) for i in range(2)]
        psg = ps1("psg", [32, 1024], F32)

        r_xp = Ring(P, "r_xp", 2)
        done_tr = {}
        evac_tr = {}
        sq_cons = {}

        for t in range(8):
            r_xp.load("sync", t, [(xp[t % 2][:], XP[128 * t:128 * (t + 1), :])])
            r_xp.ready("scalar", t)
            if t >= 2:
                P.wait("scalar", sv, sq_cons[t - 2])
            sq_done = P.op("scalar", lambda e, a=xsq[t % 2], b=xp[t % 2]:
                           e.activation(out=a[:], in_=b[:], func=AF.Square), sem=sx)
            r_xp.ready("vector", t)
            m1 = P.op("vector", lambda e, o=st1, i=xp[t % 2]:
                      e.reduce_sum(out=o[:, 0:1], in_=i[:], axis=AX.X), sem=sv)
            P.wait("vector", sx, sq_done)
            m2 = P.op("vector", lambda e, o=st1, i=xsq[t % 2]:
                      e.reduce_sum(out=o[:, 1:2], in_=i[:], axis=AX.X), sem=sv)
            sq_cons[t] = m2
            P.wait("vector", sv, m2)
            m3 = P.op("vector", lambda e, o=st1: e.tensor_scalar(
                out=o[:, 2:3], in0=o[:, 0:1], scalar1=1.0 / D, scalar2=None, op0=ALU.mult), sem=sv)
            P.wait("vector", sv, m3)
            m4 = P.op("vector", lambda e, o=st1: e.tensor_tensor(
                out=o[:, 3:4], in0=o[:, 2:3], in1=o[:, 2:3], op=ALU.mult), sem=sv)
            P.wait("vector", sv, m4)
            m5 = P.op("vector", lambda e, o=st1: e.tensor_scalar(
                out=o[:, 3:4], in0=o[:, 3:4], scalar1=-1.0, scalar2=EPS,
                op0=ALU.mult, op1=ALU.add), sem=sv)
            P.wait("scalar", sv, m5)
            m6 = P.op("scalar", lambda e, o=st1: e.activation(
                out=o[:, 4:5], in_=o[:, 1:2], func=AF.Sqrt, bias=o[:, 3:4], scale=1.0 / D), sem=sx)
            P.wait("vector", sx, m6)
            m7 = P.op("vector", lambda e, o=st1: e.reciprocal(out=o[:, 5:6], in_=o[:, 4:5]), sem=sv)
            P.wait("vector", sv, m7)
            m8 = P.op("vector", lambda e, o=st1: e.tensor_scalar(
                out=o[:, 6:7], in0=o[:, 2:3], scalar1=o[:, 5:6], scalar2=-1.0,
                op0=ALU.mult, op1=ALU.mult), sem=sv)
            P.wait("vector", sv, m8)
            if t >= 2:
                P.wait("vector", sp, done_tr[(t - 2, 3)])
            m9 = P.op("vector", lambda e, o=xnb[t % 2], i=xp[t % 2], s=st1: e.tensor_scalar(
                out=o[:], in0=i[:], scalar1=s[:, 5:6], scalar2=None, op0=ALU.mult), sem=sv)
            P.wait("vector", sv, m9)
            m10 = P.op("vector", lambda e, o=xnb[t % 2], s=st1: e.tensor_scalar(
                out=o[:], in0=o[:], scalar1=s[:, 6:7], scalar2=None, op0=ALU.add), sem=sv)
            r_xp.consumed_multi(t, [(sv, m10), (sx, sq_done)])

            P.wait("tensor", sv, m10)
            if t == 0:
                P.wait("tensor", qc_g, QCG)
            for j8 in range(4):
                key = (t, j8)
                prevkey = (t, j8 - 2) if j8 >= 2 else ((t - 1, j8 + 2) if t >= 1 else None)
                if prevkey is not None and prevkey in evac_tr:
                    P.wait("tensor", sv, evac_tr[prevkey])
                for i in range(8):
                    j = 8 * j8 + i
                    done_tr[key] = P.op("tensor", lambda e, pt=ptr_[j8 % 2], ii=i, xb=xnb[t % 2], jj=j:
                                        e.transpose(out=pt[:, 128 * ii:128 * (ii + 1)],
                                                    in_=xb[:, 128 * jj:128 * (jj + 1)],
                                                    identity=c_idt[:]), sem=sp)
                P.wait("vector", sp, done_tr[key])
                evac_tr[key] = P.op("vector", lambda e, pt=ptr_[j8 % 2], jb=j8, tt=t:
                                    e.tensor_copy(
                                        out=xnt_v[:, 8 * jb:8 * (jb + 1), 128 * tt:128 * (tt + 1)],
                                        in_=pt[:].rearrange("p (i c) -> p i c", c=128)), sem=sv)

        xnt_ready = evac_tr[(7, 3)]
        P.wait("tensor", sv, xnt_ready)
        last_mm = None
        for w in range(4):
            for n in range(8):
                j = 8 * w + n
                for hf in range(2):
                    last_mm = P.op("tensor", lambda e, jj=j, h=hf, ww=w,
                                   st=(w == 0 and n == 0), sp_=(w == 3 and n == 7):
                                   e.matmul(out=psg[0:32, 512 * h:512 * (h + 1)],
                                            lhsT=c_eneg[:, 32 * ww:32 * (ww + 1)],
                                            rhs=xnt[:, 1024 * jj + 512 * h:1024 * jj + 512 * (h + 1)],
                                            start=st, stop=sp_), sem=sp)
        P.wait("vector", sp, last_mm)
        sg_evac = P.op("vector", lambda e: e.tensor_copy(
            out=sgneg[0:32, :], in_=psg[0:32, :]), sem=sv)

        if DEBUG:
            P.wait("sync", sv, sg_evac)
            P.dma("sync", dbg["XNT"].rearrange("(j p) c -> p j c", p=128),
                  xnt[:].rearrange("p (j c) -> p j c", c=1024), qdbg)
            P.dma("sync", dbg["SG"], sgneg[:], qdbg)

        s1.close()
        P.set_pguard((sv, sg_evac))
        P.barrier()

        if PHASES >= 2:
            s2 = ExitStack()
            sb2 = lambda name, shape, dt: s2.enter_context(nc.sbuf_tensor(name, shape, dt))
            s2p = ExitStack()
            ps2 = lambda name, shape, dt=F32: s2p.enter_context(nc.psum_tensor(name, shape, dt))
            wst = [sb2(f"wst{i}", [128, 32 * 128], BF16) for i in range(3)]
            rx = [sb2(f"rx{i}", [128, 512], BF16) for i in range(4)]
            t1b = [sb2(f"t1b{i}", [128, 512], F32) for i in range(4)]
            t2b = [sb2(f"t2b{i}", [128, 512], F32) for i in range(4)]
            qkst = [sb2(f"qkst{i}", [128, 512], BF16) for i in range(4)]
            psq = [ps2(f"psq{i}", [128, 512]) for i in range(4)]
            psw = [ps2(f"psw{i}", [128, 512]) for i in range(2)]

            r_wst = Ring(P, "r_wst", 3)
            r_qko = Ring(P, "r_qko", 4)
            rx_done = {}
            t1_done = {}
            t2_done = {}
            add_done = {}
            swap_done = {}
            idx = 0
            for (Wsrc, dstT, scale) in ((WQ, QSP, ATT_SCALE), (WK, agK_in, 1.0)):
                for oc in range(32):
                    gi = idx * 32 + oc
                    r_wst.load("gpsimd", gi,
                               [(wst[gi % 3][:].rearrange("p (j c) -> p j c", c=128),
                                 Wsrc[:, 128 * oc:128 * (oc + 1)].rearrange("(j p) c -> p j c", p=128))])
                    for hf in range(2):
                        ii = 2 * gi + hf
                        if ii == 0:
                            P.guard("tensor")
                        r_wst.ready("tensor", gi)
                        if ii >= 4:
                            P.wait("tensor", sx, rx_done[ii - 4])
                        mm_last = None
                        for j in range(32):
                            mm_last = P.op("tensor", lambda e, jj=j, h=hf, b=gi % 3, pq=psq[ii % 4]:
                                           e.matmul(out=pq[:],
                                                    lhsT=wst[b][:, 128 * jj:128 * (jj + 1)],
                                                    rhs=xnt[:, 1024 * jj + 512 * h:1024 * jj + 512 * (h + 1)],
                                                    start=(jj == 0), stop=(jj == 31)), sem=sp)
                        if hf == 1:
                            r_wst.consumed(gi, sp, mm_last)
                        P.wait("scalar", sp, mm_last)
                        if ii >= 4:
                            P.wait("scalar", sp, swap_done[ii - 4])
                            P.wait("scalar", sg, t2_done[ii - 4])
                        rd = P.op("scalar", lambda e, b=ii % 4, pq=psq[ii % 4], s=scale:
                                  e.mul(out=rx[b][:], in_=pq[:], mul=s), sem=sx)
                        rx_done[ii] = rd
                        P.wait("tensor", sx, rd)
                        if ii >= 2:
                            P.wait("tensor", sv, t1_done[ii - 2])
                        sw = P.op("tensor", lambda e, b=ii % 4, w2=ii % 2:
                                  e.matmul(out=psw[w2][:], lhsT=c_perm[:], rhs=rx[b][:],
                                           start=True, stop=True), sem=sp)
                        swap_done[ii] = sw
                        P.wait("vector", sp, sw)
                        if ii >= 4:
                            P.wait("vector", sv, add_done[ii - 4])
                        td1 = P.op("vector", lambda e, b=ii % 4, h=hf, w2=ii % 2:
                                   e.tensor_tensor(out=t1b[b][:], in0=psw[w2][:],
                                                   in1=c_ropes[:, 512 * h:512 * (h + 1)],
                                                   op=ALU.mult), sem=sv)
                        t1_done[ii] = td1
                        P.wait("gpsimd", sx, rd)
                        if ii == 0:
                            P.wait("gpsimd", qc_g, QCG)
                        if ii >= 4:
                            P.wait("gpsimd", sv, add_done[ii - 4])
                        td2 = P.op("gpsimd", lambda e, b=ii % 4, h=hf:
                                   e.tensor_tensor(out=t2b[b][:], in0=rx[b][:],
                                                   in1=c_ropec[:, 512 * h:512 * (h + 1)],
                                                   op=ALU.mult), sem=sg)
                        t2_done[ii] = td2
                        P.wait("vector", sg, td2)
                        P.wait("vector", sv, td1)
                        r_qko.guard_write("vector", ii)
                        ad = P.op("vector", lambda e, b=ii % 4:
                                  e.tensor_tensor(out=qkst[b][:], in0=t1b[b][:], in1=t2b[b][:],
                                                  op=ALU.add), sem=sv)
                        add_done[ii] = ad
                        P.wait("scalar", sv, ad)
                        r_qko.load("scalar", ii,
                                   [(dstT[128 * oc:128 * (oc + 1), 512 * hf:512 * (hf + 1)],
                                     qkst[ii % 4][:])])
                idx += 1
            s2p.close()
            for s_, v_ in r_qko.totals():
                P.wait("gpsimd", s_, v_)
            P.op("gpsimd", lambda e: e.collective_compute(
                "AllGather", ALU.bypass, replica_groups=[list(range(NCORES))],
                ins=[agK_in.opt()], outs=[agK_out.opt()]), sem=cc)
            P.set_pguard((sx, rx_done[127]), (sv, max(t1_done[127], add_done[127])))
            P.barrier()

            s2v = ExitStack()
            ps2v = lambda name, shape: s2v.enter_context(nc.psum_tensor(name, shape, F32))
            psv = [ps2v(f"psv{i}", [128, 512]) for i in range(8)]
            vw = [sb2(f"vw{i}", [128, 512], BF16) for i in range(4)]
            vst = [sb2(f"vst{i}", [128, 512], BF16) for i in range(2)]
            r_vw = Ring(P, "r_vw", 4)
            r_vo = Ring(P, "r_vo", 2)
            v_evac = {}
            for fc in range(8):
                for j in range(32):
                    gi = fc * 32 + j
                    r_vw.load("gpsimd", gi,
                              [(vw[gi % 4][:], WV[128 * j:128 * (j + 1), 512 * fc:512 * (fc + 1)])])
                    r_vw.ready("tensor", gi)
                    if gi == 0:
                        P.guard("tensor")
                    if j == 0 and fc > 0:
                        P.wait("tensor", sx, v_evac[(fc - 1, 7)])
                    last = None
                    for t in range(8):
                        last = P.op("tensor", lambda e, jj=j, tt=t, b=gi % 4:
                                    e.matmul(out=psv[tt][:],
                                             lhsT=xnt[:, 1024 * jj + 128 * tt:1024 * jj + 128 * (tt + 1)],
                                             rhs=vw[b][:],
                                             start=(jj == 0), stop=(jj == 31)), sem=sp)
                    r_vw.consumed(gi, sp, last)
                v_fill = last
                for t in range(8):
                    gi2 = fc * 8 + t
                    P.wait("scalar", sp, v_fill)
                    r_vo.guard_write("scalar", gi2)
                    ev = P.op("scalar", lambda e, tt=t: e.copy(out=vst[tt % 2][:], in_=psv[tt][:]), sem=sx)
                    v_evac[(fc, t)] = ev
                    P.wait("scalar", sx, ev)
                    r_vo.load("scalar", gi2,
                              [(agV_in[128 * t:128 * (t + 1), 512 * fc:512 * (fc + 1)], vst[t % 2][:])])
            s2v.close()

            for s_, v_ in r_vo.totals():
                P.wait("gpsimd", s_, v_)
            P.op("gpsimd", lambda e: e.collective_compute(
                "AllGather", ALU.bypass, replica_groups=[list(range(NCORES))],
                ins=[agV_in.opt()], outs=[agV_out.opt()]), sem=cc)

            if DEBUG:
                for s_, v_ in r_qko.totals() + r_vo.totals():
                    P.wait("sync", s_, v_)
                P.dma("sync", dbg["QSP"], QSP, qdbg)
                P.dma("sync", dbg["KIN"], agK_in, qdbg)
                P.dma("sync", dbg["VIN"], agV_in, qdbg)
            s2.close()
            P.set_pguard((sx, v_evac[(7, 7)]))
            P.barrier()

        if PHASES >= 3:
            s3 = ExitStack()
            sb3 = lambda name, shape, dt: s3.enter_context(nc.sbuf_tensor(name, shape, dt))
            ps3 = lambda name, shape, dt=F32: s3.enter_context(nc.psum_tensor(name, shape, dt))
            qst = [sb3(f"qst{i}", [128, 1024], U32) for i in range(2)]
            lowb = [sb3(f"lowb{i}", [128, 1024], U32) for i in range(2)]
            highb = [sb3(f"highb{i}", [128, 1024], U32) for i in range(2)]
            scb = [sb3(f"scb{i}", [128, 1024], BF16) for i in range(2)]
            w1t = [sb3(f"w1t{i}", [128, 32 * 256], BF16) for i in range(3)]
            zqst = [sb3(f"zqst{i}", [32, 256], U32) for i in range(2)]
            sczb = [sb3(f"sczb{i}", [32, 256], BF16) for i in range(2)]
            zu32 = sb3("zu32", [32, 256], U32)
            zf32 = sb3("zf32", [32, 256], F32)
            zbuf = [sb3(f"zbuf{i}", [32, 256], BF16) for i in range(2)]
            hgate = [sb3(f"hgate{i}", [128, 512], BF16) for i in range(2)]
            hchunk = [sb3(f"hchunk{i}", [128, 1024], BF16) for i in range(2)]
            pg = [ps3(f"pg{i}", [128, 512]) for i in range(2)]
            pu = [ps3(f"pu{i}", [128, 512]) for i in range(2)]
            psh = ps3("psh", [86, 1024])

            r_qw = Ring(P, "r_qw", 2)
            r_sc = Ring(P, "r_sc", 2)
            r_zq = Ring(P, "r_zq", 2)
            r_scz = Ring(P, "r_scz", 2)
            r_hsp = Ring(P, "r_hsp", 2)
            mult_done = {}
            h_done = {}
            gu_fill = {}
            hg_done = {}
            bit_done = {}
            sh_mm_done = {}

            for c_ in pc_chunk["scgub"]:
                P.wait("scalar", *c_)
            z_done = {}

            def stageA(ci):
                wp, n, kk = DN_CHUNKS[ci]
                cb = 1024 * wp + kk * n
                b = ci % 2
                r_qw.load("sync", ci, [
                    (qst[b][:].rearrange("p (w g c) -> p w g c", w=4, g=2)[:, :, 0, 0:kk],
                     QWGU[:, cb:cb + kk].rearrange("(w p) c -> p w c", p=128)),
                    (qst[b][:].rearrange("p (w g c) -> p w g c", w=4, g=2)[:, :, 1, 0:kk],
                     QWGU[:, INTER + cb:INTER + cb + kk].rearrange("(w p) c -> p w c", p=128)),
                ])
                r_sc.load("gpsimd", ci, [(scb[b][:], SCEXP[:, 1024 * ci:1024 * (ci + 1)])])
                r_zq.load("sync", ci, [
                    (zqst[b][0:32, 0:kk], QZGU[:, 128 * wp:128 * wp + kk]),
                    (zqst[b][0:32, 128:128 + kk], QZGU[:, 1376 + 128 * wp:1376 + 128 * wp + kk]),
                ])
                r_scz.load("scalar", ci, [
                    (sczb[b][0:32, 0:kk], SCGUB[:, cb:cb + kk]),
                    (sczb[b][0:32, 128:128 + kk], SCGUB[:, INTER + cb:INTER + cb + kk]),
                ])

                r_qw.ready("vector", ci)
                if ci >= 2:
                    P.wait("vector", sv, mult_done[ci - 2][0])
                    P.wait("vector", sg, mult_done[ci - 2][1])
                P.op("vector", lambda e, bb=b: e.tensor_scalar(
                    out=lowb[bb][:], in0=qst[bb][:], scalar1=0x0F0F0F0F, scalar2=None,
                    op0=ALU.bitwise_and), sem=sv)
                bd = P.op("vector", lambda e, bb=b: e.tensor_scalar(
                    out=highb[bb][:], in0=qst[bb][:], scalar1=4, scalar2=0x0F0F0F0F,
                    op0=ALU.logical_shift_right, op1=ALU.bitwise_and), sem=sv)
                bit_done[ci] = bd
                r_qw.consumed(ci, sv, bd)

                r_zq.ready("vector", ci)
                r_scz.ready("vector", ci)
                if ci >= 2:
                    P.wait("vector", sp, gu_fill[(ci - 2, 1, 1)])
                z1 = P.op("vector", lambda e, bb=b, nn=n: e.tensor_scalar(
                    out=zu32[0:32, :], in0=zqst[bb][:], scalar1=4 * nn, scalar2=15,
                    op0=ALU.logical_shift_right, op1=ALU.bitwise_and), sem=sv)
                r_zq.consumed(ci, sv, z1)
                P.wait("vector", sv, z1)
                z2 = P.op("vector", lambda e: e.tensor_scalar(
                    out=zf32[0:32, :], in0=zu32[0:32, :], scalar1=1.0, scalar2=None,
                    op0=ALU.add), sem=sv)
                P.wait("vector", sv, z2)
                z3 = P.op("vector", lambda e, bb=b: e.tensor_tensor(
                    out=zbuf[bb][0:32, :], in0=zf32[0:32, :], in1=sczb[bb][0:32, :],
                    op=ALU.mult), sem=sv)
                z_done[ci] = z3
                r_scz.consumed(ci, sv, z3)

                P.wait("vector", sv, bd)
                r_sc.ready("vector", ci)
                P.wait("gpsimd", sv, bd)
                r_sc.ready("gpsimd", ci)
                if ci >= 3:
                    P.wait("vector", sp, gu_fill[(ci - 3, 1, 1)])
                    P.wait("gpsimd", sp, gu_fill[(ci - 3, 1, 1)])
                mv = mg = 0
                for j in range(32):
                    w, nib = j // 8, j % 8
                    plane = (highb if (nib % 2) else lowb)[b]
                    byte_off = 1024 * w + (nib // 2)
                    eng = "vector" if (j % 2 == 0) else "gpsimd"
                    msem = sv if eng == "vector" else sg
                    val = P.op(eng, lambda e, pl=plane, bo=byte_off, ww=w, bb=b, w3=ci % 3, jj=j: e.tensor_tensor(
                        out=w1t[w3][:, 256 * jj:256 * (jj + 1)],
                        in0=pl[:].bitcast(U8)[:, bo:(bo - bo % 1024) + 1024:4],
                        in1=scb[bb][:, 256 * ww:256 * (ww + 1)],
                        op=ALU.mult), sem=msem)
                    if eng == "vector":
                        mv = val
                    else:
                        mg = val
                mult_done[ci] = (mv, mg)
                r_sc.consumed_multi(ci, [(sv, mv), (sg, mg)])

            def emit_sh_tail(cj):
                wpj, nj, kkj = DN_CHUNKS[cj]
                bj = cj % 2
                P.wait("tensor", sv, h_done[(cj, 1)])
                for hf2 in range(2):
                    sh_mm_done[cj] = P.op("tensor", lambda e, h=hf2, bb=bj, k2=kkj, w2=wpj,
                                          st=(cj == 0), sp_=(cj == 87):
                                          e.matmul(out=psh[0:86, 512 * h:512 * (h + 1)],
                                                   lhsT=c_enegd[0:k2, 86 * w2:86 * (w2 + 1)],
                                                   rhs=hchunk[bb][0:k2, 512 * h:512 * (h + 1)],
                                                   start=st, stop=sp_), sem=sp)
                P.wait("scalar", sv, h_done[(cj, 1)])
                P.wait("scalar", sp, sh_mm_done[cj])
                rbj = DN_ROWBASE[cj]
                r_hsp.load("scalar", cj, [(HTS[rbj:rbj + kkj, :], hchunk[bj][0:kkj, :])])

            def stageB(ci):
                wp, n, kk = DN_CHUNKS[ci]
                b = ci % 2
                if ci == 0:
                    P.guard("tensor")
                P.wait("tensor", sv, mult_done[ci][0])
                P.wait("tensor", sg, mult_done[ci][1])
                P.wait("tensor", sv, z_done[ci])
                for hf in range(2):
                    if ci >= 1:
                        P.wait("tensor", sx, hg_done[(ci - 1, hf)])
                        P.wait("tensor", sv, h_done[(ci - 1, hf)])
                    for gu in range(2):
                        tgt = (pg if gu == 0 else pu)[hf]
                        for j in range(32):
                            P.op("tensor", lambda e, jj=j, g=gu, h=hf, w3=ci % 3, tg=tgt, k2=kk:
                                 e.matmul(out=tg[0:k2, :],
                                          lhsT=w1t[w3][:, 256 * jj + 128 * g:256 * jj + 128 * g + k2],
                                          rhs=xnt[:, 1024 * jj + 512 * h:1024 * jj + 512 * (h + 1)],
                                          start=(jj == 0), stop=False), sem=None)
                        gu_fill[(ci, hf, gu)] = P.op("tensor", lambda e, g=gu, h=hf, bb=b, tg=tgt, k2=kk:
                                                     e.matmul(out=tg[0:k2, :],
                                                              lhsT=zbuf[bb][0:32, 128 * g:128 * g + k2],
                                                              rhs=sgneg[:, 512 * h:512 * (h + 1)],
                                                              start=False, stop=True), sem=sp)
                    P.wait("scalar", sp, gu_fill[(ci, hf, 0)])
                    hg = P.op("scalar", lambda e, h=hf, k2=kk: e.activation(
                        out=hgate[h][0:k2, :], in_=pg[h][0:k2, :], func=AF.Silu), sem=sx)
                    hg_done[(ci, hf)] = hg
                    P.wait("vector", sp, gu_fill[(ci, hf, 1)])
                    P.wait("vector", sx, hg)
                    if hf == 0:
                        r_hsp.guard_write("vector", ci)
                    if ci >= 2:
                        P.wait("vector", sp, sh_mm_done.get(ci - 2, 0))
                    hd = P.op("vector", lambda e, h=hf, bb=b, k2=kk: e.tensor_tensor(
                        out=hchunk[bb][0:k2, 512 * h:512 * (h + 1)],
                        in0=pu[h][0:k2, :], in1=hgate[h][0:k2, :], op=ALU.mult), sem=sv)
                    h_done[(ci, hf)] = hd
                if ci >= 1:
                    emit_sh_tail(ci - 1)

            stageA(0)
            for ci in range(1, 88):
                stageA(ci)
                stageB(ci - 1)
            stageB(87)
            emit_sh_tail(87)
            P.wait("vector", sp, sh_mm_done[87])
            sh_evac = P.op("vector", lambda e: e.tensor_copy(
                out=shneg[0:86, :], in_=psh[0:86, :]), sem=sv)

            if DEBUG:
                for s_, v_ in r_hsp.totals():
                    P.wait("sync", s_, v_)
                P.wait("sync", sv, sh_evac)
                P.dma("sync", dbg["HTS"], HTS, qdbg)
                P.dma("sync", dbg["SH"], shneg[:], qdbg)
            s3.close()
            for s_, v_ in r_hsp.totals():
                P.wait("sync", s_, v_)
            P.wait("sync", qdbg, qdbg.n)
            P.set_pguard((sx, hg_done[(87, 1)]), (sv, max(h_done[(87, 1)], sh_evac)))
            P.barrier()

        scopeA.close()

        if PHASES >= 4:
            s4 = ExitStack()
            sb4 = lambda name, shape, dt: s4.enter_context(nc.sbuf_tensor(name, shape, dt))
            ps4 = lambda name, shape: s4.enter_context(nc.psum_tensor(name, shape, F32))
            zd = sb4("zd", [86, D], BF16)
            zdq = sb4("zdq", [86, 512], U32)
            zdnu = sb4("zdnu", [86, 512], U32)
            zdn = sb4("zdn", [86, 512], F32)
            scd_full = sb4("scd_full", [86, D], BF16)
            qwd = [sb4(f"qwd{i}", [128, 512], U32) for i in range(2)]
            lowd = [sb4(f"lowd{i}", [128, 512], U32) for i in range(2)]
            highd = [sb4(f"highd{i}", [128, 512], U32) for i in range(2)]
            scdx = [sb4(f"scdx{i}", [128, 512], BF16) for i in range(2)]
            w1d = [sb4(f"w1d{i}", [128, 512], BF16) for i in range(3)]
            htb = [sb4(f"htb{i}", [128, 1024], BF16) for i in range(3)]
            gst = [sb4(f"gst{i}", [128, 512], BF16) for i in range(2)]
            pd = [ps4(f"pd{i}", [128, 512]) for i in range(8)]

            r_zd = Ring(P, "r_zd", 1)
            r_scd = Ring(P, "r_scd", 1)
            r_qwd = Ring(P, "r_qwd", 2)
            r_sdx = Ring(P, "r_sdx", 2)
            r_ht = Ring(P, "r_ht", 3)
            r_go = Ring(P, "r_go", 2)

            wait_pc("sync", "scdnb")
            r_zd.load("sync", 0, [(zdq[:], QZDN)])
            r_scd.load("sync", 0, [(scd_full[:], SCDNB)])
            r_zd.ready("vector", 0)
            r_scd.ready("vector", 0)
            zprev = 0
            for n2 in range(8):
                P.wait("vector", sv, zprev)
                a1 = P.op("vector", lambda e, nn=n2: e.tensor_scalar(
                    out=zdnu[:], in0=zdq[:], scalar1=4 * nn, scalar2=15,
                    op0=ALU.logical_shift_right, op1=ALU.bitwise_and), sem=sv)
                P.wait("vector", sv, a1)
                a2 = P.op("vector", lambda e: e.tensor_scalar(
                    out=zdn[:], in0=zdnu[:], scalar1=1.0, scalar2=None, op0=ALU.add), sem=sv)
                P.wait("vector", sv, a2)
                zprev = P.op("vector", lambda e, nn=n2: e.tensor_tensor(
                    out=zd[:].rearrange("p (c e) -> p c e", e=8)[:, :, nn],
                    in0=zdn[:],
                    in1=scd_full[:].rearrange("p (c e) -> p c e", e=8)[:, :, nn],
                    op=ALU.mult), sem=sv)
            zd_ready = zprev

            dn_bit_done = {}
            dn_mult_used = {}
            g_evac = {}
            w1d_used = {}
            for hc in range(8):
                for ci, (wp, n, kk) in enumerate(DN_CHUNKS):
                    gi = hc * 88 + ci
                    b3 = gi % 3
                    wi = hc * 11 + wp
                    if n == 0:
                        r_qwd.load("sync", wi,
                                   [(qwd[wi % 2][0:kk, :],
                                     QWDN[128 * wp:128 * wp + kk, 512 * hc:512 * (hc + 1)])])
                        r_sdx.load("gpsimd", wi,
                                   [(scdx[wi % 2][:],
                                     SCDEXP[:, 4096 * wp + 512 * hc:4096 * wp + 512 * (hc + 1)])])
                        r_qwd.ready("vector", wi)
                        if wi >= 2:
                            P.wait("vector", sv, dn_mult_used[wi - 2][0])
                            P.wait("vector", sg, dn_mult_used[wi - 2][1])
                        P.op("vector", lambda e, w2=wi % 2, k2=kk: e.tensor_scalar(
                            out=lowd[w2][0:k2, :], in0=qwd[w2][0:k2, :], scalar1=0x0F0F0F0F,
                            scalar2=None, op0=ALU.bitwise_and), sem=sv)
                        bitd = P.op("vector", lambda e, w2=wi % 2, k2=kk: e.tensor_scalar(
                            out=highd[w2][0:k2, :], in0=qwd[w2][0:k2, :], scalar1=4,
                            scalar2=0x0F0F0F0F, op0=ALU.logical_shift_right,
                            op1=ALU.bitwise_and), sem=sv)
                        dn_bit_done[wi] = bitd
                        r_qwd.consumed(wi, sv, bitd)
                        dn_mult_mv = 0
                        dn_mult_mg = 0
                    rb = DN_ROWBASE[ci]
                    r_ht.load("scalar", gi, [(htb[b3][0:kk, :], HTS[rb:rb + kk, :])])
                    eng = "vector" if (ci % 2 == 0) else "gpsimd"
                    msem = sv if eng == "vector" else sg
                    P.wait(eng, sv, dn_bit_done[wi])
                    r_sdx.ready(eng, wi)
                    if gi >= 3:
                        P.wait(eng, sp, w1d_used[gi - 3])
                    plane = (highd if (n % 2) else lowd)[wi % 2]
                    mval = P.op(eng, lambda e, pl=plane, bo=(n // 2), b3_=b3, k2=kk, w2=wi % 2:
                                e.tensor_tensor(out=w1d[b3_][0:k2, :],
                                                in0=pl[0:k2, :].bitcast(U8)[:, bo:2048:4],
                                                in1=scdx[w2][0:k2, :], op=ALU.mult), sem=msem)
                    if eng == "vector":
                        dn_mult_mv = mval
                    else:
                        dn_mult_mg = mval
                    if n == 7:
                        dn_mult_used[wi] = (dn_mult_mv, dn_mult_mg)
                        r_sdx.consumed_multi(wi, [(sv, dn_mult_mv), (sg, dn_mult_mg)])
                    if gi == 0:
                        P.guard("tensor")
                    P.wait("tensor", msem, mval)
                    r_ht.ready("tensor", gi)
                    if ci == 0 and hc > 0:
                        P.wait("tensor", sx, g_evac[(hc - 1, 7)])
                    lastmm = None
                    for t in range(8):
                        lastmm = P.op("tensor", lambda e, tt=t, b3_=b3, k2=kk, st=(ci == 0):
                                      e.matmul(out=pd[tt][:],
                                               lhsT=htb[b3_][0:k2, 128 * tt:128 * (tt + 1)],
                                               rhs=w1d[b3_][0:k2, :],
                                               start=st, stop=False),
                                      sem=(sp if t == 7 else None))
                    r_ht.consumed(gi, sp, lastmm)
                    w1d_used[gi] = lastmm
                P.wait("tensor", sv, zd_ready)
                zf = None
                for t in range(8):
                    zf = P.op("tensor", lambda e, tt=t, h2=hc: e.matmul(
                        out=pd[tt][:], lhsT=shneg[:, 128 * tt:128 * (tt + 1)],
                        rhs=zd[:, 512 * h2:512 * (h2 + 1)], start=False, stop=True),
                        sem=(sp if t == 7 else None))
                for t in range(8):
                    gi2 = hc * 8 + t
                    P.wait("scalar", sp, zf)
                    r_go.guard_write("scalar", gi2)
                    ev = P.op("scalar", lambda e, tt=t: e.activation(
                        out=gst[tt % 2][:], in_=pd[tt][:], func=AF.Silu), sem=sx)
                    g_evac[(hc, t)] = ev
                    P.wait("scalar", sx, ev)
                    r_go.load("scalar", gi2,
                              [(GSP[128 * t:128 * (t + 1), 512 * hc:512 * (hc + 1)], gst[t % 2][:])])
            if DEBUG:
                for s_, v_ in r_go.totals():
                    P.wait("sync", s_, v_)
                P.dma("sync", dbg["GSP"], GSP, qdbg)
            s4.close()
            for s_, v_ in r_go.totals():
                P.wait("sync", s_, v_)
            P.set_pguard((sx, g_evac[(7, 7)]))
            P.barrier()

        if PHASES >= 5:
            s5 = ExitStack()
            sb5 = lambda name, shape, dt: s5.enter_context(nc.sbuf_tensor(name, shape, dt))
            s5p = ExitStack()
            ps5 = lambda name, shape, dt=F32: s5p.enter_context(nc.psum_tensor(name, shape, dt))
            slab5 = sb5("slab5", [128, 32 * 1024], BF16)
            ktb = [sb5(f"ktb{i}", [128, 512], BF16) for i in range(6)]
            vtb = [sb5(f"vtb{i}", [128, 512], BF16) for i in range(6)]
            pb = [sb5(f"pb{i}", [128, 512], BF16) for i in range(8)]
            linv = [sb5(f"linv{i}", [1, 512], F32) for i in range(2)]
            linvrep = [sb5(f"linvrep{i}", [128, 512], F32) for i in range(2)]
            ps_s = [ps5(f"ps_s{i}", [128, 512]) for i in range(3)]
            ps_o = [ps5(f"ps_o{i}", [128, 512]) for i in range(2)]
            ps_l = [ps5(f"ps_l{i}", [1, 512]) for i in range(2)]
            ps_b = ps5("ps_b", [128, 512])

            r_ql = Ring(P, "r_ql", 1)
            r_kt = Ring(P, "r_kt", 6)
            r_vt = Ring(P, "r_vt", 6)
            r_ql.load("sync", 0, [(slab5[:].rearrange("p (j c) -> p j c", c=1024),
                                   QSP.rearrange("(j p) c -> p j c", p=128))])
            P.wait("sync", cc, 2)
            P.wait("scalar", cc, 2)
            exp_done = {}
            pb_used = {}
            norm_done = {}
            for u in range(2):
                for h in range(H):
                    uh = u * H + h
                    qo = 1024 * h + 512 * u
                    def emit_outsum(ib2):
                        r2, i2 = divmod(ib2, 4)
                        gi2_ = uh * 9 + r2
                        ib_g = uh * 36 + ib2
                        if r2 == 8:
                            use_sem, use_val = sv, mask_done[ib_g]
                        else:
                            use_sem, use_val = sx, exp_done[ib_g]
                        P.wait("tensor", use_sem, use_val)
                        if i2 == 0:
                            r_vt.ready("tensor", gi2_)
                        first = (ib2 == 0)
                        last = (ib2 == 35)
                        if first and uh > 1:
                            P.wait("tensor", sv, norm_done[uh - 2])
                        ub2 = uh % 2
                        P.op("tensor", lambda e, g3=gi2_ % 6, ii=i2, ibb=ib_g, f=first, l=last, u2=ub2:
                             e.matmul(out=ps_o[u2][:], lhsT=vtb[g3][:, 128 * ii:128 * (ii + 1)],
                                      rhs=pb[ibb % 8][:], start=f, stop=l), sem=None)
                        omm = P.op("tensor", lambda e, ibb=ib_g, f=first, l=last, u2=ub2:
                                   e.matmul(out=ps_l[u2][:], lhsT=c_ones[:],
                                            rhs=pb[ibb % 8][:], start=f, stop=l), sem=sp)
                        pb_used[ib_g] = omm
                        if i2 == 3:
                            r_vt.consumed(gi2_, sp, omm)

                    mask_done = {}
                    for r in range(9):
                        gi = uh * 9 + r
                        if r < 8:
                            ksrc = agK_out[4096 * r + 128 * h:4096 * r + 128 * (h + 1),
                                           512 * u:512 * (u + 1)]
                            vsrc = agV_out[1024 * r + 512 * u:1024 * r + 512 * (u + 1),
                                           128 * h:128 * (h + 1)].rearrange("(i p) d -> p i d", p=128)
                        else:
                            ksrc = agK_in[128 * h:128 * (h + 1), 512 * u:512 * (u + 1)]
                            vsrc = agV_in[512 * u:512 * (u + 1),
                                          128 * h:128 * (h + 1)].rearrange("(i p) d -> p i d", p=128)
                        r_kt.load("sync", gi, [(ktb[gi % 6][:], ksrc)])
                        r_vt.load("scalar", gi,
                                  [(vtb[gi % 6][:].rearrange("p (i d) -> p i d", d=128), vsrc)])
                        if gi == 0:
                            P.guard("tensor")
                            r_ql.ready("tensor", 0)
                        r_kt.ready("tensor", gi)
                        for i in range(4):
                            kt = 4 * r + i
                            lb = r * 4 + i
                            ib = uh * 36 + lb
                            sb_ = ib % 3
                            if ib >= 3:
                                P.wait("tensor", sx, exp_done[ib - 3])
                            smm = P.op("tensor", lambda e, g3=gi % 6, ii=i, s2=sb_, q1=qo:
                                       e.matmul(out=ps_s[s2][:],
                                                lhsT=ktb[g3][:, 128 * ii:128 * (ii + 1)],
                                                rhs=slab5[:, q1:q1 + 512],
                                                start=True, stop=True), sem=sp)
                            if i == 3:
                                r_kt.consumed(gi, sp, smm)
                            P.wait("scalar", sp, smm)
                            if ib >= 8:
                                P.wait("scalar", sp, pb_used[ib - 8])
                            bias_col = kt if r < 8 else 32
                            ed = P.op("scalar", lambda e, s2=sb_, ibb=ib, bc=bias_col:
                                      e.activation(out=pb[ibb % 8][:], in_=ps_s[s2][:],
                                                   func=AF.Exp, bias=c_bias[:, bc:bc + 1],
                                                   scale=1.0), sem=sx)
                            exp_done[ib] = ed
                            if r == 8:
                                P.wait("vector", sx, ed)
                                md = P.op("vector", lambda e, ibb=ib, ii=i: e.tensor_tensor(
                                    out=pb[ibb % 8][:], in0=pb[ibb % 8][:],
                                    in1=c_masks[:, 512 * ii:512 * (ii + 1)], op=ALU.mult), sem=sv)
                                mask_done[ib] = md
                            if lb >= 2:
                                emit_outsum(lb - 2)
                    emit_outsum(34)
                    emit_outsum(35)
                    ub = uh % 2
                    P.wait("vector", sp, pb_used[uh * 36 + 35])
                    r1 = P.op("vector", lambda e, u2=ub: e.reciprocal(
                        out=linv[u2][:], in_=ps_l[u2][:]), sem=sv)
                    P.wait("tensor", sv, r1)
                    if uh >= 1:
                        P.wait("tensor", sv, linvrep_done[uh - 1])
                    bc2 = P.op("tensor", lambda e, u2=ub: e.matmul(
                        out=ps_b[:], lhsT=c_onesr[:], rhs=linv[u2][:], start=True, stop=True), sem=sp)
                    P.wait("vector", sp, bc2)
                    r2 = P.op("vector", lambda e, u2=ub: e.tensor_copy(
                        out=linvrep[u2][:], in_=ps_b[:]), sem=sv)
                    if uh == 0:
                        linvrep_done = {}
                    linvrep_done[uh] = r2
                    P.wait("vector", sv, r2)
                    nd = P.op("vector", lambda e, q1=qo, u2=ub: e.tensor_tensor(
                        out=slab5[:, q1:q1 + 512], in0=ps_o[u2][:], in1=linvrep[u2][:],
                        op=ALU.mult), sem=sv)
                    norm_done[uh] = nd

            if DEBUG:
                P.wait("sync", sv, norm_done[63])
                P.dma("sync", dbg["SPRE"].rearrange("(j p) c -> p j c", p=128),
                      slab5[:].rearrange("p (j c) -> p j c", c=1024), qdbg)
            s5p.close()
            P.set_pguard((sv, norm_done[63]))
            P.barrier()

            if PHASES >= 6:
                s6 = ExitStack()
                sb6 = lambda name, shape, dt: s6.enter_context(nc.sbuf_tensor(name, shape, dt))
                ps6 = lambda name, shape: s6.enter_context(nc.psum_tensor(name, shape, F32))
                wot = [sb6(f"wot{i}", [128, 512], BF16) for i in range(4)]
                wot32 = [sb6(f"wot32{i}", [128, 512], F32) for i in range(3)]
                xb6 = [sb6(f"xb6{i}", [128, 512], F32) for i in range(2)]
                gb6 = [sb6(f"gb6{i}", [128, 512], BF16) for i in range(2)]
                sgb = [sb6(f"sgb{i}", [128, 512], F32) for i in range(2)]
                ob6 = [sb6(f"ob6{i}", [128, 512], F32) for i in range(2)]
                pc6 = [ps6(f"pc6{i}", [128, 512]) for i in range(8)]

                r_wo = Ring(P, "r_wo", 3)
                wo_cast = {}
                r_xb = Ring(P, "r_xb", 2)
                r_gb = Ring(P, "r_gb", 2)
                r_out = Ring(P, "r_out", 2)
                comb = {}
                for fc in range(8):
                    for dc in range(32):
                        gi = fc * 32 + dc
                        r_wo.load("scalar", gi,
                                  [(wot32[gi % 3][:], WO[128 * dc:128 * (dc + 1), 512 * fc:512 * (fc + 1)])])
                        r_wo.ready("scalar", gi)
                        if gi >= 4:
                            P.wait("scalar", sp, wo_used_d[gi - 4])
                        wc = P.op("scalar", lambda e, b4=gi % 4, b3_=gi % 3:
                                  e.copy(out=wot[b4][:], in_=wot32[b3_][:]), sem=sx)
                        wo_cast[gi] = wc
                        r_wo.consumed(gi, sx, wc)
                        P.wait("tensor", sx, wc)
                        if gi == 0:
                            P.guard("tensor")
                            wo_used_d = {}
                        if dc == 0 and fc > 0:
                            P.wait("tensor", sv, comb[(fc - 1, 7)])
                        wdone = None
                        for t in range(8):
                            wdone = P.op("tensor", lambda e, tt=t, b=gi % 4, d2=dc:
                                         e.matmul(out=pc6[tt][:],
                                                  lhsT=slab5[:, 1024 * d2 + 128 * tt:1024 * d2 + 128 * (tt + 1)],
                                                  rhs=wot[b][:],
                                                  start=(d2 == 0), stop=(d2 == 31)),
                                         sem=(sp if t == 7 else None))
                        wo_used_d[gi] = wdone
                    for t in range(8):
                        gi2 = fc * 8 + t
                        r_xb.load("sync", gi2,
                                  [(xb6[t % 2][:], XNAT[128 * t:128 * (t + 1), 512 * fc:512 * (fc + 1)])])
                        r_gb.load("scalar", gi2,
                                  [(gb6[t % 2][:], GSP[128 * t:128 * (t + 1), 512 * fc:512 * (fc + 1)])])
                        P.wait("vector", sp, wdone)
                        r_gb.ready("vector", gi2)
                        r_xb.ready("vector", gi2)
                        r_out.guard_write("vector", gi2)
                        c1 = P.op("vector", lambda e, tt=t: e.tensor_tensor(
                            out=sgb[tt % 2][:], in0=pc6[tt][:], in1=gb6[tt % 2][:],
                            op=ALU.mult), sem=sv)
                        P.wait("vector", sv, c1)
                        c2 = P.op("vector", lambda e, tt=t: e.tensor_tensor(
                            out=ob6[tt % 2][:], in0=sgb[tt % 2][:], in1=xb6[tt % 2][:],
                            op=ALU.add), sem=sv)
                        comb[(fc, t)] = c2
                        r_xb.consumed(gi2, sv, c2)
                        r_gb.consumed(gi2, sv, c2)
                        P.wait("sync", sv, c2)
                        r_out.load("sync", gi2,
                                   [(OUT[128 * t:128 * (t + 1), 512 * fc:512 * (fc + 1)], ob6[t % 2][:])])
                for s_, v_ in r_out.totals():
                    P.wait("sync", s_, v_)
                s6.close()
            s5.close()

        P.wait("sync", qdbg, qdbg.n)

        block = es.enter_context(nc.Block())
        P.finalize(block)

    counts = {e: len(P.streams[e]) for e in ENGINES}
    return nc, counts



def prep_inputs(L0_attn_x, L0_attn_wq, L0_attn_wk, L0_attn_wv, L0_attn_wo,
                L0_ffn_qweight_gu, L0_ffn_qzeros_gu, L0_ffn_scales_gu, L0_ffn_g_idx_gu,
                L0_ffn_qweight_down, L0_ffn_qzeros_down, L0_ffn_scales_down, L0_ffn_g_idx_down):
    f32 = np.float32
    x = np.asarray(L0_attn_x, f32)
    wq = np.asarray(L0_attn_wq, f32)
    wk = np.asarray(L0_attn_wk, f32)
    wv = np.asarray(L0_attn_wv, f32)
    wo = np.ascontiguousarray(np.asarray(L0_attn_wo, f32))
    qwgu = np.asarray(L0_ffn_qweight_gu, np.uint32)
    qzgu = np.ascontiguousarray(np.asarray(L0_ffn_qzeros_gu, np.uint32))
    scgu = np.asarray(L0_ffn_scales_gu, f32)
    qwdn = np.ascontiguousarray(np.asarray(L0_ffn_qweight_down, np.uint32))
    qzdn = np.ascontiguousarray(np.asarray(L0_ffn_qzeros_down, np.uint32))
    scdn = np.ascontiguousarray(np.asarray(L0_ffn_scales_down, f32))

    wq_p = np.ascontiguousarray(wq[PERM_H, :])
    wk_p = np.ascontiguousarray(wk[PERM_H, :])
    wv_p = np.ascontiguousarray(wv[PERM_H, :])
    qwgu_p = np.ascontiguousarray(
        np.concatenate([qwgu[:, :INTER][:, PERM_D], qwgu[:, INTER:][:, PERM_D]], axis=1))
    scgu_p = np.ascontiguousarray(
        np.concatenate([scgu[:, :INTER][:, PERM_D], scgu[:, INTER:][:, PERM_D]], axis=1))

    gidx = np.arange(128) // 16
    scexp = np.zeros((128, 88, 4, 2, 128), f32)
    rows_by_w = [scgu_p[8 * w + gidx] for w in range(4)]
    for ci, (wp, n, kk) in enumerate(DN_CHUNKS):
        cb = 1024 * wp + kk * n
        for w in range(4):
            scexp[:, ci, w, 0, :kk] = rows_by_w[w][:, cb:cb + kk]
            scexp[:, ci, w, 1, :kk] = rows_by_w[w][:, INTER + cb:INTER + cb + kk]
    scexp = np.ascontiguousarray(scexp.reshape(128, -1))
    scdexp = np.empty((128, 11, D), f32)
    for wp in range(11):
        ridx = np.minimum(8 * wp + gidx, 85)
        scdexp[:, wp, :] = scdn[ridx]
    scdexp = np.ascontiguousarray(scdexp.reshape(128, -1))

    permm = np.zeros((128, 128), f32)
    for i in range(0, 128, 2):
        permm[i, i + 1] = 1.0
        permm[i + 1, i] = 1.0
    eneg = np.zeros((128, 4, 32), f32)
    for w in range(4):
        eneg[np.arange(128), w, 8 * w + np.arange(128) // 16] = -1.0
    eneg = eneg.reshape(128, 128)
    enegd = np.zeros((128, 11, 86), f32)
    for wp in range(11):
        kk = 128 if wp < 10 else 96
        enegd[np.arange(kk), wp, 8 * wp + np.arange(kk) // 16] = -1.0
    enegd = enegd.reshape(128, 11 * 86)
    onesc = np.ones((128, 1), f32)
    onesr = np.ones((1, 128), f32)
    masks = np.zeros((128, 4, SEG), f32)
    for dlt in range(4):
        pp = np.arange(128)[:, None]
        jj = np.arange(SEG)[None, :]
        masks[:, dlt, :] = (pp <= jj - 128 * dlt).astype(f32)
    masks = masks.reshape(128, 4 * SEG)

    inv = 1.0 / (10000.0 ** (np.arange(0, HD, 2)[:64].astype(f32) / HD))

    per_core = []
    for c in range(NCORES):
        pos = np.concatenate([512 * c + np.arange(SEG), 512 * c + np.arange(SEG)])
        f = np.outer(pos, inv).astype(f32)
        cosf = np.cos(f)
        sinf = np.sin(f)
        ropec = np.empty((128, T), f32)
        ropes = np.empty((128, T), f32)
        ii = np.arange(64)
        ropec[2 * ii, :] = cosf.T
        ropec[2 * ii + 1, :] = cosf.T
        ropes[2 * ii, :] = -sinf.T
        ropes[2 * ii + 1, :] = sinf.T
        biast = np.zeros((128, 33), f32)
        for kt in range(32):
            if kt >= 4 * c:
                biast[:, kt] = NEG
        xa = x[0, 512 * c:512 * (c + 1)]
        xb = x[1, 512 * c:512 * (c + 1)]
        xnat = np.ascontiguousarray(np.concatenate([xa, xb], axis=0))
        xperm = np.ascontiguousarray(xnat[:, PERM_H])
        per_core.append(dict(XP=xperm, XNAT=xnat, ROPEC=ropec, ROPES=ropes, BIAST=biast))

    shared = dict(
        WQ=wq_p, WK=wk_p, WV=wv_p, WO=wo, QWGU=qwgu_p, SCGU=scgu_p, QZGU=qzgu,
        QWDN=qwdn, SCDN=scdn, QZDN=qzdn, SCEXP=scexp, SCDEXP=scdexp,
        PERMM=permm, IDTI=np.eye(128, dtype=f32), ENEG=eneg, ENEGD=enegd,
        ONESC=onesc, ONESR=onesr, MASKS=masks,
    )
    in_maps = []
    for c in range(NCORES):
        m = dict(shared)
        m.update(per_core[c])
        in_maps.append(m)
    return in_maps


def kernel(**inputs):
    in_maps = prep_inputs(**inputs)
    nc, counts = build_program()
    res = run_bass_kernel_spmd(nc, in_maps, core_ids=list(range(NCORES)))
    out = np.empty((2, 4096, D), np.float32)
    for c in range(NCORES):
        o = np.asarray(res.results[c]["OUT"])
        out[0, 512 * c:512 * (c + 1)] = o[:SEG]
        out[1, 512 * c:512 * (c + 1)] = o[SEG:]
    return out


if __name__ == "__main__":
    import sys
    if sys.argv[1:] and sys.argv[1] == "build":
        nc, counts = build_program()
        print("build ok", counts, "total", sum(counts.values()))


# version 12
# speedup vs baseline: 1.0917x; 1.0423x over previous
import os
import numpy as np
import ml_dtypes
from contextlib import ExitStack
import concourse.bass as bass
import concourse.mybir as mybir
from concourse.bass_utils import run_bass_kernel_spmd

F32 = mybir.dt.float32
BF16 = mybir.dt.bfloat16
U32 = mybir.dt.uint32
U8 = mybir.dt.uint8
AF = mybir.ActivationFunctionType
ALU = mybir.AluOpType
AX = mybir.AxisListType
bf16 = ml_dtypes.bfloat16

NCORES = 8
D = 4096
T = 1024
INTER = 11008
H = 32
HD = 128
SEG = 512
EPS = 1e-5
ATT_SCALE = 1.0 / np.sqrt(128.0)
NEG = -1e9

PHASES = int(os.environ.get("WK_PHASES", "6"))
DEBUG = os.environ.get("WK_DEBUG", "0") == "1"
AGSHARED = os.environ.get("WK_AGSHARED", "1") == "1"

ENGINES = ("sync", "scalar", "vector", "tensor", "gpsimd")


def mkperm(F, win=1024):
    perm = np.empty(F, dtype=np.int64)
    for base in range(0, F, win):
        m = min(win, F - base)
        kk = m // 8
        j = np.arange(m)
        perm[base + j] = base + 8 * (j % kk) + (j // kk)
    return perm


PERM_H = mkperm(D)
PERM_D = mkperm(INTER)
DN_CHUNKS = [(wp, n, 128 if wp < 10 else 96) for wp in range(11) for n in range(8)]
DN_ROWBASE = {}
_rb = 0
for _i, (_wp, _n, _kk) in enumerate(DN_CHUNKS):
    DN_ROWBASE[_i] = _rb
    _rb += _kk
assert _rb == INTER


class Sem:
    def __init__(self, nc, es, name):
        self.h = es.enter_context(nc.semaphore(name))
        self.n = 0


class Prog:
    def __init__(self, nc, es):
        self.nc = nc
        self.streams = {e: [] for e in ENGINES}
        self.es = es
        self._sems = {}
        self.pguard = []

    def sem(self, name):
        if name not in self._sems:
            self._sems[name] = Sem(self.nc, self.es, name)
        return self._sems[name]

    def emit(self, eng, fn):
        self.streams[eng].append(fn)

    def wait(self, eng, sem, val):
        if val > 0:
            self.emit(eng, lambda e, s=sem.h, v=val: e.wait_ge(s, v))

    def dma(self, eng, out, in_, sem):
        sem.n += 16
        self.emit(eng, lambda e, o=out, i=in_, s=sem.h: e.dma_start(out=o, in_=i).then_inc(s, 16))
        return sem.n

    def op(self, eng, fn, sem=None, k=1):
        if sem is None:
            self.emit(eng, fn)
            return None
        sem.n += k
        self.emit(eng, lambda e, f=fn, s=sem.h, kk=k: f(e).then_inc(s, kk))
        return sem.n

    def set_pguard(self, *entries):
        self.pguard = [(s, v) for (s, v) in entries if v]

    def guard(self, eng):
        for s, v in self.pguard:
            self.wait(eng, s, v)

    def barrier(self):
        s = self.sem("bar")
        own = {"vector": "sv", "scalar": "sx", "tensor": "sp", "gpsimd": "sg"}
        for e in ENGINES:
            if e in own and own[e] in self._sems:
                osem = self._sems[own[e]]
                self.wait(e, osem, osem.n)
            s.n += 1
            self.emit(e, lambda en, h=s.h: en.nop().then_inc(h, 1))
        tgt = s.n
        for e in ENGINES:
            self.wait(e, s, tgt)

    def finalize(self, block):
        decs = {"sync": block.sync, "scalar": block.scalar, "vector": block.vector,
                "tensor": block.tensor, "gpsimd": block.gpsimd}
        for name, dec in decs.items():
            ops = list(self.streams[name])
            def make(ops_):
                def _(eng):
                    for f in ops_:
                        f(eng)
                return _
            dec(make(ops))


class Ring:

    def __init__(self, P, name, n):
        self.P = P
        self.n = n
        self.sems = [P.sem(f"{name}_{i}") for i in range(n)]
        self.cons = [None] * n
        self.val = [0] * n

    def load(self, eng, i, pairs):
        s = i % self.n
        if self.cons[s] is not None:
            for c_ in self.cons[s]:
                self.P.wait(eng, *c_)
            self.cons[s] = None
        for dst, src in pairs:
            self.P.dma(eng, dst, src, self.sems[s])
        self.val[s] = self.sems[s].n
        return (self.sems[s], self.val[s])

    def ready(self, eng, i):
        s = i % self.n
        self.P.wait(eng, self.sems[s], self.val[s])

    def consumed(self, i, sem, val):
        self.cons[i % self.n] = [(sem, val)]

    def consumed_multi(self, i, entries):
        self.cons[i % self.n] = [e_ for e_ in entries if e_[1]]

    def guard_write(self, eng, i):
        s = i % self.n
        if self.val[s]:
            self.P.wait(eng, self.sems[s], self.val[s])

    def totals(self):
        return [(s, s.n) for s in self.sems]


def build_program():
    nc = bass.Bass("TRN2", target_bir_lowering=False)
    ap = lambda t: t.ap()

    XP = ap(nc.dram_tensor("XP", [T, D], F32, kind="ExternalInput"))
    XNAT = ap(nc.dram_tensor("XNAT", [T, D], F32, kind="ExternalInput"))
    WQ = ap(nc.dram_tensor("WQ", [D, D], F32, kind="ExternalInput"))
    WK = ap(nc.dram_tensor("WK", [D, D], F32, kind="ExternalInput"))
    WV = ap(nc.dram_tensor("WV", [D, D], F32, kind="ExternalInput"))
    WO = ap(nc.dram_tensor("WO", [D, D], F32, kind="ExternalInput"))
    QWGU = ap(nc.dram_tensor("QWGU", [512, 2 * INTER], U32, kind="ExternalInput"))
    SCGU = ap(nc.dram_tensor("SCGU", [32, 2 * INTER], F32, kind="ExternalInput"))
    QZGU = ap(nc.dram_tensor("QZGU", [32, 2752], U32, kind="ExternalInput"))
    QWDN = ap(nc.dram_tensor("QWDN", [1376, D], U32, kind="ExternalInput"))
    SCDN = ap(nc.dram_tensor("SCDN", [86, D], F32, kind="ExternalInput"))
    QZDN = ap(nc.dram_tensor("QZDN", [86, 512], U32, kind="ExternalInput"))
    SCEXP = ap(nc.dram_tensor("SCEXP", [128, 88 * 1024], F32, kind="ExternalInput"))
    SCDEXP = ap(nc.dram_tensor("SCDEXP", [128, 11 * D], F32, kind="ExternalInput"))
    ROPEC = ap(nc.dram_tensor("ROPEC", [128, T], F32, kind="ExternalInput"))
    ROPES = ap(nc.dram_tensor("ROPES", [128, T], F32, kind="ExternalInput"))
    PERMM = ap(nc.dram_tensor("PERMM", [128, 128], F32, kind="ExternalInput"))
    IDTI = ap(nc.dram_tensor("IDTI", [128, 128], F32, kind="ExternalInput"))
    ENEG = ap(nc.dram_tensor("ENEG", [128, 128], F32, kind="ExternalInput"))
    ENEGD = ap(nc.dram_tensor("ENEGD", [128, 11 * 86], F32, kind="ExternalInput"))
    ONESC = ap(nc.dram_tensor("ONESC", [128, 1], F32, kind="ExternalInput"))
    ONESR = ap(nc.dram_tensor("ONESR", [1, 128], F32, kind="ExternalInput"))
    MASKS = ap(nc.dram_tensor("MASKS", [128, 4 * SEG], F32, kind="ExternalInput"))
    BIAST = ap(nc.dram_tensor("BIAST", [128, 33], F32, kind="ExternalInput"))

    OUT = ap(nc.dram_tensor("OUT", [T, D], F32, kind="ExternalOutput"))

    agK_in = ap(nc.dram_tensor("agK_in", [D, T], BF16))
    agV_in = ap(nc.dram_tensor("agV_in", [T, D], BF16))
    agK_out = ap(nc.dram_tensor("agK_out", [NCORES * D, T], BF16, addr_space="Shared" if AGSHARED else "Local"))
    agV_out = ap(nc.dram_tensor("agV_out", [NCORES * T, D], BF16, addr_space="Shared" if AGSHARED else "Local"))
    QSP = ap(nc.dram_tensor("QSP", [D, T], BF16))
    SCGUB = ap(nc.dram_tensor("SCGUB", [32, 2 * INTER], BF16))
    SCDNB = ap(nc.dram_tensor("SCDNB", [86, D], BF16))
    HTS = ap(nc.dram_tensor("HTS", [INTER, T], BF16))
    GSP = ap(nc.dram_tensor("GSP", [T, D], BF16))

    dbg = {}
    if DEBUG:
        def dbgt(name, shape, dt):
            dbg[name] = ap(nc.dram_tensor("DBG_" + name, shape, dt, kind="ExternalOutput"))
        dbgt("XNT", [D, T], BF16)
        dbgt("SG", [32, T], BF16)
        if PHASES >= 2:
            dbgt("QSP", [D, T], BF16)
            dbgt("KIN", [D, T], BF16)
            dbgt("VIN", [T, D], BF16)
        if PHASES >= 3:
            dbgt("HTS", [INTER, T], BF16)
            dbgt("SH", [86, T], BF16)
        if PHASES >= 4:
            dbgt("GSP", [T, D], BF16)
        if PHASES >= 5:
            dbgt("SPRE", [D, T], BF16)

    es = ExitStack()
    with es:
        P = Prog(nc, es)
        sb = lambda name, shape, dt: es.enter_context(nc.sbuf_tensor(name, shape, dt))

        c_idt = sb("c_idt", [128, 128], BF16)
        c_perm = sb("c_perm", [128, 128], BF16)
        c_eneg = sb("c_eneg", [128, 128], BF16)
        c_enegd = sb("c_enegd", [128, 11 * 86], BF16)
        c_ones = sb("c_ones", [128, 1], BF16)
        c_onesr = sb("c_onesr", [1, 128], F32)
        c_masks = sb("c_masks", [128, 4 * SEG], BF16)
        c_bias = sb("c_bias", [128, 33], F32)
        c_ropec = sb("c_ropec", [128, T], BF16)
        c_ropes = sb("c_ropes", [128, T], BF16)
        sgneg = sb("sgneg", [32, T], BF16)
        shneg = sb("shneg", [86, T], BF16)

        sv = P.sem("sv")
        sx = P.sem("sx")
        sp = P.sem("sp")
        sg = P.sem("sg")
        cc = P.sem("cc")
        qdbg = P.sem("qdbg")

        qc_g = P.sem("qc_g")
        qc_s = P.sem("qc_s")
        for dst, src in [(c_idt, IDTI), (c_perm, PERMM), (c_eneg, ENEG), (c_enegd, ENEGD),
                         (c_ones, ONESC), (c_masks, MASKS), (c_ropec, ROPEC), (c_ropes, ROPES)]:
            P.dma("gpsimd", dst[:], src, qc_g)
        P.dma("sync", c_onesr[:], ONESR, qc_s)
        P.dma("sync", c_bias[:], BIAST, qc_s)
        pc_sems = {}
        pc_chunk = {}

        def precast_cols(nm_, dst_, src_, nch):
            rows = dst_.shape[0]
            step = max(1, rows // nch)
            vals = []
            for k_ in range(nch):
                s_ = P.sem(f"qpc_{nm_}_{k_}")
                P.dma("gpsimd", dst_[k_ * step:(k_ + 1) * step, :],
                      src_[k_ * step:(k_ + 1) * step, :], s_)
                vals.append((s_, s_.n))
            pc_chunk[nm_] = vals
            pc_sems[nm_] = vals[-1]

        def wait_pc(eng, nm_):
            for c_ in pc_chunk[nm_]:
                P.wait(eng, *c_)

        precast_cols("scgub", SCGUB, SCGU, 1)
        precast_cols("scdnb", SCDNB, SCDN, 1)
        QCG = qc_g.n
        QCS = qc_s.n
        for e_ in ENGINES:
            P.wait(e_, qc_g, QCG)
            P.wait(e_, qc_s, QCS)

        scopeA = ExitStack()
        sbA = lambda name, shape, dt: scopeA.enter_context(nc.sbuf_tensor(name, shape, dt))

        xnt = sbA("xnt", [128, 32 * 1024], BF16)
        xnt_v = xnt[:].rearrange("p (j c) -> p j c", c=1024)

        s1 = ExitStack()
        sb1 = lambda name, shape, dt: s1.enter_context(nc.sbuf_tensor(name, shape, dt))
        ps1 = lambda name, shape, dt=F32: s1.enter_context(nc.psum_tensor(name, shape, dt))
        xp = [sb1(f"xp{i}", [128, D], F32) for i in range(2)]
        xsq = [sb1(f"xsq{i}", [128, D], F32) for i in range(2)]
        xnb = [sb1(f"xnb{i}", [128, D], BF16) for i in range(2)]
        st1 = sb1("st1", [128, 8], F32)
        ptr_ = [ps1(f"ptr{i}", [128, 1024], BF16) for i in range(2)]
        psg = ps1("psg", [32, 1024], F32)

        r_xp = Ring(P, "r_xp", 2)
        done_tr = {}
        evac_tr = {}
        sq_cons = {}

        for t in range(8):
            r_xp.load("sync", t, [(xp[t % 2][:], XP[128 * t:128 * (t + 1), :])])
            r_xp.ready("scalar", t)
            if t >= 2:
                P.wait("scalar", sv, sq_cons[t - 2])
            sq_done = P.op("scalar", lambda e, a=xsq[t % 2], b=xp[t % 2]:
                           e.activation(out=a[:], in_=b[:], func=AF.Square), sem=sx)
            r_xp.ready("vector", t)
            m1 = P.op("vector", lambda e, o=st1, i=xp[t % 2]:
                      e.reduce_sum(out=o[:, 0:1], in_=i[:], axis=AX.X), sem=sv)
            P.wait("vector", sx, sq_done)
            m2 = P.op("vector", lambda e, o=st1, i=xsq[t % 2]:
                      e.reduce_sum(out=o[:, 1:2], in_=i[:], axis=AX.X), sem=sv)
            sq_cons[t] = m2
            P.wait("vector", sv, m2)
            m3 = P.op("vector", lambda e, o=st1: e.tensor_scalar(
                out=o[:, 2:3], in0=o[:, 0:1], scalar1=1.0 / D, scalar2=None, op0=ALU.mult), sem=sv)
            P.wait("vector", sv, m3)
            m4 = P.op("vector", lambda e, o=st1: e.tensor_tensor(
                out=o[:, 3:4], in0=o[:, 2:3], in1=o[:, 2:3], op=ALU.mult), sem=sv)
            P.wait("vector", sv, m4)
            m5 = P.op("vector", lambda e, o=st1: e.tensor_scalar(
                out=o[:, 3:4], in0=o[:, 3:4], scalar1=-1.0, scalar2=EPS,
                op0=ALU.mult, op1=ALU.add), sem=sv)
            P.wait("scalar", sv, m5)
            m6 = P.op("scalar", lambda e, o=st1: e.activation(
                out=o[:, 4:5], in_=o[:, 1:2], func=AF.Sqrt, bias=o[:, 3:4], scale=1.0 / D), sem=sx)
            P.wait("vector", sx, m6)
            m7 = P.op("vector", lambda e, o=st1: e.reciprocal(out=o[:, 5:6], in_=o[:, 4:5]), sem=sv)
            P.wait("vector", sv, m7)
            m8 = P.op("vector", lambda e, o=st1: e.tensor_scalar(
                out=o[:, 6:7], in0=o[:, 2:3], scalar1=o[:, 5:6], scalar2=-1.0,
                op0=ALU.mult, op1=ALU.mult), sem=sv)
            P.wait("vector", sv, m8)
            if t >= 2:
                P.wait("vector", sp, done_tr[(t - 2, 3)])
            m9 = P.op("vector", lambda e, o=xnb[t % 2], i=xp[t % 2], s=st1: e.tensor_scalar(
                out=o[:], in0=i[:], scalar1=s[:, 5:6], scalar2=None, op0=ALU.mult), sem=sv)
            P.wait("vector", sv, m9)
            m10 = P.op("vector", lambda e, o=xnb[t % 2], s=st1: e.tensor_scalar(
                out=o[:], in0=o[:], scalar1=s[:, 6:7], scalar2=None, op0=ALU.add), sem=sv)
            r_xp.consumed_multi(t, [(sv, m10), (sx, sq_done)])

            P.wait("tensor", sv, m10)
            if t == 0:
                P.wait("tensor", qc_g, QCG)
            for j8 in range(4):
                key = (t, j8)
                prevkey = (t, j8 - 2) if j8 >= 2 else ((t - 1, j8 + 2) if t >= 1 else None)
                if prevkey is not None and prevkey in evac_tr:
                    P.wait("tensor", sv, evac_tr[prevkey])
                for i in range(8):
                    j = 8 * j8 + i
                    done_tr[key] = P.op("tensor", lambda e, pt=ptr_[j8 % 2], ii=i, xb=xnb[t % 2], jj=j:
                                        e.transpose(out=pt[:, 128 * ii:128 * (ii + 1)],
                                                    in_=xb[:, 128 * jj:128 * (jj + 1)],
                                                    identity=c_idt[:]), sem=sp)
                P.wait("vector", sp, done_tr[key])
                evac_tr[key] = P.op("vector", lambda e, pt=ptr_[j8 % 2], jb=j8, tt=t:
                                    e.tensor_copy(
                                        out=xnt_v[:, 8 * jb:8 * (jb + 1), 128 * tt:128 * (tt + 1)],
                                        in_=pt[:].rearrange("p (i c) -> p i c", c=128)), sem=sv)

        xnt_ready = evac_tr[(7, 3)]
        P.wait("tensor", sv, xnt_ready)
        last_mm = None
        for w in range(4):
            for n in range(8):
                j = 8 * w + n
                for hf in range(2):
                    last_mm = P.op("tensor", lambda e, jj=j, h=hf, ww=w,
                                   st=(w == 0 and n == 0), sp_=(w == 3 and n == 7):
                                   e.matmul(out=psg[0:32, 512 * h:512 * (h + 1)],
                                            lhsT=c_eneg[:, 32 * ww:32 * (ww + 1)],
                                            rhs=xnt[:, 1024 * jj + 512 * h:1024 * jj + 512 * (h + 1)],
                                            start=st, stop=sp_), sem=sp)
        P.wait("vector", sp, last_mm)
        sg_evac = P.op("vector", lambda e: e.tensor_copy(
            out=sgneg[0:32, :], in_=psg[0:32, :]), sem=sv)

        if DEBUG:
            P.wait("sync", sv, sg_evac)
            P.dma("sync", dbg["XNT"].rearrange("(j p) c -> p j c", p=128),
                  xnt[:].rearrange("p (j c) -> p j c", c=1024), qdbg)
            P.dma("sync", dbg["SG"], sgneg[:], qdbg)

        s1.close()
        P.set_pguard((sv, sg_evac))
        P.barrier()

        if PHASES >= 2:
            s2 = ExitStack()
            sb2 = lambda name, shape, dt: s2.enter_context(nc.sbuf_tensor(name, shape, dt))
            s2p = ExitStack()
            ps2 = lambda name, shape, dt=F32: s2p.enter_context(nc.psum_tensor(name, shape, dt))
            wst = [sb2(f"wst{i}", [128, 32 * 128], BF16) for i in range(3)]
            wst32 = [sb2(f"wst32{i}", [128, 32 * 128], F32) for i in range(3)]
            rx = [sb2(f"rx{i}", [128, 512], BF16) for i in range(4)]
            t1b = [sb2(f"t1b{i}", [128, 512], F32) for i in range(4)]
            t2b = [sb2(f"t2b{i}", [128, 512], F32) for i in range(4)]
            qkst = [sb2(f"qkst{i}", [128, 512], BF16) for i in range(4)]
            psq = [ps2(f"psq{i}", [128, 512]) for i in range(4)]
            psw = [ps2(f"psw{i}", [128, 512]) for i in range(2)]

            r_wst = Ring(P, "r_wst", 3)
            r_qko = Ring(P, "r_qko", 4)
            rx_done = {}
            t1_done = {}
            t2_done = {}
            add_done = {}
            swap_done = {}
            idx = 0
            for (Wsrc, dstT, scale) in ((WQ, QSP, ATT_SCALE), (WK, agK_in, 1.0)):
                for oc in range(32):
                    gi = idx * 32 + oc
                    r_wst.load("sync", gi,
                               [(wst32[gi % 3][:].rearrange("p (j c) -> p j c", c=128),
                                 Wsrc[:, 128 * oc:128 * (oc + 1)].rearrange("(j p) c -> p j c", p=128))])
                    r_wst.ready("scalar", gi)
                    if gi >= 3:
                        P.wait("scalar", sp, wst_used[gi - 3])
                    wstc = P.op("scalar", lambda e, b3=gi % 3: e.copy(
                        out=wst[b3][:], in_=wst32[b3][:]), sem=sx)
                    r_wst.consumed(gi, sx, wstc)
                    if gi == 0:
                        wst_used = {}
                    for hf in range(2):
                        ii = 2 * gi + hf
                        if ii == 0:
                            P.guard("tensor")
                        P.wait("tensor", sx, wstc)
                        if ii >= 4:
                            P.wait("tensor", sx, rx_done[ii - 4])
                        mm_last = None
                        for j in range(32):
                            mm_last = P.op("tensor", lambda e, jj=j, h=hf, b=gi % 3, pq=psq[ii % 4]:
                                           e.matmul(out=pq[:],
                                                    lhsT=wst[b][:, 128 * jj:128 * (jj + 1)],
                                                    rhs=xnt[:, 1024 * jj + 512 * h:1024 * jj + 512 * (h + 1)],
                                                    start=(jj == 0), stop=(jj == 31)), sem=sp)
                        if hf == 1:
                            wst_used[gi] = mm_last
                        P.wait("scalar", sp, mm_last)
                        if ii >= 4:
                            P.wait("scalar", sp, swap_done[ii - 4])
                            P.wait("scalar", sg, t2_done[ii - 4])
                        rd = P.op("scalar", lambda e, b=ii % 4, pq=psq[ii % 4], s=scale:
                                  e.mul(out=rx[b][:], in_=pq[:], mul=s), sem=sx)
                        rx_done[ii] = rd
                        P.wait("tensor", sx, rd)
                        if ii >= 2:
                            P.wait("tensor", sv, t1_done[ii - 2])
                        sw = P.op("tensor", lambda e, b=ii % 4, w2=ii % 2:
                                  e.matmul(out=psw[w2][:], lhsT=c_perm[:], rhs=rx[b][:],
                                           start=True, stop=True), sem=sp)
                        swap_done[ii] = sw
                        P.wait("vector", sp, sw)
                        if ii >= 4:
                            P.wait("vector", sv, add_done[ii - 4])
                        td1 = P.op("vector", lambda e, b=ii % 4, h=hf, w2=ii % 2:
                                   e.tensor_tensor(out=t1b[b][:], in0=psw[w2][:],
                                                   in1=c_ropes[:, 512 * h:512 * (h + 1)],
                                                   op=ALU.mult), sem=sv)
                        t1_done[ii] = td1
                        P.wait("gpsimd", sx, rd)
                        if ii == 0:
                            P.wait("gpsimd", qc_g, QCG)
                        if ii >= 4:
                            P.wait("gpsimd", sv, add_done[ii - 4])
                        td2 = P.op("gpsimd", lambda e, b=ii % 4, h=hf:
                                   e.tensor_tensor(out=t2b[b][:], in0=rx[b][:],
                                                   in1=c_ropec[:, 512 * h:512 * (h + 1)],
                                                   op=ALU.mult), sem=sg)
                        t2_done[ii] = td2
                        P.wait("vector", sg, td2)
                        P.wait("vector", sv, td1)
                        r_qko.guard_write("vector", ii)
                        ad = P.op("vector", lambda e, b=ii % 4:
                                  e.tensor_tensor(out=qkst[b][:], in0=t1b[b][:], in1=t2b[b][:],
                                                  op=ALU.add), sem=sv)
                        add_done[ii] = ad
                        P.wait("scalar", sv, ad)
                        r_qko.load("scalar", ii,
                                   [(dstT[128 * oc:128 * (oc + 1), 512 * hf:512 * (hf + 1)],
                                     qkst[ii % 4][:])])
                idx += 1
            s2p.close()
            for s_, v_ in r_qko.totals():
                P.wait("gpsimd", s_, v_)
            P.op("gpsimd", lambda e: e.collective_compute(
                "AllGather", ALU.bypass, replica_groups=[list(range(NCORES))],
                ins=[agK_in.opt()], outs=[agK_out.opt()]), sem=cc)
            P.set_pguard((sx, rx_done[127]), (sv, max(t1_done[127], add_done[127])))
            P.barrier()

            s2v = ExitStack()
            ps2v = lambda name, shape: s2v.enter_context(nc.psum_tensor(name, shape, F32))
            psv = [ps2v(f"psv{i}", [128, 512]) for i in range(8)]
            vw = [sb2(f"vw{i}", [128, 512], BF16) for i in range(4)]
            vw32 = [sb2(f"vw32{i}", [128, 512], F32) for i in range(3)]
            vst = [sb2(f"vst{i}", [128, 512], BF16) for i in range(2)]
            r_vw = Ring(P, "r_vw", 3)
            r_vo = Ring(P, "r_vo", 2)
            v_evac = {}
            for fc in range(8):
                for j in range(32):
                    gi = fc * 32 + j
                    r_vw.load("sync", gi,
                              [(vw32[gi % 3][:], WV[128 * j:128 * (j + 1), 512 * fc:512 * (fc + 1)])])
                    r_vw.ready("scalar", gi)
                    if gi >= 4:
                        P.wait("scalar", sp, vw_used[gi - 4])
                    vwc = P.op("scalar", lambda e, b4=gi % 4, b3=gi % 3: e.copy(
                        out=vw[b4][:], in_=vw32[b3][:]), sem=sx)
                    r_vw.consumed(gi, sx, vwc)
                    if gi == 0:
                        vw_used = {}
                    P.wait("tensor", sx, vwc)
                    if gi == 0:
                        P.guard("tensor")
                    if j == 0 and fc > 0:
                        P.wait("tensor", sx, v_evac[(fc - 1, 7)])
                    last = None
                    for t in range(8):
                        last = P.op("tensor", lambda e, jj=j, tt=t, b=gi % 4:
                                    e.matmul(out=psv[tt][:],
                                             lhsT=xnt[:, 1024 * jj + 128 * tt:1024 * jj + 128 * (tt + 1)],
                                             rhs=vw[b][:],
                                             start=(jj == 0), stop=(jj == 31)), sem=sp)
                    vw_used[gi] = last
                v_fill = last
                for t in range(8):
                    gi2 = fc * 8 + t
                    P.wait("scalar", sp, v_fill)
                    r_vo.guard_write("scalar", gi2)
                    ev = P.op("scalar", lambda e, tt=t: e.copy(out=vst[tt % 2][:], in_=psv[tt][:]), sem=sx)
                    v_evac[(fc, t)] = ev
                    P.wait("scalar", sx, ev)
                    r_vo.load("scalar", gi2,
                              [(agV_in[128 * t:128 * (t + 1), 512 * fc:512 * (fc + 1)], vst[t % 2][:])])
            s2v.close()

            for s_, v_ in r_vo.totals():
                P.wait("gpsimd", s_, v_)
            P.op("gpsimd", lambda e: e.collective_compute(
                "AllGather", ALU.bypass, replica_groups=[list(range(NCORES))],
                ins=[agV_in.opt()], outs=[agV_out.opt()]), sem=cc)

            if DEBUG:
                for s_, v_ in r_qko.totals() + r_vo.totals():
                    P.wait("sync", s_, v_)
                P.dma("sync", dbg["QSP"], QSP, qdbg)
                P.dma("sync", dbg["KIN"], agK_in, qdbg)
                P.dma("sync", dbg["VIN"], agV_in, qdbg)
            s2.close()
            P.set_pguard((sx, v_evac[(7, 7)]))
            P.barrier()

        if PHASES >= 3:
            s3 = ExitStack()
            sb3 = lambda name, shape, dt: s3.enter_context(nc.sbuf_tensor(name, shape, dt))
            ps3 = lambda name, shape, dt=F32: s3.enter_context(nc.psum_tensor(name, shape, dt))
            qst = [sb3(f"qst{i}", [128, 1024], U32) for i in range(2)]
            lowb = [sb3(f"lowb{i}", [128, 1024], U32) for i in range(2)]
            highb = [sb3(f"highb{i}", [128, 1024], U32) for i in range(2)]
            scb = [sb3(f"scb{i}", [128, 1024], BF16) for i in range(2)]
            scb32 = [sb3(f"scb32{i}", [128, 1024], F32) for i in range(2)]
            w1t = [sb3(f"w1t{i}", [128, 32 * 256], BF16) for i in range(3)]
            zqst = [sb3(f"zqst{i}", [32, 256], U32) for i in range(2)]
            sczb = [sb3(f"sczb{i}", [32, 256], BF16) for i in range(2)]
            zu32 = sb3("zu32", [32, 256], U32)
            zf32 = sb3("zf32", [32, 256], F32)
            zbuf = [sb3(f"zbuf{i}", [32, 256], BF16) for i in range(2)]
            hgate = [sb3(f"hgate{i}", [128, 512], BF16) for i in range(2)]
            hchunk = [sb3(f"hchunk{i}", [128, 1024], BF16) for i in range(2)]
            pg = [ps3(f"pg{i}", [128, 512]) for i in range(2)]
            pu = [ps3(f"pu{i}", [128, 512]) for i in range(2)]
            psh = ps3("psh", [86, 1024])

            r_qw = Ring(P, "r_qw", 2)
            r_sc = Ring(P, "r_sc", 2)
            r_zq = Ring(P, "r_zq", 2)
            r_scz = Ring(P, "r_scz", 2)
            r_hsp = Ring(P, "r_hsp", 2)
            mult_done = {}
            h_done = {}
            gu_fill = {}
            hg_done = {}
            bit_done = {}
            sh_mm_done = {}

            for c_ in pc_chunk["scgub"]:
                P.wait("scalar", *c_)
            z_done = {}
            scb_cast = {}

            def stageA(ci):
                wp, n, kk = DN_CHUNKS[ci]
                cb = 1024 * wp + kk * n
                b = ci % 2
                r_qw.load("sync", ci, [
                    (qst[b][:].rearrange("p (w g c) -> p w g c", w=4, g=2)[:, :, 0, 0:kk],
                     QWGU[:, cb:cb + kk].rearrange("(w p) c -> p w c", p=128)),
                    (qst[b][:].rearrange("p (w g c) -> p w g c", w=4, g=2)[:, :, 1, 0:kk],
                     QWGU[:, INTER + cb:INTER + cb + kk].rearrange("(w p) c -> p w c", p=128)),
                ])
                r_sc.load("sync", ci, [(scb32[b][:], SCEXP[:, 1024 * ci:1024 * (ci + 1)])])
                r_sc.ready("scalar", ci)
                if ci >= 2:
                    P.wait("scalar", sv, mult_done[ci - 2][0])
                    P.wait("scalar", sg, mult_done[ci - 2][1])
                scbc = P.op("scalar", lambda e, bb=b: e.copy(
                    out=scb[bb][:], in_=scb32[bb][:]), sem=sx)
                r_sc.consumed(ci, sx, scbc)
                scb_cast[ci] = scbc
                r_zq.load("sync", ci, [
                    (zqst[b][0:32, 0:kk], QZGU[:, 128 * wp:128 * wp + kk]),
                    (zqst[b][0:32, 128:128 + kk], QZGU[:, 1376 + 128 * wp:1376 + 128 * wp + kk]),
                ])
                r_scz.load("scalar", ci, [
                    (sczb[b][0:32, 0:kk], SCGUB[:, cb:cb + kk]),
                    (sczb[b][0:32, 128:128 + kk], SCGUB[:, INTER + cb:INTER + cb + kk]),
                ])

                r_qw.ready("vector", ci)
                if ci >= 2:
                    P.wait("vector", sv, mult_done[ci - 2][0])
                    P.wait("vector", sg, mult_done[ci - 2][1])
                P.op("vector", lambda e, bb=b: e.tensor_scalar(
                    out=lowb[bb][:], in0=qst[bb][:], scalar1=0x0F0F0F0F, scalar2=None,
                    op0=ALU.bitwise_and), sem=sv)
                bd = P.op("vector", lambda e, bb=b: e.tensor_scalar(
                    out=highb[bb][:], in0=qst[bb][:], scalar1=4, scalar2=0x0F0F0F0F,
                    op0=ALU.logical_shift_right, op1=ALU.bitwise_and), sem=sv)
                bit_done[ci] = bd
                r_qw.consumed(ci, sv, bd)

                r_zq.ready("vector", ci)
                r_scz.ready("vector", ci)
                if ci >= 2:
                    P.wait("vector", sp, gu_fill[(ci - 2, 1, 1)])
                z1 = P.op("vector", lambda e, bb=b, nn=n: e.tensor_scalar(
                    out=zu32[0:32, :], in0=zqst[bb][:], scalar1=4 * nn, scalar2=15,
                    op0=ALU.logical_shift_right, op1=ALU.bitwise_and), sem=sv)
                r_zq.consumed(ci, sv, z1)
                P.wait("vector", sv, z1)
                z2 = P.op("vector", lambda e: e.tensor_scalar(
                    out=zf32[0:32, :], in0=zu32[0:32, :], scalar1=1.0, scalar2=None,
                    op0=ALU.add), sem=sv)
                P.wait("vector", sv, z2)
                z3 = P.op("vector", lambda e, bb=b: e.tensor_tensor(
                    out=zbuf[bb][0:32, :], in0=zf32[0:32, :], in1=sczb[bb][0:32, :],
                    op=ALU.mult), sem=sv)
                z_done[ci] = z3
                r_scz.consumed(ci, sv, z3)

                P.wait("vector", sv, bd)
                P.wait("vector", sx, scb_cast[ci])
                P.wait("gpsimd", sv, bd)
                P.wait("gpsimd", sx, scb_cast[ci])
                if ci >= 3:
                    P.wait("vector", sp, gu_fill[(ci - 3, 1, 1)])
                    P.wait("gpsimd", sp, gu_fill[(ci - 3, 1, 1)])
                mv = mg = 0
                for j in range(32):
                    w, nib = j // 8, j % 8
                    plane = (highb if (nib % 2) else lowb)[b]
                    byte_off = 1024 * w + (nib // 2)
                    eng = "vector" if (j % 2 == 0) else "gpsimd"
                    msem = sv if eng == "vector" else sg
                    val = P.op(eng, lambda e, pl=plane, bo=byte_off, ww=w, bb=b, w3=ci % 3, jj=j: e.tensor_tensor(
                        out=w1t[w3][:, 256 * jj:256 * (jj + 1)],
                        in0=pl[:].bitcast(U8)[:, bo:(bo - bo % 1024) + 1024:4],
                        in1=scb[bb][:, 256 * ww:256 * (ww + 1)],
                        op=ALU.mult), sem=msem)
                    if eng == "vector":
                        mv = val
                    else:
                        mg = val
                mult_done[ci] = (mv, mg)

            def emit_sh_tail(cj):
                wpj, nj, kkj = DN_CHUNKS[cj]
                bj = cj % 2
                P.wait("tensor", sv, h_done[(cj, 1)])
                for hf2 in range(2):
                    sh_mm_done[cj] = P.op("tensor", lambda e, h=hf2, bb=bj, k2=kkj, w2=wpj,
                                          st=(cj == 0), sp_=(cj == 87):
                                          e.matmul(out=psh[0:86, 512 * h:512 * (h + 1)],
                                                   lhsT=c_enegd[0:k2, 86 * w2:86 * (w2 + 1)],
                                                   rhs=hchunk[bb][0:k2, 512 * h:512 * (h + 1)],
                                                   start=st, stop=sp_), sem=sp)
                P.wait("scalar", sv, h_done[(cj, 1)])
                P.wait("scalar", sp, sh_mm_done[cj])
                rbj = DN_ROWBASE[cj]
                r_hsp.load("scalar", cj, [(HTS[rbj:rbj + kkj, :], hchunk[bj][0:kkj, :])])

            def stageB(ci):
                wp, n, kk = DN_CHUNKS[ci]
                b = ci % 2
                if ci == 0:
                    P.guard("tensor")
                P.wait("tensor", sv, mult_done[ci][0])
                P.wait("tensor", sg, mult_done[ci][1])
                P.wait("tensor", sv, z_done[ci])
                for hf in range(2):
                    if ci >= 1:
                        P.wait("tensor", sx, hg_done[(ci - 1, hf)])
                        P.wait("tensor", sv, h_done[(ci - 1, hf)])
                    for gu in range(2):
                        tgt = (pg if gu == 0 else pu)[hf]
                        for j in range(32):
                            P.op("tensor", lambda e, jj=j, g=gu, h=hf, w3=ci % 3, tg=tgt, k2=kk:
                                 e.matmul(out=tg[0:k2, :],
                                          lhsT=w1t[w3][:, 256 * jj + 128 * g:256 * jj + 128 * g + k2],
                                          rhs=xnt[:, 1024 * jj + 512 * h:1024 * jj + 512 * (h + 1)],
                                          start=(jj == 0), stop=False), sem=None)
                        gu_fill[(ci, hf, gu)] = P.op("tensor", lambda e, g=gu, h=hf, bb=b, tg=tgt, k2=kk:
                                                     e.matmul(out=tg[0:k2, :],
                                                              lhsT=zbuf[bb][0:32, 128 * g:128 * g + k2],
                                                              rhs=sgneg[:, 512 * h:512 * (h + 1)],
                                                              start=False, stop=True), sem=sp)
                    P.wait("scalar", sp, gu_fill[(ci, hf, 0)])
                    hg = P.op("scalar", lambda e, h=hf, k2=kk: e.activation(
                        out=hgate[h][0:k2, :], in_=pg[h][0:k2, :], func=AF.Silu), sem=sx)
                    hg_done[(ci, hf)] = hg
                    P.wait("vector", sp, gu_fill[(ci, hf, 1)])
                    P.wait("vector", sx, hg)
                    if hf == 0:
                        r_hsp.guard_write("vector", ci)
                    if ci >= 2:
                        P.wait("vector", sp, sh_mm_done.get(ci - 2, 0))
                    hd = P.op("vector", lambda e, h=hf, bb=b, k2=kk: e.tensor_tensor(
                        out=hchunk[bb][0:k2, 512 * h:512 * (h + 1)],
                        in0=pu[h][0:k2, :], in1=hgate[h][0:k2, :], op=ALU.mult), sem=sv)
                    h_done[(ci, hf)] = hd
                if ci >= 1:
                    emit_sh_tail(ci - 1)

            stageA(0)
            for ci in range(1, 88):
                stageA(ci)
                stageB(ci - 1)
            stageB(87)
            emit_sh_tail(87)
            P.wait("vector", sp, sh_mm_done[87])
            sh_evac = P.op("vector", lambda e: e.tensor_copy(
                out=shneg[0:86, :], in_=psh[0:86, :]), sem=sv)

            if DEBUG:
                for s_, v_ in r_hsp.totals():
                    P.wait("sync", s_, v_)
                P.wait("sync", sv, sh_evac)
                P.dma("sync", dbg["HTS"], HTS, qdbg)
                P.dma("sync", dbg["SH"], shneg[:], qdbg)
            s3.close()
            for s_, v_ in r_hsp.totals():
                P.wait("sync", s_, v_)
            P.wait("sync", qdbg, qdbg.n)
            P.set_pguard((sx, hg_done[(87, 1)]), (sv, max(h_done[(87, 1)], sh_evac)))
            P.barrier()

        scopeA.close()

        if PHASES >= 4:
            s4 = ExitStack()
            sb4 = lambda name, shape, dt: s4.enter_context(nc.sbuf_tensor(name, shape, dt))
            ps4 = lambda name, shape: s4.enter_context(nc.psum_tensor(name, shape, F32))
            zd = sb4("zd", [86, D], BF16)
            zdq = sb4("zdq", [86, 512], U32)
            zdnu = sb4("zdnu", [86, 512], U32)
            zdn = sb4("zdn", [86, 512], F32)
            scd_full = sb4("scd_full", [86, D], BF16)
            qwd = [sb4(f"qwd{i}", [128, 512], U32) for i in range(2)]
            lowd = [sb4(f"lowd{i}", [128, 512], U32) for i in range(2)]
            highd = [sb4(f"highd{i}", [128, 512], U32) for i in range(2)]
            scdx = [sb4(f"scdx{i}", [128, 512], BF16) for i in range(2)]
            scdx32 = [sb4(f"scdx32{i}", [128, 512], F32) for i in range(2)]
            w1d = [sb4(f"w1d{i}", [128, 512], BF16) for i in range(3)]
            htb = [sb4(f"htb{i}", [128, 1024], BF16) for i in range(3)]
            gst = [sb4(f"gst{i}", [128, 512], BF16) for i in range(2)]
            pd = [ps4(f"pd{i}", [128, 512]) for i in range(8)]

            r_zd = Ring(P, "r_zd", 1)
            r_scd = Ring(P, "r_scd", 1)
            r_qwd = Ring(P, "r_qwd", 2)
            r_sdx = Ring(P, "r_sdx", 2)
            r_ht = Ring(P, "r_ht", 3)
            r_go = Ring(P, "r_go", 2)

            wait_pc("sync", "scdnb")
            r_zd.load("sync", 0, [(zdq[:], QZDN)])
            r_scd.load("sync", 0, [(scd_full[:], SCDNB)])
            r_zd.ready("vector", 0)
            r_scd.ready("vector", 0)
            zprev = 0
            for n2 in range(8):
                P.wait("vector", sv, zprev)
                a1 = P.op("vector", lambda e, nn=n2: e.tensor_scalar(
                    out=zdnu[:], in0=zdq[:], scalar1=4 * nn, scalar2=15,
                    op0=ALU.logical_shift_right, op1=ALU.bitwise_and), sem=sv)
                P.wait("vector", sv, a1)
                a2 = P.op("vector", lambda e: e.tensor_scalar(
                    out=zdn[:], in0=zdnu[:], scalar1=1.0, scalar2=None, op0=ALU.add), sem=sv)
                P.wait("vector", sv, a2)
                zprev = P.op("vector", lambda e, nn=n2: e.tensor_tensor(
                    out=zd[:].rearrange("p (c e) -> p c e", e=8)[:, :, nn],
                    in0=zdn[:],
                    in1=scd_full[:].rearrange("p (c e) -> p c e", e=8)[:, :, nn],
                    op=ALU.mult), sem=sv)
            zd_ready = zprev

            dn_bit_done = {}
            dn_mult_used = {}
            sdx_cast = {}
            g_evac = {}
            w1d_used = {}
            for hc in range(8):
                for ci, (wp, n, kk) in enumerate(DN_CHUNKS):
                    gi = hc * 88 + ci
                    b3 = gi % 3
                    wi = hc * 11 + wp
                    if n == 0:
                        r_qwd.load("sync", wi,
                                   [(qwd[wi % 2][0:kk, :],
                                     QWDN[128 * wp:128 * wp + kk, 512 * hc:512 * (hc + 1)])])
                        r_sdx.load("sync", wi,
                                   [(scdx32[wi % 2][:],
                                     SCDEXP[:, 4096 * wp + 512 * hc:4096 * wp + 512 * (hc + 1)])])
                        r_sdx.ready("scalar", wi)
                        if wi >= 2:
                            P.wait("scalar", sv, dn_mult_used[wi - 2][0])
                            P.wait("scalar", sg, dn_mult_used[wi - 2][1])
                        sdxc = P.op("scalar", lambda e, w2=wi % 2: e.copy(
                            out=scdx[w2][:], in_=scdx32[w2][:]), sem=sx)
                        r_sdx.consumed(wi, sx, sdxc)
                        sdx_cast[wi] = sdxc
                        r_qwd.ready("vector", wi)
                        if wi >= 2:
                            P.wait("vector", sv, dn_mult_used[wi - 2][0])
                            P.wait("vector", sg, dn_mult_used[wi - 2][1])
                        P.op("vector", lambda e, w2=wi % 2, k2=kk: e.tensor_scalar(
                            out=lowd[w2][0:k2, :], in0=qwd[w2][0:k2, :], scalar1=0x0F0F0F0F,
                            scalar2=None, op0=ALU.bitwise_and), sem=sv)
                        bitd = P.op("vector", lambda e, w2=wi % 2, k2=kk: e.tensor_scalar(
                            out=highd[w2][0:k2, :], in0=qwd[w2][0:k2, :], scalar1=4,
                            scalar2=0x0F0F0F0F, op0=ALU.logical_shift_right,
                            op1=ALU.bitwise_and), sem=sv)
                        dn_bit_done[wi] = bitd
                        r_qwd.consumed(wi, sv, bitd)
                        dn_mult_mv = 0
                        dn_mult_mg = 0
                    rb = DN_ROWBASE[ci]
                    r_ht.load("scalar", gi, [(htb[b3][0:kk, :], HTS[rb:rb + kk, :])])
                    eng = "vector" if (ci % 2 == 0) else "gpsimd"
                    msem = sv if eng == "vector" else sg
                    P.wait(eng, sv, dn_bit_done[wi])
                    P.wait(eng, sx, sdx_cast[wi])
                    if gi >= 3:
                        P.wait(eng, sp, w1d_used[gi - 3])
                    plane = (highd if (n % 2) else lowd)[wi % 2]
                    mval = P.op(eng, lambda e, pl=plane, bo=(n // 2), b3_=b3, k2=kk, w2=wi % 2:
                                e.tensor_tensor(out=w1d[b3_][0:k2, :],
                                                in0=pl[0:k2, :].bitcast(U8)[:, bo:2048:4],
                                                in1=scdx[w2][0:k2, :], op=ALU.mult), sem=msem)
                    if eng == "vector":
                        dn_mult_mv = mval
                    else:
                        dn_mult_mg = mval
                    if n == 7:
                        dn_mult_used[wi] = (dn_mult_mv, dn_mult_mg)
                    if gi == 0:
                        P.guard("tensor")
                    P.wait("tensor", msem, mval)
                    r_ht.ready("tensor", gi)
                    if ci == 0 and hc > 0:
                        P.wait("tensor", sx, g_evac[(hc - 1, 7)])
                    lastmm = None
                    for t in range(8):
                        lastmm = P.op("tensor", lambda e, tt=t, b3_=b3, k2=kk, st=(ci == 0):
                                      e.matmul(out=pd[tt][:],
                                               lhsT=htb[b3_][0:k2, 128 * tt:128 * (tt + 1)],
                                               rhs=w1d[b3_][0:k2, :],
                                               start=st, stop=False),
                                      sem=(sp if t == 7 else None))
                    r_ht.consumed(gi, sp, lastmm)
                    w1d_used[gi] = lastmm
                P.wait("tensor", sv, zd_ready)
                zf = None
                for t in range(8):
                    zf = P.op("tensor", lambda e, tt=t, h2=hc: e.matmul(
                        out=pd[tt][:], lhsT=shneg[:, 128 * tt:128 * (tt + 1)],
                        rhs=zd[:, 512 * h2:512 * (h2 + 1)], start=False, stop=True),
                        sem=(sp if t == 7 else None))
                for t in range(8):
                    gi2 = hc * 8 + t
                    P.wait("scalar", sp, zf)
                    r_go.guard_write("scalar", gi2)
                    ev = P.op("scalar", lambda e, tt=t: e.activation(
                        out=gst[tt % 2][:], in_=pd[tt][:], func=AF.Silu), sem=sx)
                    g_evac[(hc, t)] = ev
                    P.wait("scalar", sx, ev)
                    r_go.load("scalar", gi2,
                              [(GSP[128 * t:128 * (t + 1), 512 * hc:512 * (hc + 1)], gst[t % 2][:])])
            if DEBUG:
                for s_, v_ in r_go.totals():
                    P.wait("sync", s_, v_)
                P.dma("sync", dbg["GSP"], GSP, qdbg)
            s4.close()
            for s_, v_ in r_go.totals():
                P.wait("sync", s_, v_)
            P.set_pguard((sx, g_evac[(7, 7)]))
            P.barrier()

        if PHASES >= 5:
            s5 = ExitStack()
            sb5 = lambda name, shape, dt: s5.enter_context(nc.sbuf_tensor(name, shape, dt))
            s5p = ExitStack()
            ps5 = lambda name, shape, dt=F32: s5p.enter_context(nc.psum_tensor(name, shape, dt))
            slab5 = sb5("slab5", [128, 32 * 1024], BF16)
            ktb = [sb5(f"ktb{i}", [128, 512], BF16) for i in range(6)]
            vtb = [sb5(f"vtb{i}", [128, 512], BF16) for i in range(6)]
            pb = [sb5(f"pb{i}", [128, 512], BF16) for i in range(8)]
            linv = [sb5(f"linv{i}", [1, 512], F32) for i in range(2)]
            linvrep = [sb5(f"linvrep{i}", [128, 512], F32) for i in range(2)]
            ps_s = [ps5(f"ps_s{i}", [128, 512]) for i in range(3)]
            ps_o = [ps5(f"ps_o{i}", [128, 512]) for i in range(2)]
            ps_l = [ps5(f"ps_l{i}", [1, 512]) for i in range(2)]
            ps_b = ps5("ps_b", [128, 512])

            r_ql = Ring(P, "r_ql", 1)
            r_kt = Ring(P, "r_kt", 6)
            r_vt = Ring(P, "r_vt", 6)
            r_ql.load("sync", 0, [(slab5[:].rearrange("p (j c) -> p j c", c=1024),
                                   QSP.rearrange("(j p) c -> p j c", p=128))])
            P.wait("sync", cc, 2)
            P.wait("scalar", cc, 2)
            exp_done = {}
            pb_used = {}
            norm_done = {}
            for u in range(2):
                for h in range(H):
                    uh = u * H + h
                    qo = 1024 * h + 512 * u
                    def emit_outsum(ib2):
                        r2, i2 = divmod(ib2, 4)
                        gi2_ = uh * 9 + r2
                        ib_g = uh * 36 + ib2
                        if r2 == 8:
                            use_sem, use_val = sv, mask_done[ib_g]
                        else:
                            use_sem, use_val = sx, exp_done[ib_g]
                        P.wait("tensor", use_sem, use_val)
                        if i2 == 0:
                            r_vt.ready("tensor", gi2_)
                        first = (ib2 == 0)
                        last = (ib2 == 35)
                        if first and uh > 1:
                            P.wait("tensor", sv, norm_done[uh - 2])
                        ub2 = uh % 2
                        P.op("tensor", lambda e, g3=gi2_ % 6, ii=i2, ibb=ib_g, f=first, l=last, u2=ub2:
                             e.matmul(out=ps_o[u2][:], lhsT=vtb[g3][:, 128 * ii:128 * (ii + 1)],
                                      rhs=pb[ibb % 8][:], start=f, stop=l), sem=None)
                        omm = P.op("tensor", lambda e, ibb=ib_g, f=first, l=last, u2=ub2:
                                   e.matmul(out=ps_l[u2][:], lhsT=c_ones[:],
                                            rhs=pb[ibb % 8][:], start=f, stop=l), sem=sp)
                        pb_used[ib_g] = omm
                        if i2 == 3:
                            r_vt.consumed(gi2_, sp, omm)

                    mask_done = {}
                    for r in range(9):
                        gi = uh * 9 + r
                        if r < 8:
                            ksrc = agK_out[4096 * r + 128 * h:4096 * r + 128 * (h + 1),
                                           512 * u:512 * (u + 1)]
                            vsrc = agV_out[1024 * r + 512 * u:1024 * r + 512 * (u + 1),
                                           128 * h:128 * (h + 1)].rearrange("(i p) d -> p i d", p=128)
                        else:
                            ksrc = agK_in[128 * h:128 * (h + 1), 512 * u:512 * (u + 1)]
                            vsrc = agV_in[512 * u:512 * (u + 1),
                                          128 * h:128 * (h + 1)].rearrange("(i p) d -> p i d", p=128)
                        r_kt.load("sync", gi, [(ktb[gi % 6][:], ksrc)])
                        r_vt.load("scalar", gi,
                                  [(vtb[gi % 6][:].rearrange("p (i d) -> p i d", d=128), vsrc)])
                        if gi == 0:
                            P.guard("tensor")
                            r_ql.ready("tensor", 0)
                        r_kt.ready("tensor", gi)
                        for i in range(4):
                            kt = 4 * r + i
                            lb = r * 4 + i
                            ib = uh * 36 + lb
                            sb_ = ib % 3
                            if ib >= 3:
                                P.wait("tensor", sx, exp_done[ib - 3])
                            smm = P.op("tensor", lambda e, g3=gi % 6, ii=i, s2=sb_, q1=qo:
                                       e.matmul(out=ps_s[s2][:],
                                                lhsT=ktb[g3][:, 128 * ii:128 * (ii + 1)],
                                                rhs=slab5[:, q1:q1 + 512],
                                                start=True, stop=True), sem=sp)
                            if i == 3:
                                r_kt.consumed(gi, sp, smm)
                            P.wait("scalar", sp, smm)
                            if ib >= 8:
                                P.wait("scalar", sp, pb_used[ib - 8])
                            bias_col = kt if r < 8 else 32
                            ed = P.op("scalar", lambda e, s2=sb_, ibb=ib, bc=bias_col:
                                      e.activation(out=pb[ibb % 8][:], in_=ps_s[s2][:],
                                                   func=AF.Exp, bias=c_bias[:, bc:bc + 1],
                                                   scale=1.0), sem=sx)
                            exp_done[ib] = ed
                            if r == 8:
                                P.wait("vector", sx, ed)
                                md = P.op("vector", lambda e, ibb=ib, ii=i: e.tensor_tensor(
                                    out=pb[ibb % 8][:], in0=pb[ibb % 8][:],
                                    in1=c_masks[:, 512 * ii:512 * (ii + 1)], op=ALU.mult), sem=sv)
                                mask_done[ib] = md
                            if lb >= 2:
                                emit_outsum(lb - 2)
                    emit_outsum(34)
                    emit_outsum(35)
                    ub = uh % 2
                    P.wait("vector", sp, pb_used[uh * 36 + 35])
                    r1 = P.op("vector", lambda e, u2=ub: e.reciprocal(
                        out=linv[u2][:], in_=ps_l[u2][:]), sem=sv)
                    P.wait("tensor", sv, r1)
                    if uh >= 1:
                        P.wait("tensor", sv, linvrep_done[uh - 1])
                    bc2 = P.op("tensor", lambda e, u2=ub: e.matmul(
                        out=ps_b[:], lhsT=c_onesr[:], rhs=linv[u2][:], start=True, stop=True), sem=sp)
                    P.wait("vector", sp, bc2)
                    r2 = P.op("vector", lambda e, u2=ub: e.tensor_copy(
                        out=linvrep[u2][:], in_=ps_b[:]), sem=sv)
                    if uh == 0:
                        linvrep_done = {}
                    linvrep_done[uh] = r2
                    P.wait("vector", sv, r2)
                    nd = P.op("vector", lambda e, q1=qo, u2=ub: e.tensor_tensor(
                        out=slab5[:, q1:q1 + 512], in0=ps_o[u2][:], in1=linvrep[u2][:],
                        op=ALU.mult), sem=sv)
                    norm_done[uh] = nd

            if DEBUG:
                P.wait("sync", sv, norm_done[63])
                P.dma("sync", dbg["SPRE"].rearrange("(j p) c -> p j c", p=128),
                      slab5[:].rearrange("p (j c) -> p j c", c=1024), qdbg)
            s5p.close()
            P.set_pguard((sv, norm_done[63]))
            P.barrier()

            if PHASES >= 6:
                s6 = ExitStack()
                sb6 = lambda name, shape, dt: s6.enter_context(nc.sbuf_tensor(name, shape, dt))
                ps6 = lambda name, shape: s6.enter_context(nc.psum_tensor(name, shape, F32))
                wot = [sb6(f"wot{i}", [128, 512], BF16) for i in range(4)]
                wot32 = [sb6(f"wot32{i}", [128, 512], F32) for i in range(3)]
                xb6 = [sb6(f"xb6{i}", [128, 512], F32) for i in range(2)]
                gb6 = [sb6(f"gb6{i}", [128, 512], BF16) for i in range(2)]
                sgb = [sb6(f"sgb{i}", [128, 512], F32) for i in range(2)]
                ob6 = [sb6(f"ob6{i}", [128, 512], F32) for i in range(2)]
                pc6 = [ps6(f"pc6{i}", [128, 512]) for i in range(8)]

                r_wo = Ring(P, "r_wo", 3)
                wo_cast = {}
                r_xb = Ring(P, "r_xb", 2)
                r_gb = Ring(P, "r_gb", 2)
                r_out = Ring(P, "r_out", 2)
                comb = {}
                for fc in range(8):
                    for dc in range(32):
                        gi = fc * 32 + dc
                        r_wo.load("scalar", gi,
                                  [(wot32[gi % 3][:], WO[128 * dc:128 * (dc + 1), 512 * fc:512 * (fc + 1)])])
                        r_wo.ready("scalar", gi)
                        if gi >= 4:
                            P.wait("scalar", sp, wo_used_d[gi - 4])
                        wc = P.op("scalar", lambda e, b4=gi % 4, b3_=gi % 3:
                                  e.copy(out=wot[b4][:], in_=wot32[b3_][:]), sem=sx)
                        wo_cast[gi] = wc
                        r_wo.consumed(gi, sx, wc)
                        P.wait("tensor", sx, wc)
                        if gi == 0:
                            P.guard("tensor")
                            wo_used_d = {}
                        if dc == 0 and fc > 0:
                            P.wait("tensor", sv, comb[(fc - 1, 7)])
                        wdone = None
                        for t in range(8):
                            wdone = P.op("tensor", lambda e, tt=t, b=gi % 4, d2=dc:
                                         e.matmul(out=pc6[tt][:],
                                                  lhsT=slab5[:, 1024 * d2 + 128 * tt:1024 * d2 + 128 * (tt + 1)],
                                                  rhs=wot[b][:],
                                                  start=(d2 == 0), stop=(d2 == 31)),
                                         sem=(sp if t == 7 else None))
                        wo_used_d[gi] = wdone
                    for t in range(8):
                        gi2 = fc * 8 + t
                        r_xb.load("sync", gi2,
                                  [(xb6[t % 2][:], XNAT[128 * t:128 * (t + 1), 512 * fc:512 * (fc + 1)])])
                        r_gb.load("scalar", gi2,
                                  [(gb6[t % 2][:], GSP[128 * t:128 * (t + 1), 512 * fc:512 * (fc + 1)])])
                        P.wait("vector", sp, wdone)
                        r_gb.ready("vector", gi2)
                        r_xb.ready("vector", gi2)
                        r_out.guard_write("vector", gi2)
                        c1 = P.op("vector", lambda e, tt=t: e.tensor_tensor(
                            out=sgb[tt % 2][:], in0=pc6[tt][:], in1=gb6[tt % 2][:],
                            op=ALU.mult), sem=sv)
                        P.wait("vector", sv, c1)
                        c2 = P.op("vector", lambda e, tt=t: e.tensor_tensor(
                            out=ob6[tt % 2][:], in0=sgb[tt % 2][:], in1=xb6[tt % 2][:],
                            op=ALU.add), sem=sv)
                        comb[(fc, t)] = c2
                        r_xb.consumed(gi2, sv, c2)
                        r_gb.consumed(gi2, sv, c2)
                        P.wait("sync", sv, c2)
                        r_out.load("sync", gi2,
                                   [(OUT[128 * t:128 * (t + 1), 512 * fc:512 * (fc + 1)], ob6[t % 2][:])])
                for s_, v_ in r_out.totals():
                    P.wait("sync", s_, v_)
                s6.close()
            s5.close()

        P.wait("sync", qdbg, qdbg.n)

        block = es.enter_context(nc.Block())
        P.finalize(block)

    counts = {e: len(P.streams[e]) for e in ENGINES}
    return nc, counts



def prep_inputs(L0_attn_x, L0_attn_wq, L0_attn_wk, L0_attn_wv, L0_attn_wo,
                L0_ffn_qweight_gu, L0_ffn_qzeros_gu, L0_ffn_scales_gu, L0_ffn_g_idx_gu,
                L0_ffn_qweight_down, L0_ffn_qzeros_down, L0_ffn_scales_down, L0_ffn_g_idx_down):
    f32 = np.float32
    x = np.asarray(L0_attn_x, f32)
    wq = np.asarray(L0_attn_wq, f32)
    wk = np.asarray(L0_attn_wk, f32)
    wv = np.asarray(L0_attn_wv, f32)
    wo = np.ascontiguousarray(np.asarray(L0_attn_wo, f32))
    qwgu = np.asarray(L0_ffn_qweight_gu, np.uint32)
    qzgu = np.ascontiguousarray(np.asarray(L0_ffn_qzeros_gu, np.uint32))
    scgu = np.asarray(L0_ffn_scales_gu, f32)
    qwdn = np.ascontiguousarray(np.asarray(L0_ffn_qweight_down, np.uint32))
    qzdn = np.ascontiguousarray(np.asarray(L0_ffn_qzeros_down, np.uint32))
    scdn = np.ascontiguousarray(np.asarray(L0_ffn_scales_down, f32))

    wq_p = np.ascontiguousarray(wq[PERM_H, :])
    wk_p = np.ascontiguousarray(wk[PERM_H, :])
    wv_p = np.ascontiguousarray(wv[PERM_H, :])
    qwgu_p = np.ascontiguousarray(
        np.concatenate([qwgu[:, :INTER][:, PERM_D], qwgu[:, INTER:][:, PERM_D]], axis=1))
    scgu_p = np.ascontiguousarray(
        np.concatenate([scgu[:, :INTER][:, PERM_D], scgu[:, INTER:][:, PERM_D]], axis=1))

    gidx = np.arange(128) // 16
    scexp = np.zeros((128, 88, 4, 2, 128), f32)
    rows_by_w = [scgu_p[8 * w + gidx] for w in range(4)]
    for ci, (wp, n, kk) in enumerate(DN_CHUNKS):
        cb = 1024 * wp + kk * n
        for w in range(4):
            scexp[:, ci, w, 0, :kk] = rows_by_w[w][:, cb:cb + kk]
            scexp[:, ci, w, 1, :kk] = rows_by_w[w][:, INTER + cb:INTER + cb + kk]
    scexp = np.ascontiguousarray(scexp.reshape(128, -1))
    scdexp = np.empty((128, 11, D), f32)
    for wp in range(11):
        ridx = np.minimum(8 * wp + gidx, 85)
        scdexp[:, wp, :] = scdn[ridx]
    scdexp = np.ascontiguousarray(scdexp.reshape(128, -1))

    permm = np.zeros((128, 128), f32)
    for i in range(0, 128, 2):
        permm[i, i + 1] = 1.0
        permm[i + 1, i] = 1.0
    eneg = np.zeros((128, 4, 32), f32)
    for w in range(4):
        eneg[np.arange(128), w, 8 * w + np.arange(128) // 16] = -1.0
    eneg = eneg.reshape(128, 128)
    enegd = np.zeros((128, 11, 86), f32)
    for wp in range(11):
        kk = 128 if wp < 10 else 96
        enegd[np.arange(kk), wp, 8 * wp + np.arange(kk) // 16] = -1.0
    enegd = enegd.reshape(128, 11 * 86)
    onesc = np.ones((128, 1), f32)
    onesr = np.ones((1, 128), f32)
    masks = np.zeros((128, 4, SEG), f32)
    for dlt in range(4):
        pp = np.arange(128)[:, None]
        jj = np.arange(SEG)[None, :]
        masks[:, dlt, :] = (pp <= jj - 128 * dlt).astype(f32)
    masks = masks.reshape(128, 4 * SEG)

    inv = 1.0 / (10000.0 ** (np.arange(0, HD, 2)[:64].astype(f32) / HD))

    per_core = []
    for c in range(NCORES):
        pos = np.concatenate([512 * c + np.arange(SEG), 512 * c + np.arange(SEG)])
        f = np.outer(pos, inv).astype(f32)
        cosf = np.cos(f)
        sinf = np.sin(f)
        ropec = np.empty((128, T), f32)
        ropes = np.empty((128, T), f32)
        ii = np.arange(64)
        ropec[2 * ii, :] = cosf.T
        ropec[2 * ii + 1, :] = cosf.T
        ropes[2 * ii, :] = -sinf.T
        ropes[2 * ii + 1, :] = sinf.T
        biast = np.zeros((128, 33), f32)
        for kt in range(32):
            if kt >= 4 * c:
                biast[:, kt] = NEG
        xa = x[0, 512 * c:512 * (c + 1)]
        xb = x[1, 512 * c:512 * (c + 1)]
        xnat = np.ascontiguousarray(np.concatenate([xa, xb], axis=0))
        xperm = np.ascontiguousarray(xnat[:, PERM_H])
        per_core.append(dict(XP=xperm, XNAT=xnat, ROPEC=ropec, ROPES=ropes, BIAST=biast))

    shared = dict(
        WQ=wq_p, WK=wk_p, WV=wv_p, WO=wo, QWGU=qwgu_p, SCGU=scgu_p, QZGU=qzgu,
        QWDN=qwdn, SCDN=scdn, QZDN=qzdn, SCEXP=scexp, SCDEXP=scdexp,
        PERMM=permm, IDTI=np.eye(128, dtype=f32), ENEG=eneg, ENEGD=enegd,
        ONESC=onesc, ONESR=onesr, MASKS=masks,
    )
    in_maps = []
    for c in range(NCORES):
        m = dict(shared)
        m.update(per_core[c])
        in_maps.append(m)
    return in_maps


def kernel(**inputs):
    in_maps = prep_inputs(**inputs)
    nc, counts = build_program()
    res = run_bass_kernel_spmd(nc, in_maps, core_ids=list(range(NCORES)))
    out = np.empty((2, 4096, D), np.float32)
    for c in range(NCORES):
        o = np.asarray(res.results[c]["OUT"])
        out[0, 512 * c:512 * (c + 1)] = o[:SEG]
        out[1, 512 * c:512 * (c + 1)] = o[SEG:]
    return out


if __name__ == "__main__":
    import sys
    if sys.argv[1:] and sys.argv[1] == "build":
        nc, counts = build_program()
        print("build ok", counts, "total", sum(counts.values()))


# version 13
# speedup vs baseline: 1.0989x; 1.0066x over previous
import os
import numpy as np
import ml_dtypes
from contextlib import ExitStack
import concourse.bass as bass
import concourse.mybir as mybir
from concourse.bass_utils import run_bass_kernel_spmd

F32 = mybir.dt.float32
BF16 = mybir.dt.bfloat16
U32 = mybir.dt.uint32
U8 = mybir.dt.uint8
AF = mybir.ActivationFunctionType
ALU = mybir.AluOpType
AX = mybir.AxisListType
bf16 = ml_dtypes.bfloat16

NCORES = 8
D = 4096
T = 1024
INTER = 11008
H = 32
HD = 128
SEG = 512
EPS = 1e-5
ATT_SCALE = 1.0 / np.sqrt(128.0)
NEG = -1e9

PHASES = int(os.environ.get("WK_PHASES", "6"))
DEBUG = os.environ.get("WK_DEBUG", "0") == "1"
AGSHARED = os.environ.get("WK_AGSHARED", "1") == "1"

ENGINES = ("sync", "scalar", "vector", "tensor", "gpsimd")


def mkperm(F, win=1024):
    perm = np.empty(F, dtype=np.int64)
    for base in range(0, F, win):
        m = min(win, F - base)
        kk = m // 8
        j = np.arange(m)
        perm[base + j] = base + 8 * (j % kk) + (j // kk)
    return perm


PERM_H = mkperm(D)
PERM_D = mkperm(INTER)
DN_CHUNKS = [(wp, n, 128 if wp < 10 else 96) for wp in range(11) for n in range(8)]
DN_ROWBASE = {}
_rb = 0
for _i, (_wp, _n, _kk) in enumerate(DN_CHUNKS):
    DN_ROWBASE[_i] = _rb
    _rb += _kk
assert _rb == INTER


class Sem:
    def __init__(self, nc, es, name):
        self.h = es.enter_context(nc.semaphore(name))
        self.n = 0


class Prog:
    def __init__(self, nc, es):
        self.nc = nc
        self.streams = {e: [] for e in ENGINES}
        self.es = es
        self._sems = {}
        self.pguard = []

    def sem(self, name):
        if name not in self._sems:
            self._sems[name] = Sem(self.nc, self.es, name)
        return self._sems[name]

    def emit(self, eng, fn):
        self.streams[eng].append(fn)

    def wait(self, eng, sem, val):
        if val > 0:
            self.emit(eng, lambda e, s=sem.h, v=val: e.wait_ge(s, v))

    def dma(self, eng, out, in_, sem):
        sem.n += 16
        self.emit(eng, lambda e, o=out, i=in_, s=sem.h: e.dma_start(out=o, in_=i).then_inc(s, 16))
        return sem.n

    def op(self, eng, fn, sem=None, k=1):
        if sem is None:
            self.emit(eng, fn)
            return None
        sem.n += k
        self.emit(eng, lambda e, f=fn, s=sem.h, kk=k: f(e).then_inc(s, kk))
        return sem.n

    def set_pguard(self, *entries):
        self.pguard = [(s, v) for (s, v) in entries if v]

    def guard(self, eng):
        for s, v in self.pguard:
            self.wait(eng, s, v)

    def barrier(self):
        s = self.sem("bar")
        own = {"vector": "sv", "scalar": "sx", "tensor": "sp", "gpsimd": "sg"}
        for e in ENGINES:
            if e in own and own[e] in self._sems:
                osem = self._sems[own[e]]
                self.wait(e, osem, osem.n)
            s.n += 1
            self.emit(e, lambda en, h=s.h: en.nop().then_inc(h, 1))
        tgt = s.n
        for e in ENGINES:
            self.wait(e, s, tgt)

    def finalize(self, block):
        decs = {"sync": block.sync, "scalar": block.scalar, "vector": block.vector,
                "tensor": block.tensor, "gpsimd": block.gpsimd}
        for name, dec in decs.items():
            ops = list(self.streams[name])
            def make(ops_):
                def _(eng):
                    for f in ops_:
                        f(eng)
                return _
            dec(make(ops))


class Ring:

    def __init__(self, P, name, n):
        self.P = P
        self.n = n
        self.sems = [P.sem(f"{name}_{i}") for i in range(n)]
        self.cons = [None] * n
        self.val = [0] * n

    def load(self, eng, i, pairs):
        s = i % self.n
        if self.cons[s] is not None:
            for c_ in self.cons[s]:
                self.P.wait(eng, *c_)
            self.cons[s] = None
        for dst, src in pairs:
            self.P.dma(eng, dst, src, self.sems[s])
        self.val[s] = self.sems[s].n
        return (self.sems[s], self.val[s])

    def ready(self, eng, i):
        s = i % self.n
        self.P.wait(eng, self.sems[s], self.val[s])

    def consumed(self, i, sem, val):
        self.cons[i % self.n] = [(sem, val)]

    def consumed_multi(self, i, entries):
        self.cons[i % self.n] = [e_ for e_ in entries if e_[1]]

    def guard_write(self, eng, i):
        s = i % self.n
        if self.val[s]:
            self.P.wait(eng, self.sems[s], self.val[s])

    def totals(self):
        return [(s, s.n) for s in self.sems]


def build_program():
    nc = bass.Bass("TRN2", target_bir_lowering=False)
    ap = lambda t: t.ap()

    XP = ap(nc.dram_tensor("XP", [T, D], F32, kind="ExternalInput"))
    XNAT = ap(nc.dram_tensor("XNAT", [T, D], F32, kind="ExternalInput"))
    WQ = ap(nc.dram_tensor("WQ", [D, D], F32, kind="ExternalInput"))
    WK = ap(nc.dram_tensor("WK", [D, D], F32, kind="ExternalInput"))
    WV = ap(nc.dram_tensor("WV", [D, D], F32, kind="ExternalInput"))
    WO = ap(nc.dram_tensor("WO", [D, D], F32, kind="ExternalInput"))
    QWGU = ap(nc.dram_tensor("QWGU", [512, 2 * INTER], U32, kind="ExternalInput"))
    SCGU = ap(nc.dram_tensor("SCGU", [32, 2 * INTER], F32, kind="ExternalInput"))
    QZGU = ap(nc.dram_tensor("QZGU", [32, 2752], U32, kind="ExternalInput"))
    QWDN = ap(nc.dram_tensor("QWDN", [1376, D], U32, kind="ExternalInput"))
    SCDN = ap(nc.dram_tensor("SCDN", [86, D], F32, kind="ExternalInput"))
    QZDN = ap(nc.dram_tensor("QZDN", [86, 512], U32, kind="ExternalInput"))
    SCEXP = ap(nc.dram_tensor("SCEXP", [128, 88 * 1024], F32, kind="ExternalInput"))
    SCDEXP = ap(nc.dram_tensor("SCDEXP", [128, 11 * D], F32, kind="ExternalInput"))
    ROPEC = ap(nc.dram_tensor("ROPEC", [128, T], F32, kind="ExternalInput"))
    ROPES = ap(nc.dram_tensor("ROPES", [128, T], F32, kind="ExternalInput"))
    PERMM = ap(nc.dram_tensor("PERMM", [128, 128], F32, kind="ExternalInput"))
    IDTI = ap(nc.dram_tensor("IDTI", [128, 128], F32, kind="ExternalInput"))
    ENEG = ap(nc.dram_tensor("ENEG", [128, 128], F32, kind="ExternalInput"))
    ENEGD = ap(nc.dram_tensor("ENEGD", [128, 11 * 86], F32, kind="ExternalInput"))
    ONESC = ap(nc.dram_tensor("ONESC", [128, 1], F32, kind="ExternalInput"))
    ONESR = ap(nc.dram_tensor("ONESR", [1, 128], F32, kind="ExternalInput"))
    MASKS = ap(nc.dram_tensor("MASKS", [128, 4 * SEG], F32, kind="ExternalInput"))
    BIAST = ap(nc.dram_tensor("BIAST", [128, 33], F32, kind="ExternalInput"))

    OUT = ap(nc.dram_tensor("OUT", [T, D], F32, kind="ExternalOutput"))

    agK_in = ap(nc.dram_tensor("agK_in", [D, T], BF16))
    agV_in = ap(nc.dram_tensor("agV_in", [T, D], BF16))
    agK_out = ap(nc.dram_tensor("agK_out", [NCORES * D, T], BF16, addr_space="Shared" if AGSHARED else "Local"))
    agV_out = ap(nc.dram_tensor("agV_out", [NCORES * T, D], BF16, addr_space="Shared" if AGSHARED else "Local"))
    QSP = ap(nc.dram_tensor("QSP", [D, T], BF16))
    SCGUB = ap(nc.dram_tensor("SCGUB", [32, 2 * INTER], BF16))
    SCDNB = ap(nc.dram_tensor("SCDNB", [86, D], BF16))
    HTS = ap(nc.dram_tensor("HTS", [INTER, T], BF16))
    GSP = ap(nc.dram_tensor("GSP", [T, D], BF16))

    dbg = {}
    if DEBUG:
        def dbgt(name, shape, dt):
            dbg[name] = ap(nc.dram_tensor("DBG_" + name, shape, dt, kind="ExternalOutput"))
        dbgt("XNT", [D, T], BF16)
        dbgt("SG", [32, T], BF16)
        if PHASES >= 2:
            dbgt("QSP", [D, T], BF16)
            dbgt("KIN", [D, T], BF16)
            dbgt("VIN", [T, D], BF16)
        if PHASES >= 3:
            dbgt("HTS", [INTER, T], BF16)
            dbgt("SH", [86, T], BF16)
        if PHASES >= 4:
            dbgt("GSP", [T, D], BF16)
        if PHASES >= 5:
            dbgt("SPRE", [D, T], BF16)

    es = ExitStack()
    with es:
        P = Prog(nc, es)
        sb = lambda name, shape, dt: es.enter_context(nc.sbuf_tensor(name, shape, dt))

        c_idt = sb("c_idt", [128, 128], BF16)
        c_perm = sb("c_perm", [128, 128], BF16)
        c_eneg = sb("c_eneg", [128, 128], BF16)
        c_enegd = sb("c_enegd", [128, 11 * 86], BF16)
        c_ones = sb("c_ones", [128, 1], BF16)
        c_onesr = sb("c_onesr", [1, 128], F32)
        c_masks = sb("c_masks", [128, 4 * SEG], BF16)
        c_bias = sb("c_bias", [128, 33], F32)
        c_ropec = sb("c_ropec", [128, T], BF16)
        c_ropes = sb("c_ropes", [128, T], BF16)
        sgneg = sb("sgneg", [32, T], BF16)
        shneg = sb("shneg", [86, T], BF16)

        sv = P.sem("sv")
        sx = P.sem("sx")
        sp = P.sem("sp")
        sg = P.sem("sg")
        cc = P.sem("cc")
        qdbg = P.sem("qdbg")

        qc_g = P.sem("qc_g")
        qc_s = P.sem("qc_s")
        for dst, src in [(c_idt, IDTI), (c_perm, PERMM), (c_eneg, ENEG), (c_enegd, ENEGD),
                         (c_ones, ONESC), (c_masks, MASKS), (c_ropec, ROPEC), (c_ropes, ROPES)]:
            P.dma("gpsimd", dst[:], src, qc_g)
        P.dma("sync", c_onesr[:], ONESR, qc_s)
        P.dma("sync", c_bias[:], BIAST, qc_s)
        pc_sems = {}
        pc_chunk = {}

        def precast_cols(nm_, dst_, src_, nch):
            rows = dst_.shape[0]
            step = max(1, rows // nch)
            vals = []
            for k_ in range(nch):
                s_ = P.sem(f"qpc_{nm_}_{k_}")
                P.dma("gpsimd", dst_[k_ * step:(k_ + 1) * step, :],
                      src_[k_ * step:(k_ + 1) * step, :], s_)
                vals.append((s_, s_.n))
            pc_chunk[nm_] = vals
            pc_sems[nm_] = vals[-1]

        def wait_pc(eng, nm_):
            for c_ in pc_chunk[nm_]:
                P.wait(eng, *c_)

        precast_cols("scgub", SCGUB, SCGU, 1)
        precast_cols("scdnb", SCDNB, SCDN, 1)
        QCG = qc_g.n
        QCS = qc_s.n
        for e_ in ENGINES:
            P.wait(e_, qc_g, QCG)
            P.wait(e_, qc_s, QCS)

        scopeA = ExitStack()
        sbA = lambda name, shape, dt: scopeA.enter_context(nc.sbuf_tensor(name, shape, dt))

        xnt = sbA("xnt", [128, 32 * 1024], BF16)
        xnt_v = xnt[:].rearrange("p (j c) -> p j c", c=1024)

        s1 = ExitStack()
        sb1 = lambda name, shape, dt: s1.enter_context(nc.sbuf_tensor(name, shape, dt))
        ps1 = lambda name, shape, dt=F32: s1.enter_context(nc.psum_tensor(name, shape, dt))
        xp = [sb1(f"xp{i}", [128, D], F32) for i in range(2)]
        xsq = [sb1(f"xsq{i}", [128, D], F32) for i in range(2)]
        xnb = [sb1(f"xnb{i}", [128, D], BF16) for i in range(2)]
        st1 = sb1("st1", [128, 8], F32)
        ptr_ = [ps1(f"ptr{i}", [128, 1024], BF16) for i in range(2)]
        psg = ps1("psg", [32, 1024], F32)

        r_xp = Ring(P, "r_xp", 2)
        done_tr = {}
        evac_tr = {}
        sq_cons = {}

        for t in range(8):
            r_xp.load("sync", t, [(xp[t % 2][:], XP[128 * t:128 * (t + 1), :])])
            r_xp.ready("scalar", t)
            if t >= 2:
                P.wait("scalar", sv, sq_cons[t - 2])
            sq_done = P.op("scalar", lambda e, a=xsq[t % 2], b=xp[t % 2]:
                           e.activation(out=a[:], in_=b[:], func=AF.Square), sem=sx)
            r_xp.ready("vector", t)
            m1 = P.op("vector", lambda e, o=st1, i=xp[t % 2]:
                      e.reduce_sum(out=o[:, 0:1], in_=i[:], axis=AX.X), sem=sv)
            P.wait("vector", sx, sq_done)
            m2 = P.op("vector", lambda e, o=st1, i=xsq[t % 2]:
                      e.reduce_sum(out=o[:, 1:2], in_=i[:], axis=AX.X), sem=sv)
            sq_cons[t] = m2
            P.wait("vector", sv, m2)
            m3 = P.op("vector", lambda e, o=st1: e.tensor_scalar(
                out=o[:, 2:3], in0=o[:, 0:1], scalar1=1.0 / D, scalar2=None, op0=ALU.mult), sem=sv)
            P.wait("vector", sv, m3)
            m4 = P.op("vector", lambda e, o=st1: e.tensor_tensor(
                out=o[:, 3:4], in0=o[:, 2:3], in1=o[:, 2:3], op=ALU.mult), sem=sv)
            P.wait("vector", sv, m4)
            m5 = P.op("vector", lambda e, o=st1: e.tensor_scalar(
                out=o[:, 3:4], in0=o[:, 3:4], scalar1=-1.0, scalar2=EPS,
                op0=ALU.mult, op1=ALU.add), sem=sv)
            P.wait("scalar", sv, m5)
            m6 = P.op("scalar", lambda e, o=st1: e.activation(
                out=o[:, 4:5], in_=o[:, 1:2], func=AF.Sqrt, bias=o[:, 3:4], scale=1.0 / D), sem=sx)
            P.wait("vector", sx, m6)
            m7 = P.op("vector", lambda e, o=st1: e.reciprocal(out=o[:, 5:6], in_=o[:, 4:5]), sem=sv)
            P.wait("vector", sv, m7)
            m8 = P.op("vector", lambda e, o=st1: e.tensor_scalar(
                out=o[:, 6:7], in0=o[:, 2:3], scalar1=o[:, 5:6], scalar2=-1.0,
                op0=ALU.mult, op1=ALU.mult), sem=sv)
            P.wait("vector", sv, m8)
            if t >= 2:
                P.wait("vector", sp, done_tr[(t - 2, 3)])
            m9 = P.op("vector", lambda e, o=xnb[t % 2], i=xp[t % 2], s=st1: e.tensor_scalar(
                out=o[:], in0=i[:], scalar1=s[:, 5:6], scalar2=None, op0=ALU.mult), sem=sv)
            P.wait("vector", sv, m9)
            m10 = P.op("vector", lambda e, o=xnb[t % 2], s=st1: e.tensor_scalar(
                out=o[:], in0=o[:], scalar1=s[:, 6:7], scalar2=None, op0=ALU.add), sem=sv)
            r_xp.consumed_multi(t, [(sv, m10), (sx, sq_done)])

            P.wait("tensor", sv, m10)
            if t == 0:
                P.wait("tensor", qc_g, QCG)
            for j8 in range(4):
                key = (t, j8)
                prevkey = (t, j8 - 2) if j8 >= 2 else ((t - 1, j8 + 2) if t >= 1 else None)
                if prevkey is not None and prevkey in evac_tr:
                    P.wait("tensor", sv, evac_tr[prevkey])
                for i in range(8):
                    j = 8 * j8 + i
                    done_tr[key] = P.op("tensor", lambda e, pt=ptr_[j8 % 2], ii=i, xb=xnb[t % 2], jj=j:
                                        e.transpose(out=pt[:, 128 * ii:128 * (ii + 1)],
                                                    in_=xb[:, 128 * jj:128 * (jj + 1)],
                                                    identity=c_idt[:]), sem=sp)
                P.wait("vector", sp, done_tr[key])
                evac_tr[key] = P.op("vector", lambda e, pt=ptr_[j8 % 2], jb=j8, tt=t:
                                    e.tensor_copy(
                                        out=xnt_v[:, 8 * jb:8 * (jb + 1), 128 * tt:128 * (tt + 1)],
                                        in_=pt[:].rearrange("p (i c) -> p i c", c=128)), sem=sv)

        xnt_ready = evac_tr[(7, 3)]
        P.wait("tensor", sv, xnt_ready)
        last_mm = None
        for w in range(4):
            for n in range(8):
                j = 8 * w + n
                for hf in range(2):
                    last_mm = P.op("tensor", lambda e, jj=j, h=hf, ww=w,
                                   st=(w == 0 and n == 0), sp_=(w == 3 and n == 7):
                                   e.matmul(out=psg[0:32, 512 * h:512 * (h + 1)],
                                            lhsT=c_eneg[:, 32 * ww:32 * (ww + 1)],
                                            rhs=xnt[:, 1024 * jj + 512 * h:1024 * jj + 512 * (h + 1)],
                                            start=st, stop=sp_), sem=sp)
        P.wait("vector", sp, last_mm)
        sg_evac = P.op("vector", lambda e: e.tensor_copy(
            out=sgneg[0:32, :], in_=psg[0:32, :]), sem=sv)

        if DEBUG:
            P.wait("sync", sv, sg_evac)
            P.dma("sync", dbg["XNT"].rearrange("(j p) c -> p j c", p=128),
                  xnt[:].rearrange("p (j c) -> p j c", c=1024), qdbg)
            P.dma("sync", dbg["SG"], sgneg[:], qdbg)

        s1.close()
        P.set_pguard((sv, sg_evac))
        P.barrier()

        if PHASES >= 2:
            s2 = ExitStack()
            sb2 = lambda name, shape, dt: s2.enter_context(nc.sbuf_tensor(name, shape, dt))
            s2p = ExitStack()
            ps2 = lambda name, shape, dt=F32: s2p.enter_context(nc.psum_tensor(name, shape, dt))
            wst = [sb2(f"wst{i}", [128, 32 * 128], BF16) for i in range(3)]
            wst32 = [sb2(f"wst32{i}", [128, 32 * 128], F32) for i in range(3)]
            rx = [sb2(f"rx{i}", [128, 512], BF16) for i in range(4)]
            t1b = [sb2(f"t1b{i}", [128, 512], F32) for i in range(4)]
            t2b = [sb2(f"t2b{i}", [128, 512], F32) for i in range(4)]
            qkst = [sb2(f"qkst{i}", [128, 512], BF16) for i in range(4)]
            psq = [ps2(f"psq{i}", [128, 512]) for i in range(4)]
            psw = [ps2(f"psw{i}", [128, 512]) for i in range(2)]

            r_wst = Ring(P, "r_wst", 3)
            r_qko = Ring(P, "r_qko", 4)
            rx_done = {}
            pend_store = []
            t1_done = {}
            t2_done = {}
            add_done = {}
            swap_done = {}
            idx = 0
            for (Wsrc, dstT, scale) in ((WQ, QSP, ATT_SCALE), (WK, agK_in, 1.0)):
                for oc in range(32):
                    gi = idx * 32 + oc
                    r_wst.load("sync", gi,
                               [(wst32[gi % 3][:].rearrange("p (j c) -> p j c", c=128),
                                 Wsrc[:, 128 * oc:128 * (oc + 1)].rearrange("(j p) c -> p j c", p=128))])
                    r_wst.ready("scalar", gi)
                    if gi >= 3:
                        P.wait("scalar", sp, wst_used[gi - 3])
                    wstc = P.op("scalar", lambda e, b3=gi % 3: e.copy(
                        out=wst[b3][:], in_=wst32[b3][:]), sem=sx)
                    r_wst.consumed(gi, sx, wstc)
                    if gi == 0:
                        wst_used = {}
                    for hf in range(2):
                        ii = 2 * gi + hf
                        if ii == 0:
                            P.guard("tensor")
                        P.wait("tensor", sx, wstc)
                        if ii >= 4:
                            P.wait("tensor", sx, rx_done[ii - 4])
                        mm_last = None
                        for j in range(32):
                            mm_last = P.op("tensor", lambda e, jj=j, h=hf, b=gi % 3, pq=psq[ii % 4]:
                                           e.matmul(out=pq[:],
                                                    lhsT=wst[b][:, 128 * jj:128 * (jj + 1)],
                                                    rhs=xnt[:, 1024 * jj + 512 * h:1024 * jj + 512 * (h + 1)],
                                                    start=(jj == 0), stop=(jj == 31)), sem=sp)
                        if hf == 1:
                            wst_used[gi] = mm_last
                        P.wait("scalar", sp, mm_last)
                        if ii >= 4:
                            P.wait("scalar", sp, swap_done[ii - 4])
                            P.wait("scalar", sg, t2_done[ii - 4])
                        rd = P.op("scalar", lambda e, b=ii % 4, pq=psq[ii % 4], s=scale:
                                  e.mul(out=rx[b][:], in_=pq[:], mul=s), sem=sx)
                        rx_done[ii] = rd
                        P.wait("tensor", sx, rd)
                        if ii >= 2:
                            P.wait("tensor", sv, t1_done[ii - 2])
                        sw = P.op("tensor", lambda e, b=ii % 4, w2=ii % 2:
                                  e.matmul(out=psw[w2][:], lhsT=c_perm[:], rhs=rx[b][:],
                                           start=True, stop=True), sem=sp)
                        swap_done[ii] = sw
                        P.wait("vector", sp, sw)
                        if ii >= 4:
                            P.wait("vector", sv, add_done[ii - 4])
                        td1 = P.op("vector", lambda e, b=ii % 4, h=hf, w2=ii % 2:
                                   e.tensor_tensor(out=t1b[b][:], in0=psw[w2][:],
                                                   in1=c_ropes[:, 512 * h:512 * (h + 1)],
                                                   op=ALU.mult), sem=sv)
                        t1_done[ii] = td1
                        P.wait("gpsimd", sx, rd)
                        if ii == 0:
                            P.wait("gpsimd", qc_g, QCG)
                        if ii >= 4:
                            P.wait("gpsimd", sv, add_done[ii - 4])
                        td2 = P.op("gpsimd", lambda e, b=ii % 4, h=hf:
                                   e.tensor_tensor(out=t2b[b][:], in0=rx[b][:],
                                                   in1=c_ropec[:, 512 * h:512 * (h + 1)],
                                                   op=ALU.mult), sem=sg)
                        t2_done[ii] = td2
                        P.wait("vector", sg, td2)
                        P.wait("vector", sv, td1)
                        r_qko.guard_write("vector", ii)
                        ad = P.op("vector", lambda e, b=ii % 4:
                                  e.tensor_tensor(out=qkst[b][:], in0=t1b[b][:], in1=t2b[b][:],
                                                  op=ALU.add), sem=sv)
                        add_done[ii] = ad
                        pend_store.append((ii, dstT, oc, hf))
                        if len(pend_store) > 2:
                            jj, dT, oc2, hf2 = pend_store.pop(0)
                            P.wait("scalar", sv, add_done[jj])
                            r_qko.load("scalar", jj,
                                       [(dT[128 * oc2:128 * (oc2 + 1), 512 * hf2:512 * (hf2 + 1)],
                                         qkst[jj % 4][:])])
                idx += 1
            while pend_store:
                jj, dT, oc2, hf2 = pend_store.pop(0)
                P.wait("scalar", sv, add_done[jj])
                r_qko.load("scalar", jj,
                           [(dT[128 * oc2:128 * (oc2 + 1), 512 * hf2:512 * (hf2 + 1)],
                             qkst[jj % 4][:])])
            s2p.close()
            for s_, v_ in r_qko.totals():
                P.wait("gpsimd", s_, v_)
            P.op("gpsimd", lambda e: e.collective_compute(
                "AllGather", ALU.bypass, replica_groups=[list(range(NCORES))],
                ins=[agK_in.opt()], outs=[agK_out.opt()]), sem=cc)
            P.set_pguard((sx, rx_done[127]), (sv, max(t1_done[127], add_done[127])))
            P.barrier()

            s2v = ExitStack()
            ps2v = lambda name, shape: s2v.enter_context(nc.psum_tensor(name, shape, F32))
            psv = [ps2v(f"psv{i}", [128, 512]) for i in range(8)]
            vw = [sb2(f"vw{i}", [128, 512], BF16) for i in range(4)]
            vw32 = [sb2(f"vw32{i}", [128, 512], F32) for i in range(3)]
            vst = [sb2(f"vst{i}", [128, 512], BF16) for i in range(2)]
            r_vw = Ring(P, "r_vw", 3)
            r_vo = Ring(P, "r_vo", 2)
            v_evac = {}
            for fc in range(8):
                for j in range(32):
                    gi = fc * 32 + j
                    r_vw.load("sync", gi,
                              [(vw32[gi % 3][:], WV[128 * j:128 * (j + 1), 512 * fc:512 * (fc + 1)])])
                    r_vw.ready("scalar", gi)
                    if gi >= 4:
                        P.wait("scalar", sp, vw_used[gi - 4])
                    vwc = P.op("scalar", lambda e, b4=gi % 4, b3=gi % 3: e.copy(
                        out=vw[b4][:], in_=vw32[b3][:]), sem=sx)
                    r_vw.consumed(gi, sx, vwc)
                    if gi == 0:
                        vw_used = {}
                    P.wait("tensor", sx, vwc)
                    if gi == 0:
                        P.guard("tensor")
                    if j == 0 and fc > 0:
                        P.wait("tensor", sx, v_evac[(fc - 1, 7)])
                    last = None
                    for t in range(8):
                        last = P.op("tensor", lambda e, jj=j, tt=t, b=gi % 4:
                                    e.matmul(out=psv[tt][:],
                                             lhsT=xnt[:, 1024 * jj + 128 * tt:1024 * jj + 128 * (tt + 1)],
                                             rhs=vw[b][:],
                                             start=(jj == 0), stop=(jj == 31)), sem=sp)
                    vw_used[gi] = last
                v_fill = last
                for t in range(8):
                    gi2 = fc * 8 + t
                    P.wait("scalar", sp, v_fill)
                    r_vo.guard_write("scalar", gi2)
                    ev = P.op("scalar", lambda e, tt=t: e.copy(out=vst[tt % 2][:], in_=psv[tt][:]), sem=sx)
                    v_evac[(fc, t)] = ev
                    P.wait("scalar", sx, ev)
                    r_vo.load("scalar", gi2,
                              [(agV_in[128 * t:128 * (t + 1), 512 * fc:512 * (fc + 1)], vst[t % 2][:])])
            s2v.close()

            for s_, v_ in r_vo.totals():
                P.wait("gpsimd", s_, v_)
            P.op("gpsimd", lambda e: e.collective_compute(
                "AllGather", ALU.bypass, replica_groups=[list(range(NCORES))],
                ins=[agV_in.opt()], outs=[agV_out.opt()]), sem=cc)

            if DEBUG:
                for s_, v_ in r_qko.totals() + r_vo.totals():
                    P.wait("sync", s_, v_)
                P.dma("sync", dbg["QSP"], QSP, qdbg)
                P.dma("sync", dbg["KIN"], agK_in, qdbg)
                P.dma("sync", dbg["VIN"], agV_in, qdbg)
            s2.close()
            P.set_pguard((sx, v_evac[(7, 7)]))
            P.barrier()

        if PHASES >= 3:
            s3 = ExitStack()
            sb3 = lambda name, shape, dt: s3.enter_context(nc.sbuf_tensor(name, shape, dt))
            ps3 = lambda name, shape, dt=F32: s3.enter_context(nc.psum_tensor(name, shape, dt))
            qst = [sb3(f"qst{i}", [128, 1024], U32) for i in range(2)]
            lowb = [sb3(f"lowb{i}", [128, 1024], U32) for i in range(2)]
            highb = [sb3(f"highb{i}", [128, 1024], U32) for i in range(2)]
            scb = [sb3(f"scb{i}", [128, 1024], BF16) for i in range(2)]
            scb32 = [sb3(f"scb32{i}", [128, 1024], F32) for i in range(2)]
            w1t = [sb3(f"w1t{i}", [128, 32 * 256], BF16) for i in range(3)]
            zqst = [sb3(f"zqst{i}", [32, 256], U32) for i in range(2)]
            sczb = [sb3(f"sczb{i}", [32, 256], BF16) for i in range(2)]
            zu32 = sb3("zu32", [32, 256], U32)
            zf32 = sb3("zf32", [32, 256], F32)
            zbuf = [sb3(f"zbuf{i}", [32, 256], BF16) for i in range(2)]
            hgate = [sb3(f"hgate{i}", [128, 512], BF16) for i in range(2)]
            hchunk = [sb3(f"hchunk{i}", [128, 1024], BF16) for i in range(2)]
            pg = [ps3(f"pg{i}", [128, 512]) for i in range(2)]
            pu = [ps3(f"pu{i}", [128, 512]) for i in range(2)]
            psh = ps3("psh", [86, 1024])

            r_qw = Ring(P, "r_qw", 2)
            r_sc = Ring(P, "r_sc", 2)
            r_zq = Ring(P, "r_zq", 2)
            r_scz = Ring(P, "r_scz", 2)
            r_hsp = Ring(P, "r_hsp", 2)
            mult_done = {}
            h_done = {}
            gu_fill = {}
            hg_done = {}
            bit_done = {}
            sh_mm_done = {}

            for c_ in pc_chunk["scgub"]:
                P.wait("scalar", *c_)
            z_done = {}
            scb_cast = {}

            def stageA(ci):
                wp, n, kk = DN_CHUNKS[ci]
                cb = 1024 * wp + kk * n
                b = ci % 2
                r_qw.load("sync", ci, [
                    (qst[b][:].rearrange("p (w g c) -> p w g c", w=4, g=2)[:, :, 0, 0:kk],
                     QWGU[:, cb:cb + kk].rearrange("(w p) c -> p w c", p=128)),
                    (qst[b][:].rearrange("p (w g c) -> p w g c", w=4, g=2)[:, :, 1, 0:kk],
                     QWGU[:, INTER + cb:INTER + cb + kk].rearrange("(w p) c -> p w c", p=128)),
                ])
                r_sc.load("sync", ci, [(scb32[b][:], SCEXP[:, 1024 * ci:1024 * (ci + 1)])])
                r_sc.ready("scalar", ci)
                if ci >= 2:
                    P.wait("scalar", sv, mult_done[ci - 2][0])
                    P.wait("scalar", sg, mult_done[ci - 2][1])
                scbc = P.op("scalar", lambda e, bb=b: e.copy(
                    out=scb[bb][:], in_=scb32[bb][:]), sem=sx)
                r_sc.consumed(ci, sx, scbc)
                scb_cast[ci] = scbc
                r_zq.load("sync", ci, [
                    (zqst[b][0:32, 0:kk], QZGU[:, 128 * wp:128 * wp + kk]),
                    (zqst[b][0:32, 128:128 + kk], QZGU[:, 1376 + 128 * wp:1376 + 128 * wp + kk]),
                ])
                r_scz.load("scalar", ci, [
                    (sczb[b][0:32, 0:kk], SCGUB[:, cb:cb + kk]),
                    (sczb[b][0:32, 128:128 + kk], SCGUB[:, INTER + cb:INTER + cb + kk]),
                ])

                r_qw.ready("vector", ci)
                if ci >= 2:
                    P.wait("vector", sv, mult_done[ci - 2][0])
                    P.wait("vector", sg, mult_done[ci - 2][1])
                P.op("vector", lambda e, bb=b: e.tensor_scalar(
                    out=lowb[bb][:], in0=qst[bb][:], scalar1=0x0F0F0F0F, scalar2=None,
                    op0=ALU.bitwise_and), sem=sv)
                bd = P.op("vector", lambda e, bb=b: e.tensor_scalar(
                    out=highb[bb][:], in0=qst[bb][:], scalar1=4, scalar2=0x0F0F0F0F,
                    op0=ALU.logical_shift_right, op1=ALU.bitwise_and), sem=sv)
                bit_done[ci] = bd
                r_qw.consumed(ci, sv, bd)

                r_zq.ready("vector", ci)
                r_scz.ready("vector", ci)
                if ci >= 2:
                    P.wait("vector", sp, gu_fill[(ci - 2, 1, 1)])
                z1 = P.op("vector", lambda e, bb=b, nn=n: e.tensor_scalar(
                    out=zu32[0:32, :], in0=zqst[bb][:], scalar1=4 * nn, scalar2=15,
                    op0=ALU.logical_shift_right, op1=ALU.bitwise_and), sem=sv)
                r_zq.consumed(ci, sv, z1)
                P.wait("vector", sv, z1)
                z2 = P.op("vector", lambda e: e.tensor_scalar(
                    out=zf32[0:32, :], in0=zu32[0:32, :], scalar1=1.0, scalar2=None,
                    op0=ALU.add), sem=sv)
                P.wait("vector", sv, z2)
                z3 = P.op("vector", lambda e, bb=b: e.tensor_tensor(
                    out=zbuf[bb][0:32, :], in0=zf32[0:32, :], in1=sczb[bb][0:32, :],
                    op=ALU.mult), sem=sv)
                z_done[ci] = z3
                r_scz.consumed(ci, sv, z3)

                P.wait("vector", sv, bd)
                P.wait("vector", sx, scb_cast[ci])
                P.wait("gpsimd", sv, bd)
                P.wait("gpsimd", sx, scb_cast[ci])
                if ci >= 3:
                    P.wait("vector", sp, gu_fill[(ci - 3, 1, 1)])
                    P.wait("gpsimd", sp, gu_fill[(ci - 3, 1, 1)])
                mv = mg = 0
                for j in range(32):
                    w, nib = j // 8, j % 8
                    plane = (highb if (nib % 2) else lowb)[b]
                    byte_off = 1024 * w + (nib // 2)
                    eng = "vector" if (j % 2 == 0) else "gpsimd"
                    msem = sv if eng == "vector" else sg
                    val = P.op(eng, lambda e, pl=plane, bo=byte_off, ww=w, bb=b, w3=ci % 3, jj=j: e.tensor_tensor(
                        out=w1t[w3][:, 256 * jj:256 * (jj + 1)],
                        in0=pl[:].bitcast(U8)[:, bo:(bo - bo % 1024) + 1024:4],
                        in1=scb[bb][:, 256 * ww:256 * (ww + 1)],
                        op=ALU.mult), sem=msem)
                    if eng == "vector":
                        mv = val
                    else:
                        mg = val
                mult_done[ci] = (mv, mg)

            def emit_sh_tail(cj):
                wpj, nj, kkj = DN_CHUNKS[cj]
                bj = cj % 2
                P.wait("tensor", sv, h_done[(cj, 1)])
                for hf2 in range(2):
                    sh_mm_done[cj] = P.op("tensor", lambda e, h=hf2, bb=bj, k2=kkj, w2=wpj,
                                          st=(cj == 0), sp_=(cj == 87):
                                          e.matmul(out=psh[0:86, 512 * h:512 * (h + 1)],
                                                   lhsT=c_enegd[0:k2, 86 * w2:86 * (w2 + 1)],
                                                   rhs=hchunk[bb][0:k2, 512 * h:512 * (h + 1)],
                                                   start=st, stop=sp_), sem=sp)
                P.wait("scalar", sv, h_done[(cj, 1)])
                P.wait("scalar", sp, sh_mm_done[cj])
                rbj = DN_ROWBASE[cj]
                r_hsp.load("scalar", cj, [(HTS[rbj:rbj + kkj, :], hchunk[bj][0:kkj, :])])

            def stageB(ci):
                wp, n, kk = DN_CHUNKS[ci]
                b = ci % 2
                if ci == 0:
                    P.guard("tensor")
                P.wait("tensor", sv, mult_done[ci][0])
                P.wait("tensor", sg, mult_done[ci][1])
                P.wait("tensor", sv, z_done[ci])
                for hf in range(2):
                    if ci >= 1:
                        P.wait("tensor", sx, hg_done[(ci - 1, hf)])
                        P.wait("tensor", sv, h_done[(ci - 1, hf)])
                    for gu in range(2):
                        tgt = (pg if gu == 0 else pu)[hf]
                        for j in range(32):
                            P.op("tensor", lambda e, jj=j, g=gu, h=hf, w3=ci % 3, tg=tgt, k2=kk:
                                 e.matmul(out=tg[0:k2, :],
                                          lhsT=w1t[w3][:, 256 * jj + 128 * g:256 * jj + 128 * g + k2],
                                          rhs=xnt[:, 1024 * jj + 512 * h:1024 * jj + 512 * (h + 1)],
                                          start=(jj == 0), stop=False), sem=None)
                        gu_fill[(ci, hf, gu)] = P.op("tensor", lambda e, g=gu, h=hf, bb=b, tg=tgt, k2=kk:
                                                     e.matmul(out=tg[0:k2, :],
                                                              lhsT=zbuf[bb][0:32, 128 * g:128 * g + k2],
                                                              rhs=sgneg[:, 512 * h:512 * (h + 1)],
                                                              start=False, stop=True), sem=sp)
                    P.wait("scalar", sp, gu_fill[(ci, hf, 0)])
                    hg = P.op("scalar", lambda e, h=hf, k2=kk: e.activation(
                        out=hgate[h][0:k2, :], in_=pg[h][0:k2, :], func=AF.Silu), sem=sx)
                    hg_done[(ci, hf)] = hg
                    P.wait("vector", sp, gu_fill[(ci, hf, 1)])
                    P.wait("vector", sx, hg)
                    if hf == 0:
                        r_hsp.guard_write("vector", ci)
                    if ci >= 2:
                        P.wait("vector", sp, sh_mm_done.get(ci - 2, 0))
                    hd = P.op("vector", lambda e, h=hf, bb=b, k2=kk: e.tensor_tensor(
                        out=hchunk[bb][0:k2, 512 * h:512 * (h + 1)],
                        in0=pu[h][0:k2, :], in1=hgate[h][0:k2, :], op=ALU.mult), sem=sv)
                    h_done[(ci, hf)] = hd
                if ci >= 1:
                    emit_sh_tail(ci - 1)

            stageA(0)
            for ci in range(1, 88):
                stageA(ci)
                stageB(ci - 1)
            stageB(87)
            emit_sh_tail(87)
            P.wait("vector", sp, sh_mm_done[87])
            sh_evac = P.op("vector", lambda e: e.tensor_copy(
                out=shneg[0:86, :], in_=psh[0:86, :]), sem=sv)

            if DEBUG:
                for s_, v_ in r_hsp.totals():
                    P.wait("sync", s_, v_)
                P.wait("sync", sv, sh_evac)
                P.dma("sync", dbg["HTS"], HTS, qdbg)
                P.dma("sync", dbg["SH"], shneg[:], qdbg)
            s3.close()
            for s_, v_ in r_hsp.totals():
                P.wait("sync", s_, v_)
            P.wait("sync", qdbg, qdbg.n)
            P.set_pguard((sx, hg_done[(87, 1)]), (sv, max(h_done[(87, 1)], sh_evac)))
            P.barrier()

        scopeA.close()

        if PHASES >= 4:
            s4 = ExitStack()
            sb4 = lambda name, shape, dt: s4.enter_context(nc.sbuf_tensor(name, shape, dt))
            ps4 = lambda name, shape: s4.enter_context(nc.psum_tensor(name, shape, F32))
            zd = sb4("zd", [86, D], BF16)
            zdq = sb4("zdq", [86, 512], U32)
            zdnu = sb4("zdnu", [86, 512], U32)
            zdn = sb4("zdn", [86, 512], F32)
            scd_full = sb4("scd_full", [86, D], BF16)
            qwd = [sb4(f"qwd{i}", [128, 512], U32) for i in range(2)]
            lowd = [sb4(f"lowd{i}", [128, 512], U32) for i in range(2)]
            highd = [sb4(f"highd{i}", [128, 512], U32) for i in range(2)]
            scdx = [sb4(f"scdx{i}", [128, 512], BF16) for i in range(2)]
            scdx32 = [sb4(f"scdx32{i}", [128, 512], F32) for i in range(2)]
            w1d = [sb4(f"w1d{i}", [128, 512], BF16) for i in range(3)]
            htb = [sb4(f"htb{i}", [128, 1024], BF16) for i in range(3)]
            gst = [sb4(f"gst{i}", [128, 512], BF16) for i in range(2)]
            pd = [ps4(f"pd{i}", [128, 512]) for i in range(8)]

            r_zd = Ring(P, "r_zd", 1)
            r_scd = Ring(P, "r_scd", 1)
            r_qwd = Ring(P, "r_qwd", 2)
            r_sdx = Ring(P, "r_sdx", 2)
            r_ht = Ring(P, "r_ht", 3)
            r_go = Ring(P, "r_go", 2)

            wait_pc("sync", "scdnb")
            r_zd.load("sync", 0, [(zdq[:], QZDN)])
            r_scd.load("sync", 0, [(scd_full[:], SCDNB)])
            r_zd.ready("vector", 0)
            r_scd.ready("vector", 0)
            zprev = 0
            for n2 in range(8):
                P.wait("vector", sv, zprev)
                a1 = P.op("vector", lambda e, nn=n2: e.tensor_scalar(
                    out=zdnu[:], in0=zdq[:], scalar1=4 * nn, scalar2=15,
                    op0=ALU.logical_shift_right, op1=ALU.bitwise_and), sem=sv)
                P.wait("vector", sv, a1)
                a2 = P.op("vector", lambda e: e.tensor_scalar(
                    out=zdn[:], in0=zdnu[:], scalar1=1.0, scalar2=None, op0=ALU.add), sem=sv)
                P.wait("vector", sv, a2)
                zprev = P.op("vector", lambda e, nn=n2: e.tensor_tensor(
                    out=zd[:].rearrange("p (c e) -> p c e", e=8)[:, :, nn],
                    in0=zdn[:],
                    in1=scd_full[:].rearrange("p (c e) -> p c e", e=8)[:, :, nn],
                    op=ALU.mult), sem=sv)
            zd_ready = zprev

            dn_bit_done = {}
            dn_mult_used = {}
            sdx_cast = {}
            g_evac = {}
            w1d_used = {}
            for hc in range(8):
                for ci, (wp, n, kk) in enumerate(DN_CHUNKS):
                    gi = hc * 88 + ci
                    b3 = gi % 3
                    wi = hc * 11 + wp
                    if n == 0:
                        r_qwd.load("sync", wi,
                                   [(qwd[wi % 2][0:kk, :],
                                     QWDN[128 * wp:128 * wp + kk, 512 * hc:512 * (hc + 1)])])
                        r_sdx.load("sync", wi,
                                   [(scdx32[wi % 2][:],
                                     SCDEXP[:, 4096 * wp + 512 * hc:4096 * wp + 512 * (hc + 1)])])
                        r_sdx.ready("scalar", wi)
                        if wi >= 2:
                            P.wait("scalar", sv, dn_mult_used[wi - 2][0])
                            P.wait("scalar", sg, dn_mult_used[wi - 2][1])
                        sdxc = P.op("scalar", lambda e, w2=wi % 2: e.copy(
                            out=scdx[w2][:], in_=scdx32[w2][:]), sem=sx)
                        r_sdx.consumed(wi, sx, sdxc)
                        sdx_cast[wi] = sdxc
                        r_qwd.ready("vector", wi)
                        if wi >= 2:
                            P.wait("vector", sv, dn_mult_used[wi - 2][0])
                            P.wait("vector", sg, dn_mult_used[wi - 2][1])
                        P.op("vector", lambda e, w2=wi % 2, k2=kk: e.tensor_scalar(
                            out=lowd[w2][0:k2, :], in0=qwd[w2][0:k2, :], scalar1=0x0F0F0F0F,
                            scalar2=None, op0=ALU.bitwise_and), sem=sv)
                        bitd = P.op("vector", lambda e, w2=wi % 2, k2=kk: e.tensor_scalar(
                            out=highd[w2][0:k2, :], in0=qwd[w2][0:k2, :], scalar1=4,
                            scalar2=0x0F0F0F0F, op0=ALU.logical_shift_right,
                            op1=ALU.bitwise_and), sem=sv)
                        dn_bit_done[wi] = bitd
                        r_qwd.consumed(wi, sv, bitd)
                        dn_mult_mv = 0
                        dn_mult_mg = 0
                    rb = DN_ROWBASE[ci]
                    r_ht.load("scalar", gi, [(htb[b3][0:kk, :], HTS[rb:rb + kk, :])])
                    eng = "vector" if (ci % 2 == 0) else "gpsimd"
                    msem = sv if eng == "vector" else sg
                    P.wait(eng, sv, dn_bit_done[wi])
                    P.wait(eng, sx, sdx_cast[wi])
                    if gi >= 3:
                        P.wait(eng, sp, w1d_used[gi - 3])
                    plane = (highd if (n % 2) else lowd)[wi % 2]
                    mval = P.op(eng, lambda e, pl=plane, bo=(n // 2), b3_=b3, k2=kk, w2=wi % 2:
                                e.tensor_tensor(out=w1d[b3_][0:k2, :],
                                                in0=pl[0:k2, :].bitcast(U8)[:, bo:2048:4],
                                                in1=scdx[w2][0:k2, :], op=ALU.mult), sem=msem)
                    if eng == "vector":
                        dn_mult_mv = mval
                    else:
                        dn_mult_mg = mval
                    if n == 7:
                        dn_mult_used[wi] = (dn_mult_mv, dn_mult_mg)
                    if gi == 0:
                        P.guard("tensor")
                    P.wait("tensor", msem, mval)
                    r_ht.ready("tensor", gi)
                    if ci == 0 and hc > 0:
                        P.wait("tensor", sx, g_evac[(hc - 1, 7)])
                    lastmm = None
                    for t in range(8):
                        lastmm = P.op("tensor", lambda e, tt=t, b3_=b3, k2=kk, st=(ci == 0):
                                      e.matmul(out=pd[tt][:],
                                               lhsT=htb[b3_][0:k2, 128 * tt:128 * (tt + 1)],
                                               rhs=w1d[b3_][0:k2, :],
                                               start=st, stop=False),
                                      sem=(sp if t == 7 else None))
                    r_ht.consumed(gi, sp, lastmm)
                    w1d_used[gi] = lastmm
                P.wait("tensor", sv, zd_ready)
                zf = None
                for t in range(8):
                    zf = P.op("tensor", lambda e, tt=t, h2=hc: e.matmul(
                        out=pd[tt][:], lhsT=shneg[:, 128 * tt:128 * (tt + 1)],
                        rhs=zd[:, 512 * h2:512 * (h2 + 1)], start=False, stop=True),
                        sem=(sp if t == 7 else None))
                for t in range(8):
                    gi2 = hc * 8 + t
                    P.wait("scalar", sp, zf)
                    r_go.guard_write("scalar", gi2)
                    ev = P.op("scalar", lambda e, tt=t: e.activation(
                        out=gst[tt % 2][:], in_=pd[tt][:], func=AF.Silu), sem=sx)
                    g_evac[(hc, t)] = ev
                    P.wait("scalar", sx, ev)
                    r_go.load("scalar", gi2,
                              [(GSP[128 * t:128 * (t + 1), 512 * hc:512 * (hc + 1)], gst[t % 2][:])])
            if DEBUG:
                for s_, v_ in r_go.totals():
                    P.wait("sync", s_, v_)
                P.dma("sync", dbg["GSP"], GSP, qdbg)
            s4.close()
            for s_, v_ in r_go.totals():
                P.wait("sync", s_, v_)
            P.set_pguard((sx, g_evac[(7, 7)]))
            P.barrier()

        if PHASES >= 5:
            s5 = ExitStack()
            sb5 = lambda name, shape, dt: s5.enter_context(nc.sbuf_tensor(name, shape, dt))
            s5p = ExitStack()
            ps5 = lambda name, shape, dt=F32: s5p.enter_context(nc.psum_tensor(name, shape, dt))
            slab5 = sb5("slab5", [128, 32 * 1024], BF16)
            ktb = [sb5(f"ktb{i}", [128, 512], BF16) for i in range(6)]
            vtb = [sb5(f"vtb{i}", [128, 512], BF16) for i in range(6)]
            pb = [sb5(f"pb{i}", [128, 512], BF16) for i in range(8)]
            linv = [sb5(f"linv{i}", [1, 512], F32) for i in range(2)]
            linvrep = [sb5(f"linvrep{i}", [128, 512], F32) for i in range(2)]
            ps_s = [ps5(f"ps_s{i}", [128, 512]) for i in range(4)]
            ps_o = [ps5(f"ps_o{i}", [128, 512]) for i in range(2)]
            ps_l = [ps5(f"ps_l{i}", [1, 512]) for i in range(1)]
            ps_b = ps5("ps_b", [128, 512])

            r_ql = Ring(P, "r_ql", 1)
            r_kt = Ring(P, "r_kt", 6)
            r_vt = Ring(P, "r_vt", 6)
            r_ql.load("sync", 0, [(slab5[:].rearrange("p (j c) -> p j c", c=1024),
                                   QSP.rearrange("(j p) c -> p j c", p=128))])
            P.wait("sync", cc, 2)
            P.wait("scalar", cc, 2)
            exp_done = {}
            pb_used = {}
            norm_done = {}
            for u in range(2):
                for h in range(H):
                    uh = u * H + h
                    qo = 1024 * h + 512 * u
                    def emit_outsum(ib2):
                        r2, i2 = divmod(ib2, 4)
                        gi2_ = uh * 9 + r2
                        ib_g = uh * 36 + ib2
                        if r2 == 8:
                            use_sem, use_val = sv, mask_done[ib_g]
                        else:
                            use_sem, use_val = sx, exp_done[ib_g]
                        P.wait("tensor", use_sem, use_val)
                        if i2 == 0:
                            r_vt.ready("tensor", gi2_)
                        first = (ib2 == 0)
                        last = (ib2 == 35)
                        if first and uh > 1:
                            P.wait("tensor", sv, norm_done[uh - 2])
                        ub2 = uh % 2
                        P.op("tensor", lambda e, g3=gi2_ % 6, ii=i2, ibb=ib_g, f=first, l=last, u2=ub2:
                             e.matmul(out=ps_o[u2][:], lhsT=vtb[g3][:, 128 * ii:128 * (ii + 1)],
                                      rhs=pb[ibb % 8][:], start=f, stop=l), sem=None)
                        if first and uh >= 1:
                            P.wait("tensor", sv, r1_done[uh - 1])
                        omm = P.op("tensor", lambda e, ibb=ib_g, f=first, l=last:
                                   e.matmul(out=ps_l[0][:], lhsT=c_ones[:],
                                            rhs=pb[ibb % 8][:], start=f, stop=l), sem=sp)
                        pb_used[ib_g] = omm
                        if i2 == 3:
                            r_vt.consumed(gi2_, sp, omm)

                    mask_done = {}
                    for r in range(9):
                        gi = uh * 9 + r
                        if r < 8:
                            ksrc = agK_out[4096 * r + 128 * h:4096 * r + 128 * (h + 1),
                                           512 * u:512 * (u + 1)]
                            vsrc = agV_out[1024 * r + 512 * u:1024 * r + 512 * (u + 1),
                                           128 * h:128 * (h + 1)].rearrange("(i p) d -> p i d", p=128)
                        else:
                            ksrc = agK_in[128 * h:128 * (h + 1), 512 * u:512 * (u + 1)]
                            vsrc = agV_in[512 * u:512 * (u + 1),
                                          128 * h:128 * (h + 1)].rearrange("(i p) d -> p i d", p=128)
                        r_kt.load("sync", gi, [(ktb[gi % 6][:], ksrc)])
                        r_vt.load("scalar", gi,
                                  [(vtb[gi % 6][:].rearrange("p (i d) -> p i d", d=128), vsrc)])
                        if gi == 0:
                            P.guard("tensor")
                            r_ql.ready("tensor", 0)
                        r_kt.ready("tensor", gi)
                        for i in range(4):
                            kt = 4 * r + i
                            lb = r * 4 + i
                            ib = uh * 36 + lb
                            sb_ = ib % 4
                            if ib >= 4:
                                P.wait("tensor", sx, exp_done[ib - 4])
                            smm = P.op("tensor", lambda e, g3=gi % 6, ii=i, s2=sb_, q1=qo:
                                       e.matmul(out=ps_s[s2][:],
                                                lhsT=ktb[g3][:, 128 * ii:128 * (ii + 1)],
                                                rhs=slab5[:, q1:q1 + 512],
                                                start=True, stop=True), sem=sp)
                            if i == 3:
                                r_kt.consumed(gi, sp, smm)
                            P.wait("scalar", sp, smm)
                            if ib >= 8:
                                P.wait("scalar", sp, pb_used[ib - 8])
                            bias_col = kt if r < 8 else 32
                            ed = P.op("scalar", lambda e, s2=sb_, ibb=ib, bc=bias_col:
                                      e.activation(out=pb[ibb % 8][:], in_=ps_s[s2][:],
                                                   func=AF.Exp, bias=c_bias[:, bc:bc + 1],
                                                   scale=1.0), sem=sx)
                            exp_done[ib] = ed
                            if r == 8:
                                P.wait("vector", sx, ed)
                                md = P.op("vector", lambda e, ibb=ib, ii=i: e.tensor_tensor(
                                    out=pb[ibb % 8][:], in0=pb[ibb % 8][:],
                                    in1=c_masks[:, 512 * ii:512 * (ii + 1)], op=ALU.mult), sem=sv)
                                mask_done[ib] = md
                            if lb >= 2:
                                emit_outsum(lb - 2)
                    emit_outsum(34)
                    emit_outsum(35)
                    ub = uh % 2
                    P.wait("vector", sp, pb_used[uh * 36 + 35])
                    r1 = P.op("vector", lambda e, u2=ub: e.reciprocal(
                        out=linv[u2][:], in_=ps_l[0][:]), sem=sv)
                    if uh == 0:
                        r1_done = {}
                    r1_done[uh] = r1
                    P.wait("tensor", sv, r1)
                    if uh >= 1:
                        P.wait("tensor", sv, linvrep_done[uh - 1])
                    bc2 = P.op("tensor", lambda e, u2=ub: e.matmul(
                        out=ps_b[:], lhsT=c_onesr[:], rhs=linv[u2][:], start=True, stop=True), sem=sp)
                    P.wait("vector", sp, bc2)
                    r2 = P.op("vector", lambda e, u2=ub: e.tensor_copy(
                        out=linvrep[u2][:], in_=ps_b[:]), sem=sv)
                    if uh == 0:
                        linvrep_done = {}
                    linvrep_done[uh] = r2
                    P.wait("vector", sv, r2)
                    nd = P.op("vector", lambda e, q1=qo, u2=ub: e.tensor_tensor(
                        out=slab5[:, q1:q1 + 512], in0=ps_o[u2][:], in1=linvrep[u2][:],
                        op=ALU.mult), sem=sv)
                    norm_done[uh] = nd

            if DEBUG:
                P.wait("sync", sv, norm_done[63])
                P.dma("sync", dbg["SPRE"].rearrange("(j p) c -> p j c", p=128),
                      slab5[:].rearrange("p (j c) -> p j c", c=1024), qdbg)
            s5p.close()
            P.set_pguard((sv, norm_done[63]))
            P.barrier()

            if PHASES >= 6:
                s6 = ExitStack()
                sb6 = lambda name, shape, dt: s6.enter_context(nc.sbuf_tensor(name, shape, dt))
                ps6 = lambda name, shape: s6.enter_context(nc.psum_tensor(name, shape, F32))
                wot = [sb6(f"wot{i}", [128, 512], BF16) for i in range(4)]
                wot32 = [sb6(f"wot32{i}", [128, 512], F32) for i in range(3)]
                xb6 = [sb6(f"xb6{i}", [128, 512], F32) for i in range(3)]
                gb6 = [sb6(f"gb6{i}", [128, 512], BF16) for i in range(3)]
                sgb = [sb6(f"sgb{i}", [128, 512], F32) for i in range(2)]
                ob6 = [sb6(f"ob6{i}", [128, 512], F32) for i in range(2)]
                pc6 = [ps6(f"pc6{i}", [128, 512]) for i in range(8)]

                r_wo = Ring(P, "r_wo", 3)
                wo_cast = {}
                r_xb = Ring(P, "r_xb", 3)
                r_gb = Ring(P, "r_gb", 3)
                r_out = Ring(P, "r_out", 2)
                comb = {}
                for fc in range(8):
                    for dc in range(32):
                        gi = fc * 32 + dc
                        r_wo.load("scalar", gi,
                                  [(wot32[gi % 3][:], WO[128 * dc:128 * (dc + 1), 512 * fc:512 * (fc + 1)])])
                        r_wo.ready("scalar", gi)
                        if gi >= 4:
                            P.wait("scalar", sp, wo_used_d[gi - 4])
                        wc = P.op("scalar", lambda e, b4=gi % 4, b3_=gi % 3:
                                  e.copy(out=wot[b4][:], in_=wot32[b3_][:]), sem=sx)
                        wo_cast[gi] = wc
                        r_wo.consumed(gi, sx, wc)
                        P.wait("tensor", sx, wc)
                        if gi == 0:
                            P.guard("tensor")
                            wo_used_d = {}
                        wdone = None
                        for t in range(8):
                            if dc == 0 and fc > 0:
                                P.wait("tensor", sv, comb[(fc - 1, t)])
                            wdone = P.op("tensor", lambda e, tt=t, b=gi % 4, d2=dc:
                                         e.matmul(out=pc6[tt][:],
                                                  lhsT=slab5[:, 1024 * d2 + 128 * tt:1024 * d2 + 128 * (tt + 1)],
                                                  rhs=wot[b][:],
                                                  start=(d2 == 0), stop=(d2 == 31)),
                                         sem=(sp if t == 7 else None))
                        wo_used_d[gi] = wdone
                    for t in range(8):
                        gi2 = fc * 8 + t
                        r_xb.load("sync", gi2,
                                  [(xb6[gi2 % 3][:], XNAT[128 * t:128 * (t + 1), 512 * fc:512 * (fc + 1)])])
                        r_gb.load("scalar", gi2,
                                  [(gb6[gi2 % 3][:], GSP[128 * t:128 * (t + 1), 512 * fc:512 * (fc + 1)])])
                        P.wait("vector", sp, wdone)
                        r_gb.ready("vector", gi2)
                        r_xb.ready("vector", gi2)
                        r_out.guard_write("vector", gi2)
                        c1 = P.op("vector", lambda e, tt=t, g2=gi2 % 3: e.tensor_tensor(
                            out=sgb[tt % 2][:], in0=pc6[tt][:], in1=gb6[g2][:],
                            op=ALU.mult), sem=sv)
                        P.wait("vector", sv, c1)
                        c2 = P.op("vector", lambda e, tt=t, g2=gi2 % 3: e.tensor_tensor(
                            out=ob6[tt % 2][:], in0=sgb[tt % 2][:], in1=xb6[g2][:],
                            op=ALU.add), sem=sv)
                        comb[(fc, t)] = c2
                        r_xb.consumed(gi2, sv, c2)
                        r_gb.consumed(gi2, sv, c2)
                        P.wait("sync", sv, c2)
                        r_out.load("sync", gi2,
                                   [(OUT[128 * t:128 * (t + 1), 512 * fc:512 * (fc + 1)], ob6[t % 2][:])])
                for s_, v_ in r_out.totals():
                    P.wait("sync", s_, v_)
                s6.close()
            s5.close()

        P.wait("sync", qdbg, qdbg.n)

        block = es.enter_context(nc.Block())
        P.finalize(block)

    counts = {e: len(P.streams[e]) for e in ENGINES}
    return nc, counts



def prep_inputs(L0_attn_x, L0_attn_wq, L0_attn_wk, L0_attn_wv, L0_attn_wo,
                L0_ffn_qweight_gu, L0_ffn_qzeros_gu, L0_ffn_scales_gu, L0_ffn_g_idx_gu,
                L0_ffn_qweight_down, L0_ffn_qzeros_down, L0_ffn_scales_down, L0_ffn_g_idx_down):
    f32 = np.float32
    x = np.asarray(L0_attn_x, f32)
    wq = np.asarray(L0_attn_wq, f32)
    wk = np.asarray(L0_attn_wk, f32)
    wv = np.asarray(L0_attn_wv, f32)
    wo = np.ascontiguousarray(np.asarray(L0_attn_wo, f32))
    qwgu = np.asarray(L0_ffn_qweight_gu, np.uint32)
    qzgu = np.ascontiguousarray(np.asarray(L0_ffn_qzeros_gu, np.uint32))
    scgu = np.asarray(L0_ffn_scales_gu, f32)
    qwdn = np.ascontiguousarray(np.asarray(L0_ffn_qweight_down, np.uint32))
    qzdn = np.ascontiguousarray(np.asarray(L0_ffn_qzeros_down, np.uint32))
    scdn = np.ascontiguousarray(np.asarray(L0_ffn_scales_down, f32))

    wq_p = np.ascontiguousarray(wq[PERM_H, :])
    wk_p = np.ascontiguousarray(wk[PERM_H, :])
    wv_p = np.ascontiguousarray(wv[PERM_H, :])
    qwgu_p = np.ascontiguousarray(
        np.concatenate([qwgu[:, :INTER][:, PERM_D], qwgu[:, INTER:][:, PERM_D]], axis=1))
    scgu_p = np.ascontiguousarray(
        np.concatenate([scgu[:, :INTER][:, PERM_D], scgu[:, INTER:][:, PERM_D]], axis=1))

    gidx = np.arange(128) // 16
    scexp = np.zeros((128, 88, 4, 2, 128), f32)
    rows_by_w = [scgu_p[8 * w + gidx] for w in range(4)]
    for ci, (wp, n, kk) in enumerate(DN_CHUNKS):
        cb = 1024 * wp + kk * n
        for w in range(4):
            scexp[:, ci, w, 0, :kk] = rows_by_w[w][:, cb:cb + kk]
            scexp[:, ci, w, 1, :kk] = rows_by_w[w][:, INTER + cb:INTER + cb + kk]
    scexp = np.ascontiguousarray(scexp.reshape(128, -1))
    scdexp = np.empty((128, 11, D), f32)
    for wp in range(11):
        ridx = np.minimum(8 * wp + gidx, 85)
        scdexp[:, wp, :] = scdn[ridx]
    scdexp = np.ascontiguousarray(scdexp.reshape(128, -1))

    permm = np.zeros((128, 128), f32)
    for i in range(0, 128, 2):
        permm[i, i + 1] = 1.0
        permm[i + 1, i] = 1.0
    eneg = np.zeros((128, 4, 32), f32)
    for w in range(4):
        eneg[np.arange(128), w, 8 * w + np.arange(128) // 16] = -1.0
    eneg = eneg.reshape(128, 128)
    enegd = np.zeros((128, 11, 86), f32)
    for wp in range(11):
        kk = 128 if wp < 10 else 96
        enegd[np.arange(kk), wp, 8 * wp + np.arange(kk) // 16] = -1.0
    enegd = enegd.reshape(128, 11 * 86)
    onesc = np.ones((128, 1), f32)
    onesr = np.ones((1, 128), f32)
    masks = np.zeros((128, 4, SEG), f32)
    for dlt in range(4):
        pp = np.arange(128)[:, None]
        jj = np.arange(SEG)[None, :]
        masks[:, dlt, :] = (pp <= jj - 128 * dlt).astype(f32)
    masks = masks.reshape(128, 4 * SEG)

    inv = 1.0 / (10000.0 ** (np.arange(0, HD, 2)[:64].astype(f32) / HD))

    per_core = []
    for c in range(NCORES):
        pos = np.concatenate([512 * c + np.arange(SEG), 512 * c + np.arange(SEG)])
        f = np.outer(pos, inv).astype(f32)
        cosf = np.cos(f)
        sinf = np.sin(f)
        ropec = np.empty((128, T), f32)
        ropes = np.empty((128, T), f32)
        ii = np.arange(64)
        ropec[2 * ii, :] = cosf.T
        ropec[2 * ii + 1, :] = cosf.T
        ropes[2 * ii, :] = -sinf.T
        ropes[2 * ii + 1, :] = sinf.T
        biast = np.zeros((128, 33), f32)
        for kt in range(32):
            if kt >= 4 * c:
                biast[:, kt] = NEG
        xa = x[0, 512 * c:512 * (c + 1)]
        xb = x[1, 512 * c:512 * (c + 1)]
        xnat = np.ascontiguousarray(np.concatenate([xa, xb], axis=0))
        xperm = np.ascontiguousarray(xnat[:, PERM_H])
        per_core.append(dict(XP=xperm, XNAT=xnat, ROPEC=ropec, ROPES=ropes, BIAST=biast))

    shared = dict(
        WQ=wq_p, WK=wk_p, WV=wv_p, WO=wo, QWGU=qwgu_p, SCGU=scgu_p, QZGU=qzgu,
        QWDN=qwdn, SCDN=scdn, QZDN=qzdn, SCEXP=scexp, SCDEXP=scdexp,
        PERMM=permm, IDTI=np.eye(128, dtype=f32), ENEG=eneg, ENEGD=enegd,
        ONESC=onesc, ONESR=onesr, MASKS=masks,
    )
    in_maps = []
    for c in range(NCORES):
        m = dict(shared)
        m.update(per_core[c])
        in_maps.append(m)
    return in_maps


def kernel(**inputs):
    in_maps = prep_inputs(**inputs)
    nc, counts = build_program()
    res = run_bass_kernel_spmd(nc, in_maps, core_ids=list(range(NCORES)))
    out = np.empty((2, 4096, D), np.float32)
    for c in range(NCORES):
        o = np.asarray(res.results[c]["OUT"])
        out[0, 512 * c:512 * (c + 1)] = o[:SEG]
        out[1, 512 * c:512 * (c + 1)] = o[SEG:]
    return out


if __name__ == "__main__":
    import sys
    if sys.argv[1:] and sys.argv[1] == "build":
        nc, counts = build_program()
        print("build ok", counts, "total", sum(counts.values()))


# version 14 (final)
# speedup vs baseline: 1.1701x; 1.0648x over previous
import os
import numpy as np
import ml_dtypes
from contextlib import ExitStack
import concourse.bass as bass
import concourse.mybir as mybir
from concourse.bass_utils import run_bass_kernel_spmd

F32 = mybir.dt.float32
BF16 = mybir.dt.bfloat16
U32 = mybir.dt.uint32
U8 = mybir.dt.uint8
AF = mybir.ActivationFunctionType
ALU = mybir.AluOpType
AX = mybir.AxisListType
bf16 = ml_dtypes.bfloat16

NCORES = 8
D = 4096
T = 1024
INTER = 11008
H = 32
HD = 128
SEG = 512
EPS = 1e-5
ATT_SCALE = 1.0 / np.sqrt(128.0)
NEG = -1e9

PHASES = int(os.environ.get("WK_PHASES", "6"))
DEBUG = os.environ.get("WK_DEBUG", "0") == "1"
AGSHARED = os.environ.get("WK_AGSHARED", "1") == "1"

ENGINES = ("sync", "scalar", "vector", "tensor", "gpsimd")


def mkperm(F, win=1024):
    perm = np.empty(F, dtype=np.int64)
    for base in range(0, F, win):
        m = min(win, F - base)
        kk = m // 8
        j = np.arange(m)
        perm[base + j] = base + 8 * (j % kk) + (j // kk)
    return perm


PERM_H = mkperm(D)
PERM_D = mkperm(INTER)
DN_CHUNKS = [(wp, n, 128 if wp < 10 else 96) for wp in range(11) for n in range(8)]
DN_ROWBASE = {}
_rb = 0
for _i, (_wp, _n, _kk) in enumerate(DN_CHUNKS):
    DN_ROWBASE[_i] = _rb
    _rb += _kk
assert _rb == INTER


class Sem:
    def __init__(self, nc, es, name):
        self.h = es.enter_context(nc.semaphore(name))
        self.n = 0


class Prog:
    def __init__(self, nc, es):
        self.nc = nc
        self.streams = {e: [] for e in ENGINES}
        self.es = es
        self._sems = {}
        self.pguard = []

    def sem(self, name):
        if name not in self._sems:
            self._sems[name] = Sem(self.nc, self.es, name)
        return self._sems[name]

    def emit(self, eng, fn):
        self.streams[eng].append(fn)

    def wait(self, eng, sem, val):
        if val > 0:
            self.emit(eng, lambda e, s=sem.h, v=val: e.wait_ge(s, v))

    def dma(self, eng, out, in_, sem):
        sem.n += 16
        self.emit(eng, lambda e, o=out, i=in_, s=sem.h: e.dma_start(out=o, in_=i).then_inc(s, 16))
        return sem.n

    def op(self, eng, fn, sem=None, k=1):
        if sem is None:
            self.emit(eng, fn)
            return None
        sem.n += k
        self.emit(eng, lambda e, f=fn, s=sem.h, kk=k: f(e).then_inc(s, kk))
        return sem.n

    def set_pguard(self, *entries):
        self.pguard = [(s, v) for (s, v) in entries if v]

    def guard(self, eng):
        for s, v in self.pguard:
            self.wait(eng, s, v)

    def barrier(self):
        s = self.sem("bar")
        own = {"vector": "sv", "scalar": "sx", "tensor": "sp", "gpsimd": "sg"}
        for e in ENGINES:
            if e in own and own[e] in self._sems:
                osem = self._sems[own[e]]
                self.wait(e, osem, osem.n)
            s.n += 1
            self.emit(e, lambda en, h=s.h: en.nop().then_inc(h, 1))
        tgt = s.n
        for e in ENGINES:
            self.wait(e, s, tgt)

    def finalize(self, block):
        decs = {"sync": block.sync, "scalar": block.scalar, "vector": block.vector,
                "tensor": block.tensor, "gpsimd": block.gpsimd}
        for name, dec in decs.items():
            ops = list(self.streams[name])
            def make(ops_):
                def _(eng):
                    for f in ops_:
                        f(eng)
                return _
            dec(make(ops))


class Ring:

    def __init__(self, P, name, n):
        self.P = P
        self.n = n
        self.sems = [P.sem(f"{name}_{i}") for i in range(n)]
        self.cons = [None] * n
        self.val = [0] * n

    def load(self, eng, i, pairs):
        s = i % self.n
        if self.cons[s] is not None:
            for c_ in self.cons[s]:
                self.P.wait(eng, *c_)
            self.cons[s] = None
        for dst, src in pairs:
            self.P.dma(eng, dst, src, self.sems[s])
        self.val[s] = self.sems[s].n
        return (self.sems[s], self.val[s])

    def ready(self, eng, i):
        s = i % self.n
        self.P.wait(eng, self.sems[s], self.val[s])

    def consumed(self, i, sem, val):
        self.cons[i % self.n] = [(sem, val)]

    def consumed_multi(self, i, entries):
        self.cons[i % self.n] = [e_ for e_ in entries if e_[1]]

    def guard_write(self, eng, i):
        s = i % self.n
        if self.val[s]:
            self.P.wait(eng, self.sems[s], self.val[s])

    def totals(self):
        return [(s, s.n) for s in self.sems]


def build_program():
    nc = bass.Bass("TRN2", target_bir_lowering=False)
    ap = lambda t: t.ap()

    XP = ap(nc.dram_tensor("XP", [T, D], F32, kind="ExternalInput"))
    XNAT = ap(nc.dram_tensor("XNAT", [T, D], F32, kind="ExternalInput"))
    WQ = ap(nc.dram_tensor("WQ", [D, D], F32, kind="ExternalInput"))
    WK = ap(nc.dram_tensor("WK", [D, D], F32, kind="ExternalInput"))
    WV = ap(nc.dram_tensor("WV", [D, D], F32, kind="ExternalInput"))
    WO = ap(nc.dram_tensor("WO", [D, D], F32, kind="ExternalInput"))
    QWGU = ap(nc.dram_tensor("QWGU", [512, 2 * INTER], U32, kind="ExternalInput"))
    SCGU = ap(nc.dram_tensor("SCGU", [32, 2 * INTER], F32, kind="ExternalInput"))
    QZGU = ap(nc.dram_tensor("QZGU", [32, 2752], U32, kind="ExternalInput"))
    QWDN = ap(nc.dram_tensor("QWDN", [1376, D], U32, kind="ExternalInput"))
    SCDN = ap(nc.dram_tensor("SCDN", [86, D], F32, kind="ExternalInput"))
    QZDN = ap(nc.dram_tensor("QZDN", [86, 512], U32, kind="ExternalInput"))
    SCEXP = ap(nc.dram_tensor("SCEXP", [128, 88 * 1024], F32, kind="ExternalInput"))
    SCDEXP = ap(nc.dram_tensor("SCDEXP", [128, 11 * D], F32, kind="ExternalInput"))
    ROPEC = ap(nc.dram_tensor("ROPEC", [128, T], F32, kind="ExternalInput"))
    ROPES = ap(nc.dram_tensor("ROPES", [128, T], F32, kind="ExternalInput"))
    PERMM = ap(nc.dram_tensor("PERMM", [128, 128], F32, kind="ExternalInput"))
    IDTI = ap(nc.dram_tensor("IDTI", [128, 128], F32, kind="ExternalInput"))
    ENEG = ap(nc.dram_tensor("ENEG", [128, 128], F32, kind="ExternalInput"))
    ENEGD = ap(nc.dram_tensor("ENEGD", [128, 11 * 86], F32, kind="ExternalInput"))
    ONESC = ap(nc.dram_tensor("ONESC", [128, 1], F32, kind="ExternalInput"))
    ONESR = ap(nc.dram_tensor("ONESR", [1, 128], F32, kind="ExternalInput"))
    MASKS = ap(nc.dram_tensor("MASKS", [128, 4 * SEG], F32, kind="ExternalInput"))
    BIAST = ap(nc.dram_tensor("BIAST", [128, 33], F32, kind="ExternalInput"))

    OUT = ap(nc.dram_tensor("OUT", [T, D], F32, kind="ExternalOutput"))

    agK_in = ap(nc.dram_tensor("agK_in", [D, T], BF16))
    agV_in = ap(nc.dram_tensor("agV_in", [T, D], BF16))
    agK_out = ap(nc.dram_tensor("agK_out", [NCORES * D, T], BF16, addr_space="Shared" if AGSHARED else "Local"))
    agV_out = ap(nc.dram_tensor("agV_out", [NCORES * T, D], BF16, addr_space="Shared" if AGSHARED else "Local"))
    QSP = ap(nc.dram_tensor("QSP", [D, T], BF16))
    SCGUB = ap(nc.dram_tensor("SCGUB", [32, 2 * INTER], BF16))
    SCDNB = ap(nc.dram_tensor("SCDNB", [86, D], BF16))
    HTS = ap(nc.dram_tensor("HTS", [INTER, T], BF16))
    GSP = ap(nc.dram_tensor("GSP", [T, D], BF16))

    dbg = {}
    if DEBUG:
        def dbgt(name, shape, dt):
            dbg[name] = ap(nc.dram_tensor("DBG_" + name, shape, dt, kind="ExternalOutput"))
        dbgt("XNT", [D, T], BF16)
        dbgt("SG", [32, T], BF16)
        if PHASES >= 2:
            dbgt("QSP", [D, T], BF16)
            dbgt("KIN", [D, T], BF16)
            dbgt("VIN", [T, D], BF16)
        if PHASES >= 3:
            dbgt("HTS", [INTER, T], BF16)
            dbgt("SH", [86, T], BF16)
        if PHASES >= 4:
            dbgt("GSP", [T, D], BF16)
        if PHASES >= 5:
            dbgt("SPRE", [D, T], BF16)

    es = ExitStack()
    with es:
        P = Prog(nc, es)
        sb = lambda name, shape, dt: es.enter_context(nc.sbuf_tensor(name, shape, dt))

        c_idt = sb("c_idt", [128, 128], BF16)
        c_perm = sb("c_perm", [128, 128], BF16)
        c_eneg = sb("c_eneg", [128, 128], BF16)
        c_enegd = sb("c_enegd", [128, 11 * 86], BF16)
        c_ones = sb("c_ones", [128, 1], BF16)
        c_onesr = sb("c_onesr", [1, 128], F32)
        c_masks = sb("c_masks", [128, 4 * SEG], BF16)
        c_bias = sb("c_bias", [128, 33], F32)
        c_ropec = sb("c_ropec", [128, T], BF16)
        c_ropes = sb("c_ropes", [128, T], BF16)
        sgneg = sb("sgneg", [32, T], BF16)
        shneg = sb("shneg", [86, T], BF16)

        sv = P.sem("sv")
        sx = P.sem("sx")
        sp = P.sem("sp")
        sg = P.sem("sg")
        cc = P.sem("cc")
        qdbg = P.sem("qdbg")

        qc_g = P.sem("qc_g")
        qc_s = P.sem("qc_s")
        for dst, src in [(c_idt, IDTI), (c_perm, PERMM), (c_eneg, ENEG), (c_enegd, ENEGD),
                         (c_ones, ONESC), (c_masks, MASKS), (c_ropec, ROPEC), (c_ropes, ROPES)]:
            P.dma("gpsimd", dst[:], src, qc_g)
        P.dma("sync", c_onesr[:], ONESR, qc_s)
        P.dma("sync", c_bias[:], BIAST, qc_s)
        pc_sems = {}
        pc_chunk = {}

        def precast_cols(nm_, dst_, src_, nch):
            rows = dst_.shape[0]
            step = max(1, rows // nch)
            vals = []
            for k_ in range(nch):
                s_ = P.sem(f"qpc_{nm_}_{k_}")
                P.dma("gpsimd", dst_[k_ * step:(k_ + 1) * step, :],
                      src_[k_ * step:(k_ + 1) * step, :], s_)
                vals.append((s_, s_.n))
            pc_chunk[nm_] = vals
            pc_sems[nm_] = vals[-1]

        def wait_pc(eng, nm_):
            for c_ in pc_chunk[nm_]:
                P.wait(eng, *c_)

        precast_cols("scgub", SCGUB, SCGU, 1)
        precast_cols("scdnb", SCDNB, SCDN, 1)
        QCG = qc_g.n
        QCS = qc_s.n
        for e_ in ENGINES:
            P.wait(e_, qc_g, QCG)
            P.wait(e_, qc_s, QCS)

        scopeA = ExitStack()
        sbA = lambda name, shape, dt: scopeA.enter_context(nc.sbuf_tensor(name, shape, dt))

        xnt = sbA("xnt", [128, 32 * 1024], BF16)
        xnt_v = xnt[:].rearrange("p (j c) -> p j c", c=1024)

        s1 = ExitStack()
        sb1 = lambda name, shape, dt: s1.enter_context(nc.sbuf_tensor(name, shape, dt))
        ps1 = lambda name, shape, dt=F32: s1.enter_context(nc.psum_tensor(name, shape, dt))
        xp = [sb1(f"xp{i}", [128, D], F32) for i in range(2)]
        xsq = [sb1(f"xsq{i}", [128, D], F32) for i in range(2)]
        xnb = [sb1(f"xnb{i}", [128, D], BF16) for i in range(2)]
        st1 = sb1("st1", [128, 8], F32)
        ptr_ = [ps1(f"ptr{i}", [128, 1024], BF16) for i in range(2)]
        psg = ps1("psg", [32, 1024], F32)

        r_xp = Ring(P, "r_xp", 2)
        done_tr = {}
        evac_tr = {}
        sq_cons = {}

        for t in range(8):
            r_xp.load("sync", t, [(xp[t % 2][:], XP[128 * t:128 * (t + 1), :])])
            r_xp.ready("scalar", t)
            if t >= 2:
                P.wait("scalar", sv, sq_cons[t - 2])
            sq_done = P.op("scalar", lambda e, a=xsq[t % 2], b=xp[t % 2]:
                           e.activation(out=a[:], in_=b[:], func=AF.Square), sem=sx)
            r_xp.ready("vector", t)
            m1 = P.op("vector", lambda e, o=st1, i=xp[t % 2]:
                      e.reduce_sum(out=o[:, 0:1], in_=i[:], axis=AX.X), sem=sv)
            P.wait("vector", sx, sq_done)
            m2 = P.op("vector", lambda e, o=st1, i=xsq[t % 2]:
                      e.reduce_sum(out=o[:, 1:2], in_=i[:], axis=AX.X), sem=sv)
            sq_cons[t] = m2
            P.wait("vector", sv, m2)
            m3 = P.op("vector", lambda e, o=st1: e.tensor_scalar(
                out=o[:, 2:3], in0=o[:, 0:1], scalar1=1.0 / D, scalar2=None, op0=ALU.mult), sem=sv)
            P.wait("vector", sv, m3)
            m4 = P.op("vector", lambda e, o=st1: e.tensor_tensor(
                out=o[:, 3:4], in0=o[:, 2:3], in1=o[:, 2:3], op=ALU.mult), sem=sv)
            P.wait("vector", sv, m4)
            m5 = P.op("vector", lambda e, o=st1: e.tensor_scalar(
                out=o[:, 3:4], in0=o[:, 3:4], scalar1=-1.0, scalar2=EPS,
                op0=ALU.mult, op1=ALU.add), sem=sv)
            P.wait("scalar", sv, m5)
            m6 = P.op("scalar", lambda e, o=st1: e.activation(
                out=o[:, 4:5], in_=o[:, 1:2], func=AF.Sqrt, bias=o[:, 3:4], scale=1.0 / D), sem=sx)
            P.wait("vector", sx, m6)
            m7 = P.op("vector", lambda e, o=st1: e.reciprocal(out=o[:, 5:6], in_=o[:, 4:5]), sem=sv)
            P.wait("vector", sv, m7)
            m8 = P.op("vector", lambda e, o=st1: e.tensor_scalar(
                out=o[:, 6:7], in0=o[:, 2:3], scalar1=o[:, 5:6], scalar2=-1.0,
                op0=ALU.mult, op1=ALU.mult), sem=sv)
            P.wait("vector", sv, m8)
            if t >= 2:
                P.wait("vector", sp, done_tr[(t - 2, 3)])
            m9 = P.op("vector", lambda e, o=xnb[t % 2], i=xp[t % 2], s=st1: e.tensor_scalar(
                out=o[:], in0=i[:], scalar1=s[:, 5:6], scalar2=None, op0=ALU.mult), sem=sv)
            P.wait("vector", sv, m9)
            m10 = P.op("vector", lambda e, o=xnb[t % 2], s=st1: e.tensor_scalar(
                out=o[:], in0=o[:], scalar1=s[:, 6:7], scalar2=None, op0=ALU.add), sem=sv)
            r_xp.consumed_multi(t, [(sv, m10), (sx, sq_done)])

            P.wait("tensor", sv, m10)
            if t == 0:
                P.wait("tensor", qc_g, QCG)
            for j8 in range(4):
                key = (t, j8)
                prevkey = (t, j8 - 2) if j8 >= 2 else ((t - 1, j8 + 2) if t >= 1 else None)
                if prevkey is not None and prevkey in evac_tr:
                    P.wait("tensor", sv, evac_tr[prevkey])
                for i in range(8):
                    j = 8 * j8 + i
                    done_tr[key] = P.op("tensor", lambda e, pt=ptr_[j8 % 2], ii=i, xb=xnb[t % 2], jj=j:
                                        e.transpose(out=pt[:, 128 * ii:128 * (ii + 1)],
                                                    in_=xb[:, 128 * jj:128 * (jj + 1)],
                                                    identity=c_idt[:]), sem=sp)
                P.wait("vector", sp, done_tr[key])
                evac_tr[key] = P.op("vector", lambda e, pt=ptr_[j8 % 2], jb=j8, tt=t:
                                    e.tensor_copy(
                                        out=xnt_v[:, 8 * jb:8 * (jb + 1), 128 * tt:128 * (tt + 1)],
                                        in_=pt[:].rearrange("p (i c) -> p i c", c=128)), sem=sv)

        xnt_ready = evac_tr[(7, 3)]
        P.wait("tensor", sv, xnt_ready)
        last_mm = None
        for w in range(4):
            for n in range(8):
                j = 8 * w + n
                for hf in range(2):
                    last_mm = P.op("tensor", lambda e, jj=j, h=hf, ww=w,
                                   st=(w == 0 and n == 0), sp_=(w == 3 and n == 7):
                                   e.matmul(out=psg[0:32, 512 * h:512 * (h + 1)],
                                            lhsT=c_eneg[:, 32 * ww:32 * (ww + 1)],
                                            rhs=xnt[:, 1024 * jj + 512 * h:1024 * jj + 512 * (h + 1)],
                                            start=st, stop=sp_), sem=sp)
        P.wait("vector", sp, last_mm)
        sg_evac = P.op("vector", lambda e: e.tensor_copy(
            out=sgneg[0:32, :], in_=psg[0:32, :]), sem=sv)

        if DEBUG:
            P.wait("sync", sv, sg_evac)
            P.dma("sync", dbg["XNT"].rearrange("(j p) c -> p j c", p=128),
                  xnt[:].rearrange("p (j c) -> p j c", c=1024), qdbg)
            P.dma("sync", dbg["SG"], sgneg[:], qdbg)

        s1.close()
        P.set_pguard((sv, sg_evac))
        P.barrier()

        if PHASES >= 2:
            s2 = ExitStack()
            sb2 = lambda name, shape, dt: s2.enter_context(nc.sbuf_tensor(name, shape, dt))
            s2p = ExitStack()
            ps2 = lambda name, shape, dt=F32: s2p.enter_context(nc.psum_tensor(name, shape, dt))
            wst = [sb2(f"wst{i}", [128, 32 * 128], BF16) for i in range(3)]
            wst32 = [sb2(f"wst32{i}", [128, 32 * 128], F32) for i in range(3)]
            rx = [sb2(f"rx{i}", [128, 512], BF16) for i in range(4)]
            t1b = [sb2(f"t1b{i}", [128, 512], F32) for i in range(4)]
            t2b = [sb2(f"t2b{i}", [128, 512], F32) for i in range(4)]
            qkst = [sb2(f"qkst{i}", [128, 512], BF16) for i in range(4)]
            psq = [ps2(f"psq{i}", [128, 512]) for i in range(6)]
            psw = [ps2(f"psw{i}", [128, 512]) for i in range(2)]

            r_wst = Ring(P, "r_wst", 3)
            r_qko = Ring(P, "r_qko", 4)
            rx_done = {}
            pend_store = []
            t1_done = {}
            t2_done = {}
            add_done = {}
            swap_done = {}
            idx = 0
            for (Wsrc, dstT, scale) in ((WQ, QSP, ATT_SCALE), (WK, agK_in, 1.0)):
                for oc in range(32):
                    gi = idx * 32 + oc
                    r_wst.load("sync", gi,
                               [(wst32[gi % 3][:].rearrange("p (j c) -> p j c", c=128),
                                 Wsrc[:, 128 * oc:128 * (oc + 1)].rearrange("(j p) c -> p j c", p=128))])
                    r_wst.ready("scalar", gi)
                    if gi >= 3:
                        P.wait("scalar", sp, wst_used[gi - 3])
                    wstc = P.op("scalar", lambda e, b3=gi % 3: e.copy(
                        out=wst[b3][:], in_=wst32[b3][:]), sem=sx)
                    r_wst.consumed(gi, sx, wstc)
                    if gi == 0:
                        wst_used = {}
                    for hf in range(2):
                        ii = 2 * gi + hf
                        if ii == 0:
                            P.guard("tensor")
                        P.wait("tensor", sx, wstc)
                        if ii >= 6:
                            P.wait("tensor", sx, rx_done[ii - 6])
                        mm_last = None
                        for j in range(32):
                            mm_last = P.op("tensor", lambda e, jj=j, h=hf, b=gi % 3, pq=psq[ii % 6]:
                                           e.matmul(out=pq[:],
                                                    lhsT=wst[b][:, 128 * jj:128 * (jj + 1)],
                                                    rhs=xnt[:, 1024 * jj + 512 * h:1024 * jj + 512 * (h + 1)],
                                                    start=(jj == 0), stop=(jj == 31)), sem=sp)
                        if hf == 1:
                            wst_used[gi] = mm_last
                        P.wait("scalar", sp, mm_last)
                        if ii >= 4:
                            P.wait("scalar", sp, swap_done[ii - 4])
                            P.wait("scalar", sg, t2_done[ii - 4])
                        rd = P.op("scalar", lambda e, b=ii % 4, pq=psq[ii % 6], s=scale:
                                  e.mul(out=rx[b][:], in_=pq[:], mul=s), sem=sx)
                        rx_done[ii] = rd
                        P.wait("tensor", sx, rd)
                        if ii >= 2:
                            P.wait("tensor", sv, t1_done[ii - 2])
                        sw = P.op("tensor", lambda e, b=ii % 4, w2=ii % 2:
                                  e.matmul(out=psw[w2][:], lhsT=c_perm[:], rhs=rx[b][:],
                                           start=True, stop=True), sem=sp)
                        swap_done[ii] = sw
                        P.wait("vector", sp, sw)
                        if ii >= 4:
                            P.wait("vector", sv, add_done[ii - 4])
                        td1 = P.op("vector", lambda e, b=ii % 4, h=hf, w2=ii % 2:
                                   e.tensor_tensor(out=t1b[b][:], in0=psw[w2][:],
                                                   in1=c_ropes[:, 512 * h:512 * (h + 1)],
                                                   op=ALU.mult), sem=sv)
                        t1_done[ii] = td1
                        P.wait("gpsimd", sx, rd)
                        if ii == 0:
                            P.wait("gpsimd", qc_g, QCG)
                        if ii >= 4:
                            P.wait("gpsimd", sv, add_done[ii - 4])
                        td2 = P.op("gpsimd", lambda e, b=ii % 4, h=hf:
                                   e.tensor_tensor(out=t2b[b][:], in0=rx[b][:],
                                                   in1=c_ropec[:, 512 * h:512 * (h + 1)],
                                                   op=ALU.mult), sem=sg)
                        t2_done[ii] = td2
                        P.wait("vector", sg, td2)
                        P.wait("vector", sv, td1)
                        r_qko.guard_write("vector", ii)
                        ad = P.op("vector", lambda e, b=ii % 4:
                                  e.tensor_tensor(out=qkst[b][:], in0=t1b[b][:], in1=t2b[b][:],
                                                  op=ALU.add), sem=sv)
                        add_done[ii] = ad
                        pend_store.append((ii, dstT, oc, hf))
                        if len(pend_store) > 2:
                            jj, dT, oc2, hf2 = pend_store.pop(0)
                            P.wait("scalar", sv, add_done[jj])
                            r_qko.load("scalar", jj,
                                       [(dT[128 * oc2:128 * (oc2 + 1), 512 * hf2:512 * (hf2 + 1)],
                                         qkst[jj % 4][:])])
                idx += 1
            while pend_store:
                jj, dT, oc2, hf2 = pend_store.pop(0)
                P.wait("scalar", sv, add_done[jj])
                r_qko.load("scalar", jj,
                           [(dT[128 * oc2:128 * (oc2 + 1), 512 * hf2:512 * (hf2 + 1)],
                             qkst[jj % 4][:])])
            s2p.close()
            for s_, v_ in r_qko.totals():
                P.wait("gpsimd", s_, v_)
            P.op("gpsimd", lambda e: e.collective_compute(
                "AllGather", ALU.bypass, replica_groups=[list(range(NCORES))],
                ins=[agK_in.opt()], outs=[agK_out.opt()]), sem=cc)
            P.set_pguard((sx, rx_done[127]), (sv, max(t1_done[127], add_done[127])))
            P.barrier()

            s2v = ExitStack()
            ps2v = lambda name, shape: s2v.enter_context(nc.psum_tensor(name, shape, F32))
            psv = [ps2v(f"psv{i}", [128, 512]) for i in range(8)]
            vw = [sb2(f"vw{i}", [128, 512], BF16) for i in range(4)]
            vw32 = [sb2(f"vw32{i}", [128, 512], F32) for i in range(3)]
            vst = [sb2(f"vst{i}", [128, 512], BF16) for i in range(2)]
            r_vw = Ring(P, "r_vw", 3)
            r_vo = Ring(P, "r_vo", 2)
            v_evac = {}
            for fc in range(8):
                for j in range(32):
                    gi = fc * 32 + j
                    r_vw.load("sync", gi,
                              [(vw32[gi % 3][:], WV[128 * j:128 * (j + 1), 512 * fc:512 * (fc + 1)])])
                    r_vw.ready("scalar", gi)
                    if gi >= 4:
                        P.wait("scalar", sp, vw_used[gi - 4])
                    vwc = P.op("scalar", lambda e, b4=gi % 4, b3=gi % 3: e.copy(
                        out=vw[b4][:], in_=vw32[b3][:]), sem=sx)
                    r_vw.consumed(gi, sx, vwc)
                    if gi == 0:
                        vw_used = {}
                    P.wait("tensor", sx, vwc)
                    if gi == 0:
                        P.guard("tensor")
                    if j == 0 and fc > 0:
                        P.wait("tensor", sx, v_evac[(fc - 1, 7)])
                    last = None
                    for t in range(8):
                        last = P.op("tensor", lambda e, jj=j, tt=t, b=gi % 4:
                                    e.matmul(out=psv[tt][:],
                                             lhsT=xnt[:, 1024 * jj + 128 * tt:1024 * jj + 128 * (tt + 1)],
                                             rhs=vw[b][:],
                                             start=(jj == 0), stop=(jj == 31)), sem=sp)
                    vw_used[gi] = last
                v_fill = last
                for t in range(8):
                    gi2 = fc * 8 + t
                    P.wait("scalar", sp, v_fill)
                    r_vo.guard_write("scalar", gi2)
                    ev = P.op("scalar", lambda e, tt=t: e.copy(out=vst[tt % 2][:], in_=psv[tt][:]), sem=sx)
                    v_evac[(fc, t)] = ev
                    P.wait("scalar", sx, ev)
                    r_vo.load("scalar", gi2,
                              [(agV_in[128 * t:128 * (t + 1), 512 * fc:512 * (fc + 1)], vst[t % 2][:])])
            s2v.close()

            for s_, v_ in r_vo.totals():
                P.wait("gpsimd", s_, v_)
            P.op("gpsimd", lambda e: e.collective_compute(
                "AllGather", ALU.bypass, replica_groups=[list(range(NCORES))],
                ins=[agV_in.opt()], outs=[agV_out.opt()]), sem=cc)

            if DEBUG:
                for s_, v_ in r_qko.totals() + r_vo.totals():
                    P.wait("sync", s_, v_)
                P.dma("sync", dbg["QSP"], QSP, qdbg)
                P.dma("sync", dbg["KIN"], agK_in, qdbg)
                P.dma("sync", dbg["VIN"], agV_in, qdbg)
            s2.close()
            P.set_pguard((sx, v_evac[(7, 7)]))
            P.barrier()

        if PHASES >= 3:
            s3 = ExitStack()
            sb3 = lambda name, shape, dt: s3.enter_context(nc.sbuf_tensor(name, shape, dt))
            ps3 = lambda name, shape, dt=F32: s3.enter_context(nc.psum_tensor(name, shape, dt))
            qst = [sb3(f"qst{i}", [128, 1024], U32) for i in range(2)]
            lowb = [sb3(f"lowb{i}", [128, 1024], U32) for i in range(2)]
            highb = [sb3(f"highb{i}", [128, 1024], U32) for i in range(2)]
            scb = [sb3(f"scb{i}", [128, 1024], BF16) for i in range(2)]
            scb32 = [sb3(f"scb32{i}", [128, 1024], F32) for i in range(2)]
            w1t = [sb3(f"w1t{i}", [128, 32 * 256], BF16) for i in range(3)]
            zqst = [sb3(f"zqst{i}", [32, 256], U32) for i in range(2)]
            sczb = [sb3(f"sczb{i}", [32, 256], BF16) for i in range(2)]
            zu32 = sb3("zu32", [32, 256], U32)
            zf32 = sb3("zf32", [32, 256], F32)
            zbuf = [sb3(f"zbuf{i}", [32, 256], BF16) for i in range(2)]
            hgate = [sb3(f"hgate{i}", [128, 512], BF16) for i in range(2)]
            hchunk = [sb3(f"hchunk{i}", [128, 1024], BF16) for i in range(2)]
            pg = [ps3(f"pg{i}", [128, 512]) for i in range(2)]
            pu = [ps3(f"pu{i}", [128, 512]) for i in range(2)]
            psh = ps3("psh", [86, 1024])

            r_qw = Ring(P, "r_qw", 2)
            r_sc = Ring(P, "r_sc", 2)
            r_zq = Ring(P, "r_zq", 2)
            r_scz = Ring(P, "r_scz", 2)
            r_hsp = Ring(P, "r_hsp", 2)
            mult_done = {}
            h_done = {}
            gu_fill = {}
            hg_done = {}
            bit_done = {}
            sh_mm_done = {}

            for c_ in pc_chunk["scgub"]:
                P.wait("scalar", *c_)
            z_done = {}
            scb_cast = {}

            def stageA(ci):
                wp, n, kk = DN_CHUNKS[ci]
                cb = 1024 * wp + kk * n
                b = ci % 2
                r_qw.load("sync", ci, [
                    (qst[b][:].rearrange("p (w g c) -> p w g c", w=4, g=2)[:, :, 0, 0:kk],
                     QWGU[:, cb:cb + kk].rearrange("(w p) c -> p w c", p=128)),
                    (qst[b][:].rearrange("p (w g c) -> p w g c", w=4, g=2)[:, :, 1, 0:kk],
                     QWGU[:, INTER + cb:INTER + cb + kk].rearrange("(w p) c -> p w c", p=128)),
                ])
                r_sc.load("sync", ci, [(scb32[b][:], SCEXP[:, 1024 * ci:1024 * (ci + 1)])])
                r_sc.ready("scalar", ci)
                if ci >= 2:
                    P.wait("scalar", sv, mult_done[ci - 2][0])
                    P.wait("scalar", sg, mult_done[ci - 2][1])
                scbc = P.op("scalar", lambda e, bb=b: e.copy(
                    out=scb[bb][:], in_=scb32[bb][:]), sem=sx)
                r_sc.consumed(ci, sx, scbc)
                scb_cast[ci] = scbc
                r_zq.load("sync", ci, [
                    (zqst[b][0:32, 0:kk], QZGU[:, 128 * wp:128 * wp + kk]),
                    (zqst[b][0:32, 128:128 + kk], QZGU[:, 1376 + 128 * wp:1376 + 128 * wp + kk]),
                ])
                r_scz.load("scalar", ci, [
                    (sczb[b][0:32, 0:kk], SCGUB[:, cb:cb + kk]),
                    (sczb[b][0:32, 128:128 + kk], SCGUB[:, INTER + cb:INTER + cb + kk]),
                ])

                r_qw.ready("vector", ci)
                if ci >= 2:
                    P.wait("vector", sv, mult_done[ci - 2][0])
                    P.wait("vector", sg, mult_done[ci - 2][1])
                P.op("vector", lambda e, bb=b: e.tensor_scalar(
                    out=lowb[bb][:], in0=qst[bb][:], scalar1=0x0F0F0F0F, scalar2=None,
                    op0=ALU.bitwise_and), sem=sv)
                bd = P.op("vector", lambda e, bb=b: e.tensor_scalar(
                    out=highb[bb][:], in0=qst[bb][:], scalar1=4, scalar2=0x0F0F0F0F,
                    op0=ALU.logical_shift_right, op1=ALU.bitwise_and), sem=sv)
                bit_done[ci] = bd
                r_qw.consumed(ci, sv, bd)

                r_zq.ready("vector", ci)
                r_scz.ready("vector", ci)
                if ci >= 2:
                    P.wait("vector", sp, gu_fill[(ci - 2, 1, 1)])
                z1 = P.op("vector", lambda e, bb=b, nn=n: e.tensor_scalar(
                    out=zu32[0:32, :], in0=zqst[bb][:], scalar1=4 * nn, scalar2=15,
                    op0=ALU.logical_shift_right, op1=ALU.bitwise_and), sem=sv)
                r_zq.consumed(ci, sv, z1)
                P.wait("vector", sv, z1)
                z2 = P.op("vector", lambda e: e.tensor_scalar(
                    out=zf32[0:32, :], in0=zu32[0:32, :], scalar1=1.0, scalar2=None,
                    op0=ALU.add), sem=sv)
                P.wait("vector", sv, z2)
                z3 = P.op("vector", lambda e, bb=b: e.tensor_tensor(
                    out=zbuf[bb][0:32, :], in0=zf32[0:32, :], in1=sczb[bb][0:32, :],
                    op=ALU.mult), sem=sv)
                z_done[ci] = z3
                r_scz.consumed(ci, sv, z3)

                P.wait("vector", sv, bd)
                P.wait("vector", sx, scb_cast[ci])
                P.wait("gpsimd", sv, bd)
                P.wait("gpsimd", sx, scb_cast[ci])
                if ci >= 3:
                    P.wait("vector", sp, gu_fill[(ci - 3, 1, 1)])
                    P.wait("gpsimd", sp, gu_fill[(ci - 3, 1, 1)])
                mv = mg = 0
                for j in range(32):
                    w, nib = j // 8, j % 8
                    plane = (highb if (nib % 2) else lowb)[b]
                    byte_off = 1024 * w + (nib // 2)
                    eng = "vector" if (j % 2 == 0) else "gpsimd"
                    msem = sv if eng == "vector" else sg
                    val = P.op(eng, lambda e, pl=plane, bo=byte_off, ww=w, bb=b, w3=ci % 3, jj=j: e.tensor_tensor(
                        out=w1t[w3][:, 256 * jj:256 * (jj + 1)],
                        in0=pl[:].bitcast(U8)[:, bo:(bo - bo % 1024) + 1024:4],
                        in1=scb[bb][:, 256 * ww:256 * (ww + 1)],
                        op=ALU.mult), sem=msem)
                    if eng == "vector":
                        mv = val
                    else:
                        mg = val
                mult_done[ci] = (mv, mg)

            def emit_sh_tail(cj):
                wpj, nj, kkj = DN_CHUNKS[cj]
                bj = cj % 2
                P.wait("tensor", sv, h_done[(cj, 1)])
                for hf2 in range(2):
                    sh_mm_done[cj] = P.op("tensor", lambda e, h=hf2, bb=bj, k2=kkj, w2=wpj,
                                          st=(cj == 0), sp_=(cj == 87):
                                          e.matmul(out=psh[0:86, 512 * h:512 * (h + 1)],
                                                   lhsT=c_enegd[0:k2, 86 * w2:86 * (w2 + 1)],
                                                   rhs=hchunk[bb][0:k2, 512 * h:512 * (h + 1)],
                                                   start=st, stop=sp_), sem=sp)
                P.wait("scalar", sv, h_done[(cj, 1)])
                P.wait("scalar", sp, sh_mm_done[cj])
                rbj = DN_ROWBASE[cj]
                r_hsp.load("scalar", cj, [(HTS[rbj:rbj + kkj, :], hchunk[bj][0:kkj, :])])

            def stageB(ci):
                wp, n, kk = DN_CHUNKS[ci]
                b = ci % 2
                if ci == 0:
                    P.guard("tensor")
                P.wait("tensor", sv, mult_done[ci][0])
                P.wait("tensor", sg, mult_done[ci][1])
                P.wait("tensor", sv, z_done[ci])
                for hf in range(2):
                    if ci >= 1:
                        P.wait("tensor", sx, hg_done[(ci - 1, hf)])
                        P.wait("tensor", sv, h_done[(ci - 1, hf)])
                    for gu in range(2):
                        tgt = (pg if gu == 0 else pu)[hf]
                        for j in range(32):
                            P.op("tensor", lambda e, jj=j, g=gu, h=hf, w3=ci % 3, tg=tgt, k2=kk:
                                 e.matmul(out=tg[0:k2, :],
                                          lhsT=w1t[w3][:, 256 * jj + 128 * g:256 * jj + 128 * g + k2],
                                          rhs=xnt[:, 1024 * jj + 512 * h:1024 * jj + 512 * (h + 1)],
                                          start=(jj == 0), stop=False), sem=None)
                        gu_fill[(ci, hf, gu)] = P.op("tensor", lambda e, g=gu, h=hf, bb=b, tg=tgt, k2=kk:
                                                     e.matmul(out=tg[0:k2, :],
                                                              lhsT=zbuf[bb][0:32, 128 * g:128 * g + k2],
                                                              rhs=sgneg[:, 512 * h:512 * (h + 1)],
                                                              start=False, stop=True), sem=sp)
                    P.wait("scalar", sp, gu_fill[(ci, hf, 0)])
                    hg = P.op("scalar", lambda e, h=hf, k2=kk: e.activation(
                        out=hgate[h][0:k2, :], in_=pg[h][0:k2, :], func=AF.Silu), sem=sx)
                    hg_done[(ci, hf)] = hg
                    P.wait("vector", sp, gu_fill[(ci, hf, 1)])
                    P.wait("vector", sx, hg)
                    if hf == 0:
                        r_hsp.guard_write("vector", ci)
                    if ci >= 2:
                        P.wait("vector", sp, sh_mm_done.get(ci - 2, 0))
                    hd = P.op("vector", lambda e, h=hf, bb=b, k2=kk: e.tensor_tensor(
                        out=hchunk[bb][0:k2, 512 * h:512 * (h + 1)],
                        in0=pu[h][0:k2, :], in1=hgate[h][0:k2, :], op=ALU.mult), sem=sv)
                    h_done[(ci, hf)] = hd
                if ci >= 1:
                    emit_sh_tail(ci - 1)

            stageA(0)
            for ci in range(1, 88):
                stageA(ci)
                stageB(ci - 1)
            stageB(87)
            emit_sh_tail(87)
            P.wait("vector", sp, sh_mm_done[87])
            sh_evac = P.op("vector", lambda e: e.tensor_copy(
                out=shneg[0:86, :], in_=psh[0:86, :]), sem=sv)

            if DEBUG:
                for s_, v_ in r_hsp.totals():
                    P.wait("sync", s_, v_)
                P.wait("sync", sv, sh_evac)
                P.dma("sync", dbg["HTS"], HTS, qdbg)
                P.dma("sync", dbg["SH"], shneg[:], qdbg)
            s3.close()
            for s_, v_ in r_hsp.totals():
                P.wait("sync", s_, v_)
            P.wait("sync", qdbg, qdbg.n)
            P.set_pguard((sx, hg_done[(87, 1)]), (sv, max(h_done[(87, 1)], sh_evac)))
            P.barrier()

        scopeA.close()

        if PHASES >= 4:
            s4 = ExitStack()
            sb4 = lambda name, shape, dt: s4.enter_context(nc.sbuf_tensor(name, shape, dt))
            ps4 = lambda name, shape: s4.enter_context(nc.psum_tensor(name, shape, F32))
            zd = sb4("zd", [86, D], BF16)
            zdq = sb4("zdq", [86, 512], U32)
            zdnu = sb4("zdnu", [86, 512], U32)
            zdn = sb4("zdn", [86, 512], F32)
            scd_full = sb4("scd_full", [86, D], BF16)
            qwd = [sb4(f"qwd{i}", [128, 512], U32) for i in range(2)]
            lowd = [sb4(f"lowd{i}", [128, 512], U32) for i in range(2)]
            highd = [sb4(f"highd{i}", [128, 512], U32) for i in range(2)]
            scdx = [sb4(f"scdx{i}", [128, 512], BF16) for i in range(2)]
            scdx32 = [sb4(f"scdx32{i}", [128, 512], F32) for i in range(2)]
            w1d = [sb4(f"w1d{i}", [128, 512], BF16) for i in range(3)]
            htb = [sb4(f"htb{i}", [128, 1024], BF16) for i in range(3)]
            gst = [sb4(f"gst{i}", [128, 512], BF16) for i in range(2)]
            pd = [ps4(f"pd{i}", [128, 512]) for i in range(8)]

            r_zd = Ring(P, "r_zd", 1)
            r_scd = Ring(P, "r_scd", 1)
            r_qwd = Ring(P, "r_qwd", 2)
            r_sdx = Ring(P, "r_sdx", 2)
            r_ht = Ring(P, "r_ht", 3)
            r_go = Ring(P, "r_go", 2)

            wait_pc("sync", "scdnb")
            r_zd.load("sync", 0, [(zdq[:], QZDN)])
            r_scd.load("sync", 0, [(scd_full[:], SCDNB)])
            r_zd.ready("vector", 0)
            r_scd.ready("vector", 0)
            zprev = 0
            for n2 in range(8):
                P.wait("vector", sv, zprev)
                a1 = P.op("vector", lambda e, nn=n2: e.tensor_scalar(
                    out=zdnu[:], in0=zdq[:], scalar1=4 * nn, scalar2=15,
                    op0=ALU.logical_shift_right, op1=ALU.bitwise_and), sem=sv)
                P.wait("vector", sv, a1)
                a2 = P.op("vector", lambda e: e.tensor_scalar(
                    out=zdn[:], in0=zdnu[:], scalar1=1.0, scalar2=None, op0=ALU.add), sem=sv)
                P.wait("vector", sv, a2)
                zprev = P.op("vector", lambda e, nn=n2: e.tensor_tensor(
                    out=zd[:].rearrange("p (c e) -> p c e", e=8)[:, :, nn],
                    in0=zdn[:],
                    in1=scd_full[:].rearrange("p (c e) -> p c e", e=8)[:, :, nn],
                    op=ALU.mult), sem=sv)
            zd_ready = zprev

            dn_bit_done = {}
            dn_mult_used = {}
            sdx_cast = {}
            g_evac = {}
            w1d_used = {}
            for hc in range(8):
                for ci, (wp, n, kk) in enumerate(DN_CHUNKS):
                    gi = hc * 88 + ci
                    b3 = gi % 3
                    wi = hc * 11 + wp
                    if n == 0:
                        r_qwd.load("sync", wi,
                                   [(qwd[wi % 2][0:kk, :],
                                     QWDN[128 * wp:128 * wp + kk, 512 * hc:512 * (hc + 1)])])
                        r_sdx.load("sync", wi,
                                   [(scdx32[wi % 2][:],
                                     SCDEXP[:, 4096 * wp + 512 * hc:4096 * wp + 512 * (hc + 1)])])
                        r_sdx.ready("scalar", wi)
                        if wi >= 2:
                            P.wait("scalar", sv, dn_mult_used[wi - 2][0])
                            P.wait("scalar", sg, dn_mult_used[wi - 2][1])
                        sdxc = P.op("scalar", lambda e, w2=wi % 2: e.copy(
                            out=scdx[w2][:], in_=scdx32[w2][:]), sem=sx)
                        r_sdx.consumed(wi, sx, sdxc)
                        sdx_cast[wi] = sdxc
                        r_qwd.ready("vector", wi)
                        if wi >= 2:
                            P.wait("vector", sv, dn_mult_used[wi - 2][0])
                            P.wait("vector", sg, dn_mult_used[wi - 2][1])
                        P.op("vector", lambda e, w2=wi % 2, k2=kk: e.tensor_scalar(
                            out=lowd[w2][0:k2, :], in0=qwd[w2][0:k2, :], scalar1=0x0F0F0F0F,
                            scalar2=None, op0=ALU.bitwise_and), sem=sv)
                        bitd = P.op("vector", lambda e, w2=wi % 2, k2=kk: e.tensor_scalar(
                            out=highd[w2][0:k2, :], in0=qwd[w2][0:k2, :], scalar1=4,
                            scalar2=0x0F0F0F0F, op0=ALU.logical_shift_right,
                            op1=ALU.bitwise_and), sem=sv)
                        dn_bit_done[wi] = bitd
                        r_qwd.consumed(wi, sv, bitd)
                        dn_mult_mv = 0
                        dn_mult_mg = 0
                    rb = DN_ROWBASE[ci]
                    r_ht.load("scalar", gi, [(htb[b3][0:kk, :], HTS[rb:rb + kk, :])])
                    eng = "vector" if (ci % 2 == 0) else "gpsimd"
                    msem = sv if eng == "vector" else sg
                    P.wait(eng, sv, dn_bit_done[wi])
                    P.wait(eng, sx, sdx_cast[wi])
                    if gi >= 3:
                        P.wait(eng, sp, w1d_used[gi - 3])
                    plane = (highd if (n % 2) else lowd)[wi % 2]
                    mval = P.op(eng, lambda e, pl=plane, bo=(n // 2), b3_=b3, k2=kk, w2=wi % 2:
                                e.tensor_tensor(out=w1d[b3_][0:k2, :],
                                                in0=pl[0:k2, :].bitcast(U8)[:, bo:2048:4],
                                                in1=scdx[w2][0:k2, :], op=ALU.mult), sem=msem)
                    if eng == "vector":
                        dn_mult_mv = mval
                    else:
                        dn_mult_mg = mval
                    if n == 7:
                        dn_mult_used[wi] = (dn_mult_mv, dn_mult_mg)
                    if gi == 0:
                        P.guard("tensor")
                    P.wait("tensor", msem, mval)
                    r_ht.ready("tensor", gi)
                    if ci == 0 and hc > 0:
                        P.wait("tensor", sx, g_evac[(hc - 1, 7)])
                    lastmm = None
                    for t in range(8):
                        lastmm = P.op("tensor", lambda e, tt=t, b3_=b3, k2=kk, st=(ci == 0):
                                      e.matmul(out=pd[tt][:],
                                               lhsT=htb[b3_][0:k2, 128 * tt:128 * (tt + 1)],
                                               rhs=w1d[b3_][0:k2, :],
                                               start=st, stop=False),
                                      sem=(sp if t == 7 else None))
                    r_ht.consumed(gi, sp, lastmm)
                    w1d_used[gi] = lastmm
                P.wait("tensor", sv, zd_ready)
                zf = None
                for t in range(8):
                    zf = P.op("tensor", lambda e, tt=t, h2=hc: e.matmul(
                        out=pd[tt][:], lhsT=shneg[:, 128 * tt:128 * (tt + 1)],
                        rhs=zd[:, 512 * h2:512 * (h2 + 1)], start=False, stop=True),
                        sem=(sp if t == 7 else None))
                for t in range(8):
                    gi2 = hc * 8 + t
                    P.wait("scalar", sp, zf)
                    r_go.guard_write("scalar", gi2)
                    ev = P.op("scalar", lambda e, tt=t: e.activation(
                        out=gst[tt % 2][:], in_=pd[tt][:], func=AF.Silu), sem=sx)
                    g_evac[(hc, t)] = ev
                    P.wait("scalar", sx, ev)
                    r_go.load("scalar", gi2,
                              [(GSP[128 * t:128 * (t + 1), 512 * hc:512 * (hc + 1)], gst[t % 2][:])])
            if DEBUG:
                for s_, v_ in r_go.totals():
                    P.wait("sync", s_, v_)
                P.dma("sync", dbg["GSP"], GSP, qdbg)
            s4.close()
            for s_, v_ in r_go.totals():
                P.wait("sync", s_, v_)
            P.set_pguard((sx, g_evac[(7, 7)]))
            P.barrier()

        if PHASES >= 5:
            s5 = ExitStack()
            sb5 = lambda name, shape, dt: s5.enter_context(nc.sbuf_tensor(name, shape, dt))
            s5p = ExitStack()
            ps5 = lambda name, shape, dt=F32: s5p.enter_context(nc.psum_tensor(name, shape, dt))
            slab5 = sb5("slab5", [128, 32 * 1024], BF16)
            ktb = [sb5(f"ktb{i}", [128, 512], BF16) for i in range(6)]
            vtb = [sb5(f"vtb{i}", [128, 512], BF16) for i in range(6)]
            pb = [sb5(f"pb{i}", [128, 512], BF16) for i in range(8)]
            linv = [sb5(f"linv{i}", [1, 512], F32) for i in range(2)]
            linvrep = [sb5(f"linvrep{i}", [128, 512], F32) for i in range(2)]
            ps_s = [ps5(f"ps_s{i}", [128, 512]) for i in range(4)]
            ps_o = [ps5(f"ps_o{i}", [128, 512]) for i in range(2)]
            ps_l = [ps5(f"ps_l{i}", [1, 512]) for i in range(1)]
            ps_b = ps5("ps_b", [128, 512])

            r_ql = Ring(P, "r_ql", 1)
            r_kt = Ring(P, "r_kt", 6)
            r_vt = Ring(P, "r_vt", 6)
            r_ql.load("sync", 0, [(slab5[:].rearrange("p (j c) -> p j c", c=1024),
                                   QSP.rearrange("(j p) c -> p j c", p=128))])
            P.wait("sync", cc, 2)
            P.wait("scalar", cc, 2)
            exp_done = {}
            pb_used = {}
            norm_done = {}
            for u in range(2):
                for h in range(H):
                    uh = u * H + h
                    qo = 1024 * h + 512 * u
                    def emit_outsum(ib2):
                        r2, i2 = divmod(ib2, 4)
                        gi2_ = uh * 9 + r2
                        ib_g = uh * 36 + ib2
                        if r2 == 8:
                            use_sem, use_val = sv, mask_done[ib_g]
                        else:
                            use_sem, use_val = sx, exp_done[ib_g]
                        P.wait("tensor", use_sem, use_val)
                        if i2 == 0:
                            r_vt.ready("tensor", gi2_)
                        first = (ib2 == 0)
                        last = (ib2 == 35)
                        if first and uh > 1:
                            P.wait("tensor", sv, norm_done[uh - 2])
                        ub2 = uh % 2
                        P.op("tensor", lambda e, g3=gi2_ % 6, ii=i2, ibb=ib_g, f=first, l=last, u2=ub2:
                             e.matmul(out=ps_o[u2][:], lhsT=vtb[g3][:, 128 * ii:128 * (ii + 1)],
                                      rhs=pb[ibb % 8][:], start=f, stop=l), sem=None)
                        if first and uh >= 1:
                            P.wait("tensor", sv, r1_done[uh - 1])
                        omm = P.op("tensor", lambda e, ibb=ib_g, f=first, l=last:
                                   e.matmul(out=ps_l[0][:], lhsT=c_ones[:],
                                            rhs=pb[ibb % 8][:], start=f, stop=l), sem=sp)
                        pb_used[ib_g] = omm
                        if i2 == 3:
                            r_vt.consumed(gi2_, sp, omm)

                    mask_done = {}
                    for r in range(9):
                        gi = uh * 9 + r
                        if r < 8:
                            ksrc = agK_out[4096 * r + 128 * h:4096 * r + 128 * (h + 1),
                                           512 * u:512 * (u + 1)]
                            vsrc = agV_out[1024 * r + 512 * u:1024 * r + 512 * (u + 1),
                                           128 * h:128 * (h + 1)].rearrange("(i p) d -> p i d", p=128)
                        else:
                            ksrc = agK_in[128 * h:128 * (h + 1), 512 * u:512 * (u + 1)]
                            vsrc = agV_in[512 * u:512 * (u + 1),
                                          128 * h:128 * (h + 1)].rearrange("(i p) d -> p i d", p=128)
                        r_kt.load("sync", gi, [(ktb[gi % 6][:], ksrc)])
                        r_vt.load("scalar", gi,
                                  [(vtb[gi % 6][:].rearrange("p (i d) -> p i d", d=128), vsrc)])
                        if gi == 0:
                            P.guard("tensor")
                            r_ql.ready("tensor", 0)
                        r_kt.ready("tensor", gi)
                        for i in range(4):
                            kt = 4 * r + i
                            lb = r * 4 + i
                            ib = uh * 36 + lb
                            sb_ = ib % 4
                            if ib >= 4:
                                P.wait("tensor", sx, exp_done[ib - 4])
                            smm = P.op("tensor", lambda e, g3=gi % 6, ii=i, s2=sb_, q1=qo:
                                       e.matmul(out=ps_s[s2][:],
                                                lhsT=ktb[g3][:, 128 * ii:128 * (ii + 1)],
                                                rhs=slab5[:, q1:q1 + 512],
                                                start=True, stop=True), sem=sp)
                            if i == 3:
                                r_kt.consumed(gi, sp, smm)
                            P.wait("scalar", sp, smm)
                            if ib >= 8:
                                P.wait("scalar", sp, pb_used[ib - 8])
                            bias_col = kt if r < 8 else 32
                            ed = P.op("scalar", lambda e, s2=sb_, ibb=ib, bc=bias_col:
                                      e.activation(out=pb[ibb % 8][:], in_=ps_s[s2][:],
                                                   func=AF.Exp, bias=c_bias[:, bc:bc + 1],
                                                   scale=1.0), sem=sx)
                            exp_done[ib] = ed
                            if r == 8:
                                P.wait("vector", sx, ed)
                                md = P.op("vector", lambda e, ibb=ib, ii=i: e.tensor_tensor(
                                    out=pb[ibb % 8][:], in0=pb[ibb % 8][:],
                                    in1=c_masks[:, 512 * ii:512 * (ii + 1)], op=ALU.mult), sem=sv)
                                mask_done[ib] = md
                            if lb >= 3:
                                emit_outsum(lb - 3)
                    emit_outsum(33)
                    emit_outsum(34)
                    emit_outsum(35)
                    ub = uh % 2
                    P.wait("vector", sp, pb_used[uh * 36 + 35])
                    r1 = P.op("vector", lambda e, u2=ub: e.reciprocal(
                        out=linv[u2][:], in_=ps_l[0][:]), sem=sv)
                    if uh == 0:
                        r1_done = {}
                    r1_done[uh] = r1
                    P.wait("tensor", sv, r1)
                    if uh >= 1:
                        P.wait("tensor", sv, linvrep_done[uh - 1])
                    bc2 = P.op("tensor", lambda e, u2=ub: e.matmul(
                        out=ps_b[:], lhsT=c_onesr[:], rhs=linv[u2][:], start=True, stop=True), sem=sp)
                    P.wait("vector", sp, bc2)
                    r2 = P.op("vector", lambda e, u2=ub: e.tensor_copy(
                        out=linvrep[u2][:], in_=ps_b[:]), sem=sv)
                    if uh == 0:
                        linvrep_done = {}
                    linvrep_done[uh] = r2
                    P.wait("vector", sv, r2)
                    nd = P.op("vector", lambda e, q1=qo, u2=ub: e.tensor_tensor(
                        out=slab5[:, q1:q1 + 512], in0=ps_o[u2][:], in1=linvrep[u2][:],
                        op=ALU.mult), sem=sv)
                    norm_done[uh] = nd

            if DEBUG:
                P.wait("sync", sv, norm_done[63])
                P.dma("sync", dbg["SPRE"].rearrange("(j p) c -> p j c", p=128),
                      slab5[:].rearrange("p (j c) -> p j c", c=1024), qdbg)
            s5p.close()
            P.set_pguard((sv, norm_done[63]))
            P.barrier()

            if PHASES >= 6:
                s6 = ExitStack()
                sb6 = lambda name, shape, dt: s6.enter_context(nc.sbuf_tensor(name, shape, dt))
                ps6 = lambda name, shape: s6.enter_context(nc.psum_tensor(name, shape, F32))
                wot = [sb6(f"wot{i}", [128, 512], BF16) for i in range(4)]
                wot32 = [sb6(f"wot32{i}", [128, 512], F32) for i in range(3)]
                xb6 = [sb6(f"xb6{i}", [128, 512], F32) for i in range(3)]
                gb6 = [sb6(f"gb6{i}", [128, 512], BF16) for i in range(3)]
                sgb = [sb6(f"sgb{i}", [128, 512], F32) for i in range(2)]
                ob6 = [sb6(f"ob6{i}", [128, 512], F32) for i in range(2)]
                pc6 = [ps6(f"pc6{i}", [128, 512]) for i in range(8)]

                r_wo = Ring(P, "r_wo", 3)
                wo_cast = {}
                r_xb = Ring(P, "r_xb", 3)
                r_gb = Ring(P, "r_gb", 3)
                r_out = Ring(P, "r_out", 2)
                comb = {}
                def wo_load(gi3):
                    fc3, dc3 = divmod(gi3, 32)
                    r_wo.load("scalar", gi3,
                              [(wot32[gi3 % 3][:], WO[128 * dc3:128 * (dc3 + 1), 512 * fc3:512 * (fc3 + 1)])])
                wo_load(0)
                wo_load(1)
                for fc in range(8):
                    for dc in range(32):
                        gi = fc * 32 + dc
                        if gi + 2 < 256:
                            wo_load(gi + 2)
                        r_wo.ready("scalar", gi)
                        if gi >= 4:
                            P.wait("scalar", sp, wo_used_d[gi - 4])
                        wc = P.op("scalar", lambda e, b4=gi % 4, b3_=gi % 3:
                                  e.copy(out=wot[b4][:], in_=wot32[b3_][:]), sem=sx)
                        wo_cast[gi] = wc
                        r_wo.consumed(gi, sx, wc)
                        P.wait("tensor", sx, wc)
                        if gi == 0:
                            P.guard("tensor")
                            wo_used_d = {}
                        wdone = None
                        for t in range(8):
                            if dc == 0 and fc > 0:
                                P.wait("tensor", sv, comb[(fc - 1, t)])
                            wdone = P.op("tensor", lambda e, tt=t, b=gi % 4, d2=dc:
                                         e.matmul(out=pc6[tt][:],
                                                  lhsT=slab5[:, 1024 * d2 + 128 * tt:1024 * d2 + 128 * (tt + 1)],
                                                  rhs=wot[b][:],
                                                  start=(d2 == 0), stop=(d2 == 31)),
                                         sem=(sp if t == 7 else None))
                        wo_used_d[gi] = wdone
                    for t in range(8):
                        gi2 = fc * 8 + t
                        r_xb.load("sync", gi2,
                                  [(xb6[gi2 % 3][:], XNAT[128 * t:128 * (t + 1), 512 * fc:512 * (fc + 1)])])
                        r_gb.load("scalar", gi2,
                                  [(gb6[gi2 % 3][:], GSP[128 * t:128 * (t + 1), 512 * fc:512 * (fc + 1)])])
                        P.wait("vector", sp, wdone)
                        r_gb.ready("vector", gi2)
                        r_xb.ready("vector", gi2)
                        r_out.guard_write("vector", gi2)
                        c1 = P.op("vector", lambda e, tt=t, g2=gi2 % 3: e.tensor_tensor(
                            out=sgb[tt % 2][:], in0=pc6[tt][:], in1=gb6[g2][:],
                            op=ALU.mult), sem=sv)
                        P.wait("vector", sv, c1)
                        c2 = P.op("vector", lambda e, tt=t, g2=gi2 % 3: e.tensor_tensor(
                            out=ob6[tt % 2][:], in0=sgb[tt % 2][:], in1=xb6[g2][:],
                            op=ALU.add), sem=sv)
                        comb[(fc, t)] = c2
                        r_xb.consumed(gi2, sv, c2)
                        r_gb.consumed(gi2, sv, c2)
                        P.wait("sync", sv, c2)
                        r_out.load("sync", gi2,
                                   [(OUT[128 * t:128 * (t + 1), 512 * fc:512 * (fc + 1)], ob6[t % 2][:])])
                for s_, v_ in r_out.totals():
                    P.wait("sync", s_, v_)
                s6.close()
            s5.close()

        P.wait("sync", qdbg, qdbg.n)

        block = es.enter_context(nc.Block())
        P.finalize(block)

    counts = {e: len(P.streams[e]) for e in ENGINES}
    return nc, counts



def prep_inputs(L0_attn_x, L0_attn_wq, L0_attn_wk, L0_attn_wv, L0_attn_wo,
                L0_ffn_qweight_gu, L0_ffn_qzeros_gu, L0_ffn_scales_gu, L0_ffn_g_idx_gu,
                L0_ffn_qweight_down, L0_ffn_qzeros_down, L0_ffn_scales_down, L0_ffn_g_idx_down):
    f32 = np.float32
    x = np.asarray(L0_attn_x, f32)
    wq = np.asarray(L0_attn_wq, f32)
    wk = np.asarray(L0_attn_wk, f32)
    wv = np.asarray(L0_attn_wv, f32)
    wo = np.ascontiguousarray(np.asarray(L0_attn_wo, f32))
    qwgu = np.asarray(L0_ffn_qweight_gu, np.uint32)
    qzgu = np.ascontiguousarray(np.asarray(L0_ffn_qzeros_gu, np.uint32))
    scgu = np.asarray(L0_ffn_scales_gu, f32)
    qwdn = np.ascontiguousarray(np.asarray(L0_ffn_qweight_down, np.uint32))
    qzdn = np.ascontiguousarray(np.asarray(L0_ffn_qzeros_down, np.uint32))
    scdn = np.ascontiguousarray(np.asarray(L0_ffn_scales_down, f32))

    wq_p = np.ascontiguousarray(wq[PERM_H, :])
    wk_p = np.ascontiguousarray(wk[PERM_H, :])
    wv_p = np.ascontiguousarray(wv[PERM_H, :])
    qwgu_p = np.ascontiguousarray(
        np.concatenate([qwgu[:, :INTER][:, PERM_D], qwgu[:, INTER:][:, PERM_D]], axis=1))
    scgu_p = np.ascontiguousarray(
        np.concatenate([scgu[:, :INTER][:, PERM_D], scgu[:, INTER:][:, PERM_D]], axis=1))

    gidx = np.arange(128) // 16
    scexp = np.zeros((128, 88, 4, 2, 128), f32)
    rows_by_w = [scgu_p[8 * w + gidx] for w in range(4)]
    for ci, (wp, n, kk) in enumerate(DN_CHUNKS):
        cb = 1024 * wp + kk * n
        for w in range(4):
            scexp[:, ci, w, 0, :kk] = rows_by_w[w][:, cb:cb + kk]
            scexp[:, ci, w, 1, :kk] = rows_by_w[w][:, INTER + cb:INTER + cb + kk]
    scexp = np.ascontiguousarray(scexp.reshape(128, -1))
    scdexp = np.empty((128, 11, D), f32)
    for wp in range(11):
        ridx = np.minimum(8 * wp + gidx, 85)
        scdexp[:, wp, :] = scdn[ridx]
    scdexp = np.ascontiguousarray(scdexp.reshape(128, -1))

    permm = np.zeros((128, 128), f32)
    for i in range(0, 128, 2):
        permm[i, i + 1] = 1.0
        permm[i + 1, i] = 1.0
    eneg = np.zeros((128, 4, 32), f32)
    for w in range(4):
        eneg[np.arange(128), w, 8 * w + np.arange(128) // 16] = -1.0
    eneg = eneg.reshape(128, 128)
    enegd = np.zeros((128, 11, 86), f32)
    for wp in range(11):
        kk = 128 if wp < 10 else 96
        enegd[np.arange(kk), wp, 8 * wp + np.arange(kk) // 16] = -1.0
    enegd = enegd.reshape(128, 11 * 86)
    onesc = np.ones((128, 1), f32)
    onesr = np.ones((1, 128), f32)
    masks = np.zeros((128, 4, SEG), f32)
    for dlt in range(4):
        pp = np.arange(128)[:, None]
        jj = np.arange(SEG)[None, :]
        masks[:, dlt, :] = (pp <= jj - 128 * dlt).astype(f32)
    masks = masks.reshape(128, 4 * SEG)

    inv = 1.0 / (10000.0 ** (np.arange(0, HD, 2)[:64].astype(f32) / HD))

    per_core = []
    for c in range(NCORES):
        pos = np.concatenate([512 * c + np.arange(SEG), 512 * c + np.arange(SEG)])
        f = np.outer(pos, inv).astype(f32)
        cosf = np.cos(f)
        sinf = np.sin(f)
        ropec = np.empty((128, T), f32)
        ropes = np.empty((128, T), f32)
        ii = np.arange(64)
        ropec[2 * ii, :] = cosf.T
        ropec[2 * ii + 1, :] = cosf.T
        ropes[2 * ii, :] = -sinf.T
        ropes[2 * ii + 1, :] = sinf.T
        biast = np.zeros((128, 33), f32)
        for kt in range(32):
            if kt >= 4 * c:
                biast[:, kt] = NEG
        xa = x[0, 512 * c:512 * (c + 1)]
        xb = x[1, 512 * c:512 * (c + 1)]
        xnat = np.ascontiguousarray(np.concatenate([xa, xb], axis=0))
        xperm = np.ascontiguousarray(xnat[:, PERM_H])
        per_core.append(dict(XP=xperm, XNAT=xnat, ROPEC=ropec, ROPES=ropes, BIAST=biast))

    shared = dict(
        WQ=wq_p, WK=wk_p, WV=wv_p, WO=wo, QWGU=qwgu_p, SCGU=scgu_p, QZGU=qzgu,
        QWDN=qwdn, SCDN=scdn, QZDN=qzdn, SCEXP=scexp, SCDEXP=scdexp,
        PERMM=permm, IDTI=np.eye(128, dtype=f32), ENEG=eneg, ENEGD=enegd,
        ONESC=onesc, ONESR=onesr, MASKS=masks,
    )
    in_maps = []
    for c in range(NCORES):
        m = dict(shared)
        m.update(per_core[c])
        in_maps.append(m)
    return in_maps


def kernel(**inputs):
    in_maps = prep_inputs(**inputs)
    nc, counts = build_program()
    res = run_bass_kernel_spmd(nc, in_maps, core_ids=list(range(NCORES)))
    out = np.empty((2, 4096, D), np.float32)
    for c in range(NCORES):
        o = np.asarray(res.results[c]["OUT"])
        out[0, 512 * c:512 * (c + 1)] = o[:SEG]
        out[1, 512 * c:512 * (c + 1)] = o[SEG:]
    return out


if __name__ == "__main__":
    import sys
    if sys.argv[1:] and sys.argv[1] == "build":
        nc, counts = build_program()
        print("build ok", counts, "total", sum(counts.values()))
